# version 07
# baseline (speedup 1.0000x reference)
import numpy as np
import ml_dtypes
from contextlib import ExitStack
import concourse.bass as bass
import concourse.mybir as mybir
from concourse.bass_utils import run_bass_kernel_spmd

F32 = mybir.dt.float32
BF16 = mybir.dt.bfloat16
FP8 = mybir.dt.float8e4
AF = mybir.ActivationFunctionType
ALU = mybir.AluOpType
AX = mybir.AxisListType

NCORES = 8
N = 4096
D = 1024
H = 16
DH = 64
HID = 4096
NT = 32
QC = 512
NQC = N // QC
FT = 1024
NFC = N // FT
EPS = 1e-5
ENGINES = ("sync", "tensor", "vector", "scalar", "gpsimd")


def _host_consts():
    theta = 1.0 / (10000.0 ** (np.arange(0, DH, 2, dtype=np.float64) / DH))
    pos = np.arange(N, dtype=np.float64)
    freqs = pos[:, None] * theta[None, :]
    emb = np.concatenate([freqs, freqs], axis=-1)
    cos = np.cos(emb)
    ssin = np.sin(emb)
    ssin[:, 0::2] *= -1.0

    def tl(a):
        return np.ascontiguousarray(
            a.reshape(NT, 128, DH).transpose(1, 0, 2).reshape(128, NT * DH)
        ).astype(np.float32)
    return tl(cos), tl(ssin)


def _cols(v):
    n = np.asarray(v).shape[0]
    return np.ascontiguousarray(np.asarray(v, np.float32).reshape(n // 128, 128).T)


class Ctr:
    __slots__ = ("sem", "n")

    def __init__(self, sem):
        self.sem = sem
        self.n = 0


class DmaRing:

    def __init__(self, nc, ctx, prefix, depth):
        self.ctrs = [Ctr(ctx.enter_context(nc.semaphore(f"{prefix}{i}")))
                     for i in range(depth)]
        self.i = 0

    def next(self):
        c = self.ctrs[self.i % len(self.ctrs)]
        self.i += 1
        prev = c.n
        c.n += 16
        return c, prev


class Prog:
    def __init__(self, nc, ctx):
        self.nc = nc
        self.plan = {e: [] for e in ENGINES}
        self.ctr = {}
        for name in ("sM", "sV", "sS", "sG"):
            self.ctr[name] = Ctr(ctx.enter_context(nc.semaphore(name)))
        self.rings = {
            "sync": DmaRing(nc, ctx, "rX", 10),
            "gpsimd": DmaRing(nc, ctx, "rC", 6),
            "scalar": DmaRing(nc, ctx, "rW", 6),
        }

    def emit(self, eng, fn):
        self.plan[eng].append(fn)

    def wait(self, eng, cname, val=None):
        if isinstance(val, tuple):
            return self.waitt(eng, val)
        c = self.ctr[cname]
        v = c.n if val is None else val
        if v is None or v <= 0:
            return
        assert v <= c.n, f"wait on future {cname}: {v} > {c.n}"
        self.emit(eng, lambda e, s=c.sem, v=v: e.wait_ge(s, v))

    def waitt(self, eng, tok):
        if not tok:
            return
        if isinstance(tok, list):
            for t in tok:
                self.waitt(eng, t)
            return
        c, v = tok
        self.emit(eng, lambda e, s=c.sem, v=v: e.wait_ge(s, v))

    def wait_all(self, eng, q):
        for c in self.rings[q].ctrs:
            if c.n > 0:
                self.emit(eng, lambda e, s=c.sem, v=c.n: e.wait_ge(s, v))

    def dmaq(self, q, dst, src):
        c, prev = self.rings[q].next()
        if prev > 0:
            self.emit(q, lambda e, s=c.sem, v=prev: e.wait_ge(s, v))
        self.emit(q, lambda e, d=dst, s=src, sm=c.sem: e.dma_start(
            out=d, in_=s).then_inc(sm, 16))
        return (c, c.n)

    def do(self, eng, build, inc=None):
        if inc is not None:
            c = self.ctr[inc]
            c.n += 1
            self.emit(eng, lambda e, b=build, s=c.sem: b(e).then_inc(s, 1))
            return c.n
        self.emit(eng, lambda e, b=build: b(e))
        return None

    def mark(self, cname):
        return self.ctr[cname].n


def build_nc(stop_after=None):
    nc = bass.Bass("TRN2", target_bir_lowering=False, debug=False)
    ctx = ExitStack()
    AP = bass.AP

    x_h = nc.dram_tensor("x", [N, D], F32, kind="ExternalInput")
    wq_h = nc.dram_tensor("wq", [D, D], F32, kind="ExternalInput")
    wkv_h = nc.dram_tensor("wkv", [D, 2 * D], F32, kind="ExternalInput")
    wb2_h = nc.dram_tensor("wb2", [8, 128, 128], F32, kind="ExternalInput")
    wo_h = nc.dram_tensor("wo", [D, D], F32, kind="ExternalInput")
    w1_h = nc.dram_tensor("w1", [HID, D], F32, kind="ExternalInput")
    w2_h = nc.dram_tensor("w2", [D, HID], F32, kind="ExternalInput")
    gam_h = nc.dram_tensor("gam", [128, 8], F32, kind="ExternalInput")
    g1_h = nc.dram_tensor("g1", [128, 8], F32, kind="ExternalInput")
    b1ln_h = nc.dram_tensor("b1ln", [128, 8], F32, kind="ExternalInput")
    b1_h = nc.dram_tensor("b1", [128, 32], F32, kind="ExternalInput")
    g2_h = nc.dram_tensor("g2", [128, 32], F32, kind="ExternalInput")
    b2ln_h = nc.dram_tensor("b2ln", [128, 32], F32, kind="ExternalInput")
    b2_h = nc.dram_tensor("b2", [128, 8], F32, kind="ExternalInput")
    tabc_h = nc.dram_tensor("tabc", [128, NT * DH], F32, kind="ExternalInput")
    tabs_h = nc.dram_tensor("tabs", [128, NT * DH], F32, kind="ExternalInput")
    id16_h = nc.dram_tensor("id16", [128, 128], F32, kind="ExternalInput")
    id32_h = nc.dram_tensor("id32", [128, 128], F32, kind="ExternalInput")

    xa_scr = nc.dram_tensor("xa_scr", [N, D], F32, kind="Internal")
    wqs_scr = nc.dram_tensor("wqs_scr", [D, D], BF16, kind="Internal")
    w1t_scr = nc.dram_tensor("w1t_scr", [32, 128, 1024], FP8, kind="Internal")
    w2t_scr = nc.dram_tensor("w2t_scr", [8, 128, 4096], BF16, kind="Internal")

    y_h = nc.dram_tensor("y", [N, D], F32, kind="ExternalOutput")
    dbg_h = None
    if stop_after is not None:
        dbg_h = nc.dram_tensor("dbg", [128, 40960], F32, kind="ExternalOutput")

    ps01 = ctx.enter_context(nc.psum_tensor("ps01", [128, 1024], F32))
    ps23 = ctx.enter_context(nc.psum_tensor("ps23", [128, 1024], F32))
    psPV0 = ctx.enter_context(nc.psum_tensor("psPV0", [128, 512], F32))
    psPV1 = ctx.enter_context(nc.psum_tensor("psPV1", [128, 512], F32))
    psT = ctx.enter_context(nc.psum_tensor("psT", [128, 1024], BF16))
    psM = ctx.enter_context(nc.psum_tensor("psM", [128, 512], F32))

    wb2 = ctx.enter_context(nc.sbuf_tensor("wb2_s", [128, 8 * 128], BF16))
    ident = ctx.enter_context(nc.sbuf_tensor("ident", [128, 128], BF16))
    identf = ctx.enter_context(nc.sbuf_tensor("identf", [128, 128], F32))
    onescol = ctx.enter_context(nc.sbuf_tensor("onescol", [128, 1], BF16))
    onescolf = ctx.enter_context(nc.sbuf_tensor("onescolf", [128, 1], F32))
    gam = ctx.enter_context(nc.sbuf_tensor("gam_s", [128, 8], F32))
    g1c = ctx.enter_context(nc.sbuf_tensor("g1c", [128, 8], F32))
    b1lnc = ctx.enter_context(nc.sbuf_tensor("b1lnc", [128, 16], BF16))
    b1c = ctx.enter_context(nc.sbuf_tensor("b1c", [128, 32], F32))
    g2c = ctx.enter_context(nc.sbuf_tensor("g2c", [128, 32], F32))
    b2lnc = ctx.enter_context(nc.sbuf_tensor("b2lnc", [128, 64], BF16))
    b2c = ctx.enter_context(nc.sbuf_tensor("b2c", [128, 8], F32))
    mu0 = ctx.enter_context(nc.sbuf_tensor("mu0", [128, NT], F32))
    nmu0 = ctx.enter_context(nc.sbuf_tensor("nmu0", [128, NT], F32))
    R0 = ctx.enter_context(nc.sbuf_tensor("R0", [128, NT], F32))
    ssq0 = ctx.enter_context(nc.sbuf_tensor("ssq0", [128, NT], F32))
    lnscr = ctx.enter_context(nc.sbuf_tensor("lnscr", [128, NT], F32))
    lnscr2 = ctx.enter_context(nc.sbuf_tensor("lnscr2", [128, NT], F32))
    junkb = ctx.enter_context(nc.sbuf_tensor("junkb", [128, 1024], BF16))
    junks = ctx.enter_context(nc.sbuf_tensor("junks", [128, 1024], BF16))
    rawsum = ctx.enter_context(nc.sbuf_tensor("rawsum", [128, NT], F32))
    s1bc = ctx.enter_context(nc.sbuf_tensor("s1bc", [128, 1], F32))
    m1bc = ctx.enter_context(nc.sbuf_tensor("m1bc", [128, 1], F32))
    s2bc = ctx.enter_context(nc.sbuf_tensor("s2bc", [128, 1], F32))
    m2c = ctx.enter_context(nc.sbuf_tensor("m2c", [128, 1], F32))
    gmc = ctx.enter_context(nc.sbuf_tensor("gmc", [128, 32], F32))
    gmcb = ctx.enter_context(nc.sbuf_tensor("gmcb", [128, 32], BF16))
    b1tot = ctx.enter_context(nc.sbuf_tensor("b1tot", [128, 32], F32))
    wsumr = ctx.enter_context(nc.sbuf_tensor("wsumr", [128, 1024], BF16))
    biasr = ctx.enter_context(nc.sbuf_tensor("biasr", [128, 1024], BF16))
    biasC = ctx.enter_context(nc.sbuf_tensor("biasC", [128, 8], F32))
    wsC = ctx.enter_context(nc.sbuf_tensor("wsC", [128, 8], F32))
    wpart = ctx.enter_context(nc.sbuf_tensor("wpart", [128, 1], F32))
    wtmp = ctx.enter_context(nc.sbuf_tensor("wtmp", [128, 2], F32))

    P = Prog(nc, ctx)

    def dma(q, dst, src, sem=None):
        return P.dmaq(q, dst, src)

    def LP(build):
        def f(e):
            with nc.allow_low_precision(reason="intentional bf16 operand"):
                return build(e)
        return f

    def vgate():
        P.wait("vector", "sV", P.mark("sV"))

    def phase_barrier():
        for eng in ENGINES:
            for cn in ("sM", "sV", "sS", "sG"):
                P.wait(eng, cn, P.mark(cn))
            for q in ("sync", "gpsimd", "scalar"):
                P.wait_all(eng, q)

    def ggate():
        P.wait("gpsimd", "sG", P.mark("sG"))

    dbg_state = {"off": 0}

    def dump(buf_ap, rows=128):
        w = buf_ap.shape[-1]
        off = dbg_state["off"]
        for cn in ("sV", "sS", "sM", "sG"):
            P.wait("gpsimd", cn, P.mark(cn))
        for q in ("sync", "gpsimd", "scalar"):
            P.wait_all("gpsimd", q)
        dma("gpsimd", dbg_h.ap()[0:rows, off:off + w], buf_ap)
        dbg_state["off"] += w

    def finalize():
        for cname in ("sM", "sV", "sS", "sG"):
            P.wait("gpsimd", cname, P.mark(cname))
        for q in ("sync", "gpsimd", "scalar"):
            P.wait_all("gpsimd", q)
        with nc.Block() as block:
            @block.sync
            def _(e):
                for fn in P.plan["sync"]:
                    fn(e)

            @block.tensor
            def _(e):
                for fn in P.plan["tensor"]:
                    fn(e)

            @block.vector
            def _(e):
                for fn in P.plan["vector"]:
                    fn(e)

            @block.scalar
            def _(e):
                for fn in P.plan["scalar"]:
                    fn(e)

            @block.gpsimd
            def _(e):
                for fn in P.plan["gpsimd"]:
                    fn(e)
        return nc, ctx

    dma("sync", gam[:], gam_h.ap(), "sX")
    dma("sync", g1c[:], g1_h.ap(), "sX")
    dma("sync", b1c[:], b1_h.ap(), "sX")
    dma("sync", g2c[:], g2_h.ap(), "sX")
    dma("sync", b2c[:], b2_h.ap(), "sX")
    dma("gpsimd", wb2[:], AP(wb2_h, 0, [[128, 128], [16384, 8], [1, 128]]), "sC")
    dma("gpsimd", ident[:], id16_h.ap(), "sC")
    dma("sync", identf[:], id32_h.ap(), "sX")
    dma("gpsimd", b1lnc[:, 0:8], b1ln_h.ap(), "sC")
    dma("gpsimd", b2lnc[:, 0:32], b2ln_h.ap(), "sC")
    dma("sync", lnscr[:, 0:8], b1ln_h.ap(), "sX")
    dma("sync", lnscr2[:], b2ln_h.ap(), "sX")
    P.do("vector", lambda e: e.memset(onescol[:], 1.0))
    P.do("vector", lambda e: e.memset(onescolf[:], 1.0), inc="sV")
    P.wait_all("tensor", "sync")
    P.wait_all("vector", "sync")
    P.wait_all("scalar", "sync")
    P.wait_all("tensor", "gpsimd")
    P.wait_all("gpsimd", "gpsimd")
    P.wait_all("vector", "gpsimd")
    P.do("vector", lambda e: e.tensor_tensor(
        b1lnc[:, 8:16], lnscr[:, 0:8], b1lnc[:, 0:8], ALU.subtract), inc="sV")
    P.do("vector", lambda e: e.tensor_tensor(
        b2lnc[:, 32:64], lnscr2[:], b2lnc[:, 0:32], ALU.subtract), inc="sV")
    P.wait("tensor", "sV", P.mark("sV"))
    P.wait("tensor", "sV")

    def ln_stats_tile(t, xbuf_ap, mu, nmu, ssq, xw_v):
        if xw_v:
            P.wait("vector", "sX", xw_v)
        P.do("vector", lambda e: e.reduce_sum(rawsum[:, t:t + 1], xbuf_ap, axis=AX.X),
             inc="sV")
        vgate()
        P.do("vector", lambda e: e.tensor_scalar(
            mu[:, t:t + 1], rawsum[:, t:t + 1], 1.0 / D, None, op0=ALU.mult))
        vv = P.do("vector", lambda e: e.tensor_scalar(
            nmu[:, t:t + 1], rawsum[:, t:t + 1], -1.0 / D, None, op0=ALU.mult),
            inc="sV")
        P.wait("scalar", "sV", vv)
        P.wait("scalar", "sS", P.mark("sS"))
        P.do("scalar", lambda e: e.activation(
            out=junks[:], in_=xbuf_ap, func=AF.Square,
            bias=nmu[:, t:t + 1], scale=1.0, accum_out=ssq[:, t:t + 1]), inc="sS")

    def ln_R_batch(t0, nt, ssq, Rbuf, dim):
        sl = (slice(None), slice(t0, t0 + nt))
        P.wait("vector", "sS", P.mark("sS"))
        P.do("vector", lambda e: e.tensor_scalar(
            lnscr[sl], ssq[sl], 1.0 / dim, EPS, op0=ALU.mult, op1=ALU.add), inc="sV")
        P.wait("scalar", "sV", P.mark("sV"))
        ss = P.do("scalar", lambda e: e.activation(
            out=lnscr2[sl], in_=lnscr[sl], func=AF.Sqrt, bias=0.0, scale=1.0), inc="sS")
        P.wait("vector", "sS", ss)
        P.do("vector", lambda e: e.reciprocal(Rbuf[sl], lnscr2[sl]))
        P.do("vector", lambda e: e.tensor_scalar(
            lnscr[sl], lnscr[sl], -EPS, None, op0=ALU.add), inc="sV")
        vgate()
        P.do("vector", lambda e: e.tensor_tensor(
            lnscr[sl], lnscr[sl], Rbuf[sl], ALU.mult), inc="sV")
        vgate()
        P.do("vector", lambda e: e.tensor_tensor(
            lnscr[sl], lnscr[sl], Rbuf[sl], ALU.mult), inc="sV")
        vgate()
        vv = P.do("vector", lambda e: e.tensor_scalar(
            lnscr[sl], lnscr[sl], EPS, None, op0=ALU.add), inc="sV")
        P.wait("scalar", "sV", vv)
        ss = P.do("scalar", lambda e: e.activation(
            out=lnscr2[sl], in_=lnscr[sl], func=AF.Sqrt, bias=0.0, scale=1.0), inc="sS")
        P.wait("vector", "sS", ss)
        P.do("vector", lambda e: e.reciprocal(lnscr2[sl], lnscr2[sl]), inc="sV")
        vgate()
        P.do("vector", lambda e: e.tensor_tensor(Rbuf[sl], Rbuf[sl], lnscr2[sl], ALU.mult),
             inc="sV")
        vgate()

    attctx = ctx.enter_context(ExitStack())
    kT = attctx.enter_context(nc.sbuf_tensor("kT", [128, 8 * N], BF16))
    vaug = attctx.enter_context(nc.sbuf_tensor("vaug", [128, NT * 1040], BF16))
    tabc_s = attctx.enter_context(nc.sbuf_tensor("tabc_s", [128, NT * DH], BF16))
    tabs_s = attctx.enter_context(nc.sbuf_tensor("tabs_s", [128, NT * DH], BF16))
    dma("gpsimd", tabc_s[:], tabc_h.ap(), "sC")
    dma("gpsimd", tabs_s[:], tabs_h.ap(), "sC")
    vgate()
    P.do("vector", lambda e: e.tensor_copy(
        AP(vaug, 64, [[NT * 1040, 128], [1040, NT], [65, H]]),
        onescol[:, 0:1].to_broadcast([128, NT * H])), inc="sV")
    P.wait_all("vector", "gpsimd")

    def rope_emit(dst, ps_handle, t, nheads, prow):
        w = nheads * DH
        cap = AP(tabc_s, t * DH, [[NT * DH, 128], [0, nheads], [1, DH]])
        sap = AP(tabs_s, t * DH, [[NT * DH, 128], [0, nheads], [1, DH]])
        src = AP(ps_handle, 0, [[prow, 128], [1, w]])
        partner = AP(ps_handle, 1, [[prow, 128], [DH, nheads], [2, DH // 2], [-1, 2]])
        P.do("vector", lambda e: e.tensor_tensor(dst, src, cap, ALU.mult))
        P.do("vector", lambda e: e.tensor_tensor(junkb[:, 0:w], partner, sap, ALU.mult),
             inc="sV")
        vgate()
        return P.do("vector", lambda e: e.tensor_tensor(
            dst, dst, junkb[:, 0:w], ALU.add), inc="sV")

    kvctx = ctx.enter_context(ExitStack())
    wkvb = kvctx.enter_context(nc.sbuf_tensor("wkvb", [128, 8 * 2048], BF16))
    wslf = kvctx.enter_context(nc.sbuf_tensor("wslf", [128, 1024], F32))
    xr0 = kvctx.enter_context(nc.sbuf_tensor("xr0", [128, 4 * 1024], F32))
    xn0 = kvctx.enter_context(nc.sbuf_tensor("xn0", [128, 1024], BF16))
    xnT0 = kvctx.enter_context(nc.sbuf_tensor("xnT0", [128, 1024], BF16))
    ktile = kvctx.enter_context(nc.sbuf_tensor("ktile", [128, 1024], BF16))

    pv_wsl = 0
    wq_st = None
    for dc in range(8):
        P.wait("sync", "sV", pv_wsl)
        lw = dma("sync", wslf[:], wq_h.ap()[dc * 128:(dc + 1) * 128, :])
        P.waitt("vector", lw)
        P.waitt("vector", wq_st)
        vv = P.do("vector", lambda e, dc=dc: e.tensor_scalar(
            junkb[:], wslf[:], gam[:, dc:dc + 1], 0.125,
            op0=ALU.mult, op1=ALU.mult), inc="sV")
        pv_wsl = vv
        P.wait("scalar", "sV", vv)
        wq_st = dma("scalar", wqs_scr.ap()[dc * 128:(dc + 1) * 128, :], junkb[:])
    P.waitt("vector", wq_st)
    for dc in range(8):
        for ecg in range(2):
            P.wait("sync", "sV", pv_wsl)
            lw = dma("sync", wslf[:],
                     wkv_h.ap()[dc * 128:(dc + 1) * 128, ecg * 1024:(ecg + 1) * 1024],
                     "sX")
            P.wait("vector", "sX", lw)
            pv_wsl = P.do("vector", lambda e, dc=dc, ecg=ecg: e.tensor_scalar(
                wkvb[:, dc * 2048 + ecg * 1024: dc * 2048 + (ecg + 1) * 1024],
                wslf[:], gam[:, dc:dc + 1], None, op0=ALU.mult), inc="sV")
    P.wait("tensor", "sV", P.mark("sV"))

    kv_ev = 0
    tr_ev = 0
    sq_last = 0
    for g in range(8):
        xw = []
        P.wait("sync", "sV", tr_ev)
        P.wait("sync", "sS", sq_last)
        for i in range(4):
            t = g * 4 + i
            xw.append(dma("sync", xr0[:, i * 1024:(i + 1) * 1024],
                          x_h.ap()[t * 128:(t + 1) * 128, :], "sX"))
        for i in range(4):
            ln_stats_tile(g * 4 + i, xr0[:, i * 1024:(i + 1) * 1024], mu0, nmu0, ssq0,
                          xw[i])
        sq_last = P.mark("sS")
        ln_R_batch(g * 4, 4, ssq0, R0, D)
        for i in range(4):
            t = g * 4 + i
            vv = P.do("vector", lambda e, i=i, t=t: e.tensor_scalar(
                xn0[:], xr0[:, i * 1024:(i + 1) * 1024], mu0[:, t:t + 1], R0[:, t:t + 1],
                op0=ALU.subtract, op1=ALU.mult), inc="sV")
            P.wait("tensor", "sV", vv)
            P.wait("tensor", "sV", tr_ev)
            for d8 in range(8):
                mmv = P.do("tensor", lambda e, d8=d8: e.transpose(
                    psT[:, d8 * 128:(d8 + 1) * 128],
                    xn0[:, d8 * 128:(d8 + 1) * 128], ident[:]),
                    inc="sM" if d8 == 7 else None)
            P.wait("vector", "sM", mmv)
            tr_ev = P.do("vector", lambda e: e.tensor_copy(xnT0[:], psT[:]), inc="sV")
            P.wait("tensor", "sV", tr_ev)
            P.wait("tensor", "sV", kv_ev)
            for dc in range(8):
                for ec in range(4):
                    pb = ps01 if ec < 2 else ps23
                    mmv = P.do("tensor", lambda e, dc=dc, ec=ec, pb=pb: e.matmul(
                        pb[:, (ec % 2) * 512:((ec % 2) + 1) * 512],
                        xnT0[:, dc * 128:(dc + 1) * 128],
                        wkvb[:, dc * 2048 + ec * 512: dc * 2048 + (ec + 1) * 512],
                        start=(dc == 0), stop=(dc == 7)),
                        inc="sM" if (dc == 7 and ec == 3) else None)
            P.wait("vector", "sM", mmv)
            rope_emit(ktile[:], ps01, t, 16, 1024)
            kv_ev = P.do("vector", lambda e, t=t: e.tensor_copy(
                AP(vaug, t * 1040, [[NT * 1040, 128], [65, 16], [1, 64]]), ps23[:]), inc="sV")
            P.wait("tensor", "sV", kv_ev)
            for d8 in range(8):
                mmv = P.do("tensor", lambda e, d8=d8: e.transpose(
                    psT[:, d8 * 128:(d8 + 1) * 128],
                    ktile[:, d8 * 128:(d8 + 1) * 128], ident[:]),
                    inc="sM" if d8 == 7 else None)
            P.wait("vector", "sM", mmv)
            for d8 in range(8):
                tr_ev = P.do("vector", lambda e, d8=d8, t=t: e.tensor_copy(
                    kT[:, d8 * N + t * 128: d8 * N + (t + 1) * 128],
                    psT[:, d8 * 128:(d8 + 1) * 128]),
                    inc="sV" if d8 == 7 else None)
    P.wait("tensor", "sV", tr_ev)
    phase_barrier()
    kvctx.close()

    if stop_after == "P12":
        dump(kT[:, 0:512])
        dump(kT[:, 7 * N + 1024:7 * N + 1536])
        dump(vaug[:, 0:1040])
        dump(mu0[:])
        dump(R0[:])
        return finalize()

    ackctx = ctx.enter_context(ExitStack())
    xq = ackctx.enter_context(nc.sbuf_tensor("xq", [128, 2 * 1024], F32))
    xn_t = ackctx.enter_context(nc.sbuf_tensor("xn_t", [128, 1024], BF16))
    xnT_t = ackctx.enter_context(nc.sbuf_tensor("xnT_t", [128, 1024], BF16))
    qtile = ackctx.enter_context(nc.sbuf_tensor("qtile", [128, 512], BF16))
    qTc = ackctx.enter_context(nc.sbuf_tensor("qTc", [128, 8 * 512], BF16))
    PT = ackctx.enter_context(nc.sbuf_tensor("PT", [128, 4 * 1024], BF16))
    km2b = ackctx.enter_context(nc.sbuf_tensor("km2b", [128, 128], F32))
    kmax2 = ackctx.enter_context(nc.sbuf_tensor("kmax2", [128, 16], F32))
    qm2 = ackctx.enter_context(nc.sbuf_tensor("qm2", [128, 16], F32))
    cm2 = ackctx.enter_context(nc.sbuf_tensor("cm2", [128, 16], F32))
    sq16 = ackctx.enter_context(nc.sbuf_tensor("sq16", [128, 16], F32))
    negbf = ackctx.enter_context(nc.sbuf_tensor("negbf", [128, 16], BF16))
    negc = ackctx.enter_context(nc.sbuf_tensor("negc", [128, 16], F32))
    recr = ackctx.enter_context(nc.sbuf_tensor("recr", [128, 1024], BF16))
    recb = ackctx.enter_context(nc.sbuf_tensor("recb", [128, 1024], BF16))
    oP = ackctx.enter_context(nc.sbuf_tensor("oP", [128, 8 * 512], BF16))
    otmp = ackctx.enter_context(nc.sbuf_tensor("otmp", [128, 512], BF16))
    wsh = ackctx.enter_context(nc.sbuf_tensor("wsh", [128, 8 * 512], BF16))
    xpr = ackctx.enter_context(nc.sbuf_tensor("xpr", [128, 2 * 512], F32))

    kn_mm = [0, 0]
    kn_rd = 0
    for p8 in range(8):
        for jj in range(8):
            sel = (p8 * 8 + jj) % 2
            colb = p8 * N + jj * 512
            P.wait("scalar", "sM", kn_mm[sel])
            ssq = P.do("scalar", lambda e, colb=colb, sel=sel: e.activation(
                out=junks[:, sel * 512:(sel + 1) * 512],
                in_=kT[:, colb:colb + 512], func=AF.Square, bias=0.0, scale=1.0),
                inc="sS")
            P.wait("tensor", "sS", ssq)
            P.wait("tensor", "sV", kn_rd)
            P.do("tensor", lambda e, sel=sel: e.matmul(
                psPV0[0:1, 0:512], onescol[0:64, 0:1],
                junks[0:64, sel * 512:(sel + 1) * 512], start=True, stop=True))
            mmv = P.do("tensor", lambda e, sel=sel: e.matmul(
                psPV1[0:1, 0:512], onescol[64:128, 0:1],
                junks[64:128, sel * 512:(sel + 1) * 512], start=True, stop=True,
                tile_position=(64, 0)), inc="sM")
            kn_mm[sel] = mmv
            P.wait("vector", "sM", mmv)
            P.do("vector", lambda e, p8=p8, jj=jj: e.tensor_reduce(
                km2b[0:1, p8 * 16 + jj:p8 * 16 + jj + 1], psPV0[0:1, 0:512],
                axis=AX.X, op=ALU.max))
            kn_rd = P.do("vector", lambda e, p8=p8, jj=jj: e.tensor_reduce(
                km2b[0:1, p8 * 16 + 8 + jj:p8 * 16 + 8 + jj + 1], psPV1[0:1, 0:512],
                axis=AX.X, op=ALU.max), inc="sV")
    vgate()
    for h in range(16):
        p8, hh = h // 2, h % 2
        P.do("vector", lambda e, h=h, p8=p8, hh=hh: e.tensor_reduce(
            kmax2[0:1, h:h + 1], km2b[0:1, p8 * 16 + hh * 8:p8 * 16 + hh * 8 + 8],
            axis=AX.X, op=ALU.max), inc="sV" if h == 15 else None)
    vgate()

    nchunks = 1 if stop_after == "C1" else NQC
    xq_free = [0, 0]
    wq_free = 0
    st_w = [None, None]
    odma = [None]
    odma_list = []
    for c in range(nchunks):
        for eh in range(2):
            P.wait("sync", "sM", wq_free)
            lws = []
            for dc in range(8):
                lws.append(dma("sync", wsh[:, dc * 512:(dc + 1) * 512],
                               wqs_scr.ap()[dc * 128:(dc + 1) * 128,
                                            eh * 512:(eh + 1) * 512]))
            P.waitt("tensor", lws)
            for ti in range(4):
                t = c * 4 + ti
                sl = ti % 2
                P.wait("sync", "sV", xq_free[sl])
                xw = dma("sync", xq[:, sl * 1024:(sl + 1) * 1024],
                         x_h.ap()[t * 128:(t + 1) * 128, :], "sX")
                P.wait("vector", "sX", xw)
                xq_free[sl] = P.do("vector", lambda e, sl=sl, t=t: e.tensor_scalar(
                    xn_t[:], xq[:, sl * 1024:(sl + 1) * 1024],
                    mu0[:, t:t + 1], R0[:, t:t + 1],
                    op0=ALU.subtract, op1=ALU.mult), inc="sV")
                P.wait("tensor", "sV", xq_free[sl])
                for d8 in range(8):
                    mmv = P.do("tensor", lambda e, d8=d8: e.transpose(
                        psT[:, d8 * 128:(d8 + 1) * 128],
                        xn_t[:, d8 * 128:(d8 + 1) * 128], ident[:]),
                        inc="sM" if d8 == 7 else None)
                P.wait("vector", "sM", mmv)
                vv = P.do("vector", lambda e: e.tensor_copy(xnT_t[:], psT[:]), inc="sV")
                P.wait("tensor", "sV", vv)
                for dc in range(8):
                    mmv = P.do("tensor", lambda e, dc=dc: e.matmul(
                        psM[:, 0:512],
                        xnT_t[:, dc * 128:(dc + 1) * 128],
                        wsh[:, dc * 512:(dc + 1) * 512],
                        start=(dc == 0), stop=(dc == 7)),
                        inc="sM" if dc == 7 else None)
                wq_free = mmv
                P.wait("vector", "sM", mmv)
                rope_emit(qtile[:], psM, t, 8, 512)
                P.wait("tensor", "sV", P.mark("sV"))
                for s4 in range(4):
                    mmv = P.do("tensor", lambda e, s4=s4: e.transpose(
                        psT[:, s4 * 128:(s4 + 1) * 128],
                        qtile[:, s4 * 128:(s4 + 1) * 128], ident[:]),
                        inc="sM" if s4 == 3 else None)
                P.wait("vector", "sM", mmv)
                for s4 in range(4):
                    p = eh * 4 + s4
                    qT_ev = P.do("vector", lambda e, p=p, ti=ti, s4=s4: e.tensor_copy(
                        qTc[:, p * 512 + ti * 128: p * 512 + (ti + 1) * 128],
                        psT[:, s4 * 128:(s4 + 1) * 128]),
                        inc="sV" if s4 == 3 else None)
                P.wait("tensor", "sV", qT_ev)
        qn_mm = [0, 0]
        qn_rd = 0
        for p in range(8):
            mmv = P.do("tensor", lambda e, p=p: e.matmul(
                psM[:, 0:512], wb2[:, p * 128:(p + 1) * 128],
                qTc[:, p * 512:(p + 1) * 512], start=True, stop=True), inc="sM")
            P.wait("vector", "sM", mmv)
            vv = P.do("vector", lambda e, p=p: e.tensor_copy(
                qTc[:, p * 512:(p + 1) * 512], psM[:, 0:512]), inc="sV")
            P.wait("tensor", "sV", vv)
            sel = p % 2
            P.wait("scalar", "sM", qn_mm[sel])
            P.wait("scalar", "sV", vv)
            ssq = P.do("scalar", lambda e, p=p, sel=sel: e.activation(
                out=junks[:, sel * 512:(sel + 1) * 512],
                in_=qTc[:, p * 512:(p + 1) * 512], func=AF.Square, bias=0.0,
                scale=1.0), inc="sS")
            P.wait("tensor", "sS", ssq)
            P.wait("tensor", "sV", qn_rd)
            P.do("tensor", lambda e, sel=sel: e.matmul(
                psPV0[0:1, 0:512], onescol[0:64, 0:1],
                junks[0:64, sel * 512:(sel + 1) * 512], start=True, stop=True))
            mmq = P.do("tensor", lambda e, sel=sel: e.matmul(
                psPV1[0:1, 0:512], onescol[64:128, 0:1],
                junks[64:128, sel * 512:(sel + 1) * 512], start=True, stop=True,
                tile_position=(64, 0)), inc="sM")
            qn_mm[sel] = mmq
            P.wait("vector", "sM", mmq)
            P.do("vector", lambda e, p=p: e.tensor_reduce(
                qm2[0:1, 2 * p:2 * p + 1], psPV0[0:1, 0:512], axis=AX.X, op=ALU.max))
            qn_rd = P.do("vector", lambda e, p=p: e.tensor_reduce(
                qm2[0:1, 2 * p + 1:2 * p + 2], psPV1[0:1, 0:512], axis=AX.X,
                op=ALU.max), inc="sV")
        vgate()
        vv = P.do("vector", lambda e: e.tensor_tensor(
            cm2[0:1, 0:16], qm2[0:1, 0:16], kmax2[0:1, 0:16], ALU.mult), inc="sV")
        P.wait("scalar", "sV", vv)
        ssv = P.do("scalar", lambda e: e.activation(
            out=sq16[0:1, 0:16], in_=cm2[0:1, 0:16], func=AF.Sqrt, bias=0.0,
            scale=1.0), inc="sS")
        P.wait("vector", "sS", ssv)
        vv = P.do("vector", LP(lambda e: e.tensor_scalar(
            negbf[0:1, 0:16], sq16[0:1, 0:16], -1.0, None, op0=ALU.mult)), inc="sV")
        P.wait("tensor", "sV", vv)
        mmv = P.do("tensor", lambda e: e.matmul(
            psM[:, 0:16], onescol[0:1, 0:1].to_broadcast([1, 128]),
            negbf[0:1, 0:16], start=True, stop=True), inc="sM")
        P.wait("vector", "sM", mmv)
        negc_v = P.do("vector", lambda e: e.tensor_copy(negc[:, 0:16], psM[:, 0:16]),
                      inc="sV")

        P.wait("gpsimd", "sM", wq_free)
        wo_lw0 = []
        for cc in range(8):
            wo_lw0.append(dma("gpsimd", wsh[:, cc * 512:(cc + 1) * 512],
                              wo_h.ap()[cc * 128:(cc + 1) * 128, 0:512]))

        stat_slots = [(psM, 0)]
        slot_cons = [("sV", negc_v)]
        ring = [0]

        def next_slot():
            s = ring[0] % len(stat_slots)
            ring[0] += 1
            return s

        def waitc(eng, tok):
            if tok is not None:
                P.wait(eng, tok[0], tok[1])

        def emit_stpv(p, negc_v):
            st_slots = [(ps01, 0), (ps01, 512), (ps23, 0), (ps23, 512)]
            st_cons = [0, 0, 0, 0]
            pt_cons = [0] * 8
            ovv = P.mark("sV")
            P.wait("scalar", "sV", negc_v)
            sti = 0
            first_pv = [True]
            pv_q = []
            mm2l = [0]

            def emit_pv(j, hh, pslot, expv):
                P.wait("tensor", "sS", expv)
                if first_pv[0]:
                    P.wait("tensor", "sV", ovv)
                    first_pv[0] = False
                if hh == 0:
                    h0 = 2 * p
                    mm2l[0] = P.do("tensor", lambda e, j=j, pslot=pslot, h0=h0:
                                   e.matmul(
                                       psPV0[0:65, :],
                                       vaug[:, j * 1040 + h0 * 65:
                                            j * 1040 + h0 * 65 + 65],
                                       PT[:, pslot * 512:(pslot + 1) * 512],
                                       start=(j == 0), stop=(j == 31)), inc="sM")
                else:
                    h1 = 2 * p + 1
                    mm2l[0] = P.do("tensor", lambda e, j=j, pslot=pslot, h1=h1:
                                   e.matmul(
                                       psPV1[0:65, :],
                                       vaug[:, j * 1040 + h1 * 65:
                                            j * 1040 + h1 * 65 + 65],
                                       PT[:, pslot * 512:(pslot + 1) * 512],
                                       start=(j == 0), stop=(j == 31)), inc="sM")
                pt_cons[pslot] = mm2l[0]

            for j in range(32):
                for hh in range(2):
                    ss = sti % 4
                    sti += 1
                    spt, soff = st_slots[ss]
                    P.wait("tensor", "sS", st_cons[ss])
                    mmv = P.do("tensor", lambda e, p=p, hh=hh, j=j, spt=spt,
                               soff=soff: e.matmul(
                                   spt[:, soff:soff + 512],
                                   kT[hh * 64:(hh + 1) * 64,
                                      p * N + j * 128: p * N + (j + 1) * 128],
                                   qTc[hh * 64:(hh + 1) * 64,
                                       p * 512:(p + 1) * 512],
                                   start=True, stop=True,
                                   tile_position=(hh * 64, 0)), inc="sM")
                    pslot = (j % 4) * 2 + hh
                    P.wait("scalar", "sM", mmv)
                    P.wait("scalar", "sM", pt_cons[pslot])
                    h = 2 * p + hh
                    st_cons[ss] = P.do("scalar", lambda e, spt=spt, soff=soff,
                                       pslot=pslot, h=h: e.activation(
                                           out=PT[:, pslot * 512:(pslot + 1) * 512],
                                           in_=spt[:, soff:soff + 512], func=AF.Exp,
                                           bias=negc[:, h:h + 1], scale=1.0),
                                       inc="sS")
                    pv_q.append((j, hh, pslot, st_cons[ss]))
                    if len(pv_q) > 4:
                        emit_pv(*pv_q.pop(0))
            while pv_q:
                emit_pv(*pv_q.pop(0))
            mmv = mm2l[0]
            P.wait("vector", "sM", mmv)
            P.do("vector", LP(lambda e: e.reciprocal(recr[64:65, 0:512],
                                                     psPV0[64:65, :])))
            vv = P.do("vector", LP(lambda e: e.reciprocal(
                recr[64:65, 512:1024], psPV1[64:65, :])), inc="sV")
            P.wait("tensor", "sV", vv)
            sA = next_slot()
            ptA, offA = stat_slots[sA]
            waitc("tensor", slot_cons[sA])
            P.do("tensor", lambda e, ptA=ptA, offA=offA: e.matmul(
                ptA[0:64, offA:offA + 512], onescol[64:65, 0:1].to_broadcast([1, 64]),
                recr[64:65, 0:512], start=True, stop=True, tile_position=(64, 0)))
            sB = next_slot()
            ptB, offB = stat_slots[sB]
            waitc("tensor", slot_cons[sB])
            mmv = P.do("tensor", lambda e, ptB=ptB, offB=offB: e.matmul(
                ptB[0:64, offB:offB + 512], onescol[64:65, 0:1].to_broadcast([1, 64]),
                recr[64:65, 512:1024], start=True, stop=True, tile_position=(64, 0)),
                inc="sM")
            P.wait("scalar", "sM", mmv)
            P.wait("scalar", "sS", P.mark("sS"))
            scv0 = P.do("scalar", lambda e, ptA=ptA, offA=offA: e.copy(
                recb[0:64, 0:512], ptA[0:64, offA:offA + 512]), inc="sS")
            scv = P.do("scalar", lambda e, ptB=ptB, offB=offB: e.copy(
                recb[0:64, 512:1024], ptB[0:64, offB:offB + 512]), inc="sS")
            slot_cons[sA] = ("sS", scv0)
            slot_cons[sB] = ("sS", scv)
            P.wait("vector", "sS", scv)
            P.waitt("vector", odma[0])
            P.do("vector", lambda e, p=p: e.tensor_tensor(
                oP[0:64, p * 512:(p + 1) * 512], psPV0[0:64, :], recb[0:64, 0:512],
                ALU.mult))
            ovv = P.do("vector", lambda e, p=p: e.tensor_tensor(
                otmp[0:64, :], psPV1[0:64, :], recb[0:64, 512:1024], ALU.mult),
                inc="sV")
            P.wait("gpsimd", "sV", ovv)
            odma[0] = dma("gpsimd", oP[64:128, p * 512:(p + 1) * 512], otmp[0:64, :])
            odma_list.append(odma[0])
            P.wait("tensor", "sV", ovv)
            return P.mark("sS")

        odma_list.clear()
        for p in range(8):
            emit_stpv(p, negc_v)
        mmv = P.mark("sM")

        P.waitt("tensor", list(odma_list))
        for eh in range(2):
            if eh == 1:
                P.wait("gpsimd", "sM", mmv)
                wo_lw1 = []
                for cc in range(8):
                    wo_lw1.append(dma("gpsimd", wsh[:, cc * 512:(cc + 1) * 512],
                                      wo_h.ap()[cc * 128:(cc + 1) * 128, 512:1024]))
                P.waitt("tensor", wo_lw1)
            else:
                P.waitt("tensor", wo_lw0)
            for cc in range(8):
                for ti in range(4):
                    pb = ps01 if ti < 2 else ps23
                    mmv = P.do("tensor", lambda e, cc=cc, ti=ti, pb=pb: e.matmul(
                        pb[:, (ti % 2) * 512:((ti % 2) + 1) * 512],
                        oP[:, cc * 512 + ti * 128: cc * 512 + (ti + 1) * 128],
                        wsh[:, cc * 512:(cc + 1) * 512],
                        start=(cc == 0), stop=(cc == 7)),
                        inc="sM" if (cc == 7 and ti == 3) else None)
            P.wait("vector", "sM", mmv)
            for ti in range(4):
                pb = ps01 if ti < 2 else ps23
                slw = ti % 2
                P.waitt("vector", st_w[slw])
                vv = P.do("vector", lambda e, ti=ti, pb=pb, slw=slw: e.tensor_copy(
                    xpr[:, slw * 512:(slw + 1) * 512],
                    pb[:, (ti % 2) * 512:((ti % 2) + 1) * 512]), inc="sV")
                P.wait("scalar", "sV", vv)
                t = c * 4 + ti
                st_w[slw] = dma(
                    "scalar", xa_scr.ap()[t * 128:(t + 1) * 128,
                                          eh * 512:(eh + 1) * 512],
                    xpr[:, slw * 512:(slw + 1) * 512])
            P.wait("tensor", "sV", P.mark("sV"))
        wq_free = P.mark("sM")

    if stop_after == "C1":
        dump(qTc[:, 0:512])
        dump(negc[:])
        dump(oP[:, 0:512])
        dump(oP[:, 3 * 512:4 * 512])
        dump(PT[:, 0:1024])
        dump(recb[:])
        return finalize()

    phase_barrier()
    ackctx.close()

    wpctx = ctx.enter_context(ExitStack())
    wfsl = wpctx.enter_context(nc.sbuf_tensor("wfsl", [128, 4096], F32))
    tsl = wpctx.enter_context(nc.sbuf_tensor("tsl", [128, 4096], BF16))
    tT = wpctx.enter_context(nc.sbuf_tensor("tT", [128, 4096], BF16))
    t8 = wpctx.enter_context(nc.sbuf_tensor("t8", [128, 1024], FP8))
    gtb = wpctx.enter_context(nc.sbuf_tensor("gtb", [128, 4096], BF16))

    def wmean(w_hh, nslab, slabw, outbc, outm):
        P.wait("sync", "sG", P.mark("sG"))
        P.do("vector", lambda e: e.memset(wpart[:], 0.0))
        pv = 0
        for s in range(nslab):
            P.wait("sync", "sV", pv)
            lw = dma("sync", wfsl[:, 0:slabw], w_hh.ap()[s * 128:(s + 1) * 128, :], "sX")
            P.wait("vector", "sX", lw)
            P.do("vector", lambda e, slabw=slabw: e.reduce_sum(
                wtmp[:, 0:1], wfsl[:, 0:slabw], axis=AX.X, apply_absolute_value=True),
                inc="sV")
            vgate()
            pv = P.do("vector", lambda e: e.tensor_tensor(
                wpart[:], wpart[:], wtmp[:, 0:1], ALU.add), inc="sV")
        vgate()
        vv = P.mark("sV")
        P.wait("tensor", "sV", vv)
        mmv = P.do("tensor", lambda e: e.matmul(
            psM[0:1, 0:1], onescolf[:, 0:1], wpart[:], start=True, stop=True),
            inc="sM")
        P.wait("vector", "sM", mmv)
        P.do("vector", lambda e: e.tensor_scalar(
            wtmp[0:1, 0:1], psM[0:1, 0:1], 1.0 / (HID * D), 1e-5,
            op0=ALU.mult, op1=ALU.max), inc="sV")
        vgate()
        P.do("vector", lambda e: e.reciprocal(wtmp[0:1, 1:2], wtmp[0:1, 0:1]), inc="sV")
        vgate()
        vv = P.mark("sV")
        P.wait("tensor", "sV", vv)
        mmv = P.do("tensor", lambda e: e.matmul(
            psM[:, 0:2], onescolf[0:1, 0:1].to_broadcast([1, 128]), wtmp[0:1, 0:2],
            start=True, stop=True), inc="sM")
        P.wait("vector", "sM", mmv)
        P.do("vector", lambda e: e.tensor_copy(outm[:], psM[:, 0:1]), inc="sV")
        vv = P.do("vector", lambda e: e.tensor_copy(outbc[:], psM[:, 1:2]), inc="sV")
        P.wait("gpsimd", "sV", vv)
        P.wait("tensor", "sV", vv)
        P.wait("sync", "sV", vv)
        vgate()

    wmean(w1_h, 32, 1024, s1bc, m1bc)
    ev_v = 0
    st_w1 = None
    tp_m = 0
    for s in range(32):
        P.wait("sync", "sG", P.mark("sG"))
        lw = dma("sync", wfsl[:, 0:1024], w1_h.ap()[s * 128:(s + 1) * 128, :], "sX")
        P.wait("vector", "sX", lw)
        P.wait("vector", "sM", tp_m)
        P.do("vector", lambda e: e.tensor_scalar(
            gtb[:, 0:1024], wfsl[:, 0:1024], s1bc[:], 0.5, op0=ALU.mult,
            op1=ALU.is_gt), inc="sG")
        P.do("vector", lambda e: e.tensor_scalar(
            tsl[:, 0:1024], wfsl[:, 0:1024], s1bc[:], -0.5, op0=ALU.mult,
            op1=ALU.is_lt), inc="sG")
        P.wait("vector", "sG", P.mark("sG"))
        gv = P.do("vector", lambda e: e.tensor_tensor(
            tsl[:, 0:1024], gtb[:, 0:1024], tsl[:, 0:1024], ALU.subtract), inc="sG")
        P.wait("tensor", "sG", gv)
        P.wait("tensor", "sV", ev_v)
        for dc in range(8):
            tp_m = P.do("tensor", lambda e, dc=dc: e.transpose(
                psT[:, dc * 128:(dc + 1) * 128],
                tsl[:, dc * 128:(dc + 1) * 128], ident[:]),
                inc="sM" if dc == 7 else None)
        P.wait("vector", "sM", tp_m)
        P.waitt("vector", st_w1)
        ev_v = P.do("vector", lambda e: e.tensor_copy(t8[:, 0:1024], psT[:]), inc="sV")
        P.wait("scalar", "sV", ev_v)
        st_w1 = dma("scalar", w1t_scr.ap()[s, :, :], t8[:, 0:1024])

    wmean(w2_h, 8, 4096, s2bc, m2c)
    P.do("vector", lambda e: e.tensor_scalar(
        gmc[:], g2c[:], m2c[:], None, op0=ALU.mult), inc="sV")
    vgate()
    vv = P.do("vector", lambda e: e.tensor_copy(gmcb[:], gmc[:]), inc="sV")
    P.wait("tensor", "sV", vv)
    ev_v = 0
    st_w2 = None
    tp_m = 0
    sm_m = 0
    for s in range(8):
        P.wait("sync", "sG", P.mark("sG"))
        lw = dma("sync", wfsl[:], w2_h.ap()[s * 128:(s + 1) * 128, :], "sX")
        P.wait("vector", "sX", lw)
        P.wait("vector", "sM", tp_m)
        P.do("vector", lambda e: e.tensor_scalar(
            gtb[:], wfsl[:], s2bc[:], 0.5, op0=ALU.mult, op1=ALU.is_gt), inc="sG")
        P.do("vector", lambda e: e.tensor_scalar(
            tsl[:], wfsl[:], s2bc[:], -0.5, op0=ALU.mult, op1=ALU.is_lt), inc="sG")
        P.wait("vector", "sG", P.mark("sG"))
        gv = P.do("vector", lambda e: e.tensor_tensor(
            tsl[:], gtb[:], tsl[:], ALU.subtract), inc="sG")
        P.wait("tensor", "sG", gv)
        P.waitt("tensor", st_w2)
        for hg in range(4):
            P.wait("tensor", "sV", ev_v)
            for k8 in range(8):
                hc = hg * 8 + k8
                tp_m = P.do("tensor", lambda e, hc=hc, k8=k8: e.transpose(
                    psT[:, k8 * 128:(k8 + 1) * 128],
                    tsl[:, hc * 128:(hc + 1) * 128], ident[:]),
                    inc="sM" if k8 == 7 else None)
            P.wait("vector", "sM", tp_m)
            ev_v = P.do("vector", lambda e, hg=hg: e.tensor_copy(
                tT[:, hg * 1024:(hg + 1) * 1024], psT[:]), inc="sV")
        P.wait("tensor", "sV", ev_v)
        for hc in range(32):
            P.do("tensor", lambda e, hc=hc: e.matmul(
                psPV0[:, 0:1], tT[:, hc * 128:(hc + 1) * 128], gmcb[:, hc:hc + 1],
                start=(hc == 0), stop=(hc == 31)))
            P.do("tensor", lambda e, hc=hc: e.matmul(
                psPV1[:, 0:1], tT[:, hc * 128:(hc + 1) * 128], b2lnc[:, hc:hc + 1],
                start=(hc == 0), stop=False))
            sm_m = P.do("tensor", lambda e, hc=hc: e.matmul(
                psPV1[:, 0:1], tT[:, hc * 128:(hc + 1) * 128],
                b2lnc[:, 32 + hc: 33 + hc],
                start=False, stop=(hc == 31)), inc="sM" if hc == 31 else None)
        P.wait("vector", "sM", sm_m)
        P.do("vector", lambda e, s=s: e.tensor_copy(wsC[:, s:s + 1], psPV0[:, 0:1]),
             inc="sV")
        P.do("vector", lambda e, s=s: e.tensor_copy(biasC[:, s:s + 1], psPV1[:, 0:1]),
             inc="sV")
        for hc in range(32):
            ev_v = P.do("vector", lambda e, hc=hc: e.tensor_scalar(
                tT[:, hc * 128:(hc + 1) * 128], tT[:, hc * 128:(hc + 1) * 128],
                gmc[:, hc:hc + 1], None, op0=ALU.mult),
                inc="sV" if hc == 31 else None)
        P.wait("scalar", "sV", ev_v)
        st_w2 = dma("scalar", w2t_scr.ap()[s, :, :], tT[:])
    vgate()
    P.do("vector", lambda e: e.tensor_scalar(
        biasC[:], biasC[:], m2c[:], None, op0=ALU.mult), inc="sV")
    vgate()
    P.do("vector", lambda e: e.tensor_tensor(biasC[:], biasC[:], b2c[:], ALU.add),
         inc="sV")
    vgate()
    P.do("vector", lambda e: e.tensor_copy(junkb[:, 0:8], wsC[:]))
    vv = P.do("vector", lambda e: e.tensor_copy(junkb[:, 8:16], biasC[:]), inc="sV")
    P.wait("tensor", "sV", vv)
    for s in range(8):
        P.do("tensor", lambda e, s=s: e.transpose(
            psT[0:1, 0:128], junkb[:, s:s + 1], ident[:]))
        mmv = P.do("tensor", lambda e, s=s: e.transpose(
            psT[0:1, 128:256], junkb[:, 8 + s:9 + s], ident[:]), inc="sM")
        P.wait("vector", "sM", mmv)
        P.do("vector", lambda e, s=s: e.tensor_copy(
            wsumr[0:1, s * 128:(s + 1) * 128], psT[0:1, 0:128]))
        vv = P.do("vector", lambda e, s=s: e.tensor_copy(
            biasr[0:1, s * 128:(s + 1) * 128], psT[0:1, 128:256]), inc="sV")
        P.wait("tensor", "sV", vv)
    phase_barrier()
    wpctx.close()
    attctx.close()

    if stop_after == "WPREP":
        dump(wsumr[0:1, :], rows=1)
        dump(biasr[0:1, :], rows=1)
        dump(m1bc[:, 0:1])
        dump(m2c[:, 0:1])
        return finalize()

    fctx = ctx.enter_context(ExitStack())
    xpc = fctx.enter_context(nc.sbuf_tensor("xpc", [128, 8 * 1024], F32))
    xn2 = fctx.enter_context(nc.sbuf_tensor("xn2", [128, 1024], BF16))
    xn2gT = fctx.enter_context(nc.sbuf_tensor("xn2gT", [128, 8 * 1024], BF16))
    uP = fctx.enter_context(nc.sbuf_tensor("uP", [128, 32 * 1024], BF16))
    u2r = fctx.enter_context(nc.sbuf_tensor("u2r", [128, 2 * 1024], BF16))
    w1r = fctx.enter_context(nc.sbuf_tensor("w1r", [128, 2 * 1024], FP8))
    w2r = fctx.enter_context(nc.sbuf_tensor("w2r", [128, 2 * 4096], BF16))
    rows_ = fctx.enter_context(nc.sbuf_tensor("rows_", [128, 3 * 1024], F32))
    rowsb = fctx.enter_context(nc.sbuf_tensor("rowsb", [128, 2 * 1024], BF16))
    fRc = fctx.enter_context(nc.sbuf_tensor("fRc", [128, 8], F32))
    w2o = fctx.enter_context(nc.sbuf_tensor("w2o", [128, 1024], BF16))
    ybuf = fctx.enter_context(nc.sbuf_tensor("ybuf", [128, 8 * 1024], F32))
    fmu = fctx.enter_context(nc.sbuf_tensor("fmu", [128, 8], F32))
    fnmu = fctx.enter_context(nc.sbuf_tensor("fnmu", [128, 8], F32))
    fssq = fctx.enter_context(nc.sbuf_tensor("fssq", [128, 8], F32))
    fR = fctx.enter_context(nc.sbuf_tensor("fR", [128, 8], F32))
    RA, RB, RC = 0, 1024, 2048

    yst_w = None
    ych_v = 0
    for f in range(NFC):
        P.waitt("sync", yst_w)
        P.wait("sync", "sV", ych_v)
        xw = []
        for ti in range(8):
            t = f * 8 + ti
            t1 = dma("sync", xpc[:, ti * 1024:(ti + 1) * 1024],
                     xa_scr.ap()[t * 128:(t + 1) * 128, :])
            t2 = dma("sync", ybuf[:, ti * 1024:(ti + 1) * 1024],
                     x_h.ap()[t * 128:(t + 1) * 128, :])
            xw.append([t1, t2])
        for ti in range(8):
            P.waitt("vector", xw[ti])
            P.do("vector", lambda e, ti=ti: e.tensor_tensor(
                xpc[:, ti * 1024:(ti + 1) * 1024], xpc[:, ti * 1024:(ti + 1) * 1024],
                ybuf[:, ti * 1024:(ti + 1) * 1024], ALU.add), inc="sV")
        vgate()
        for ti in range(8):
            ln_stats_tile(ti, xpc[:, ti * 1024:(ti + 1) * 1024], fmu, fnmu, fssq, 0)
        ln_R_batch(0, 8, fssq, fR, D)
        for ti in range(8):
            vv = P.do("vector", lambda e, ti=ti: e.tensor_scalar(
                xn2[:], xpc[:, ti * 1024:(ti + 1) * 1024],
                fmu[:, ti:ti + 1], fR[:, ti:ti + 1],
                op0=ALU.subtract, op1=ALU.mult), inc="sV")
            P.wait("tensor", "sV", vv)
            for d8 in range(8):
                mmv = P.do("tensor", lambda e, d8=d8: e.transpose(
                    psT[:, d8 * 128:(d8 + 1) * 128],
                    xn2[:, d8 * 128:(d8 + 1) * 128], ident[:]),
                    inc="sM" if d8 == 7 else None)
            P.wait("vector", "sM", mmv)
            for d8 in range(8):
                vv = P.do("vector", lambda e, d8=d8, ti=ti: e.tensor_scalar(
                    xn2gT[:, d8 * 1024 + ti * 128: d8 * 1024 + (ti + 1) * 128],
                    psT[:, d8 * 128:(d8 + 1) * 128],
                    g1c[:, d8:d8 + 1], None, op0=ALU.mult),
                    inc="sV" if d8 == 7 else None)
            P.wait("tensor", "sV", vv)

        gelu_s = [0, 0]
        w1mm = [0, 0]
        for hc in range(32):
            slot = hc % 2
            P.wait("sync", "sM", w1mm[slot])
            w1lw = dma("sync", w1r[:, slot * 1024:(slot + 1) * 1024],
                       w1t_scr.ap()[hc, :, :], "sX")
            pb = ps01 if slot == 0 else ps23
            P.wait("tensor", "sS", gelu_s[slot])
            P.wait("tensor", "sX", w1lw)
            for th in range(2):
                for dc in range(8):
                    mmv = P.do("tensor", lambda e, th=th, dc=dc, pb=pb, slot=slot:
                               e.matmul(
                                   pb[:, th * 512:(th + 1) * 512],
                                   w1r[:, slot * 1024 + dc * 128:
                                       slot * 1024 + (dc + 1) * 128],
                                   xn2gT[:, dc * 1024 + th * 512:
                                         dc * 1024 + (th + 1) * 512],
                                   start=(dc == 0), stop=(dc == 7)),
                               inc="sM" if (th == 1 and dc == 7) else None)
            if f == 0:
                for half16 in range(2):
                    for dc in range(8):
                        bmv = P.do("tensor", lambda e, dc=dc, slot=slot, half16=half16:
                                   e.matmul(
                                       psPV0[:, 0:1],
                                       w1r[:, slot * 1024 + dc * 128:
                                           slot * 1024 + (dc + 1) * 128],
                                       b1lnc[:, half16 * 8 + dc: half16 * 8 + dc + 1],
                                       start=(half16 == 0 and dc == 0),
                                       stop=(half16 == 1 and dc == 7)),
                                   inc="sM" if (half16 == 1 and dc == 7) else None)
                w1mm[slot] = bmv
                P.wait("vector", "sM", bmv)
                P.do("vector", lambda e, hc=hc: e.tensor_scalar(
                    b1tot[:, hc:hc + 1], psPV0[:, 0:1], m1bc[:], None, op0=ALU.mult),
                    inc="sV")
                vgate()
                vv = P.do("vector", lambda e, hc=hc: e.tensor_tensor(
                    b1tot[:, hc:hc + 1], b1tot[:, hc:hc + 1], b1c[:, hc:hc + 1],
                    ALU.add), inc="sV")
                P.wait("scalar", "sV", vv)
                P.wait("tensor", "sV", vv)
            else:
                w1mm[slot] = mmv
            P.wait("scalar", "sM", mmv)
            gelu_s[slot] = P.do("scalar", lambda e, hc=hc, pb=pb: e.activation(
                out=uP[:, hc * 1024:(hc + 1) * 1024], in_=pb[:], func=AF.Gelu,
                bias=b1tot[:, hc:hc + 1], scale=m1bc[:]), inc="sS")
        P.wait("tensor", "sS", P.mark("sS"))

        for hc in range(32):
            for th in range(2):
                grp = psM if th == 0 else psPV0
                mmv = P.do("tensor", lambda e, hc=hc, th=th, grp=grp: e.matmul(
                    grp[0:1, :], onescol[:, 0:1],
                    uP[:, hc * 1024 + th * 512: hc * 1024 + (th + 1) * 512],
                    start=(hc == 0), stop=(hc == 31)),
                    inc="sM" if (hc == 31 and th == 1) else None)
        P.wait("vector", "sM", mmv)
        P.do("vector", lambda e: e.tensor_scalar(
            rows_[0:1, RA:RA + 512], psM[0:1, :], 1.0 / HID, None, op0=ALU.mult))
        vv = P.do("vector", lambda e: e.tensor_scalar(
            rows_[0:1, RA + 512:RA + 1024], psPV0[0:1, :], 1.0 / HID, None,
            op0=ALU.mult), inc="sV")
        P.wait("tensor", "sV", vv)
        u2_m = [0, 0]
        for hc in range(32):
            slot = hc % 2
            P.wait("vector", "sM", u2_m[slot])
            uvv = P.do("vector", lambda e, hc=hc, slot=slot: e.tensor_tensor(
                u2r[:, slot * 1024:(slot + 1) * 1024],
                uP[:, hc * 1024:(hc + 1) * 1024],
                uP[:, hc * 1024:(hc + 1) * 1024], ALU.mult), inc="sV")
            P.wait("tensor", "sV", uvv)
            for th in range(2):
                grp = psM if th == 0 else psPV0
                u2_m[slot] = P.do("tensor", lambda e, hc=hc, th=th, grp=grp, slot=slot:
                                  e.matmul(
                                      grp[0:1, :], onescol[:, 0:1],
                                      u2r[:, slot * 1024 + th * 512:
                                          slot * 1024 + (th + 1) * 512],
                                      start=(hc == 0), stop=(hc == 31)),
                                  inc="sM" if th == 1 else None)
        P.wait("vector", "sM", max(u2_m))
        P.do("vector", lambda e: e.tensor_scalar(
            rows_[0:1, RB:RB + 512], psM[0:1, :], 1.0 / HID, None, op0=ALU.mult))
        P.do("vector", lambda e: e.tensor_scalar(
            rows_[0:1, RB + 512:RB + 1024], psPV0[0:1, :], 1.0 / HID, None,
            op0=ALU.mult))
        vgate()
        P.do("vector", lambda e: e.tensor_scalar(
            rowsb[0:1, 0:1024], rows_[0:1, RA:RA + 1024], -1.0, None, op0=ALU.mult))
        P.do("vector", lambda e: e.tensor_tensor(
            rows_[0:1, RC:RC + 1024], rows_[0:1, RA:RA + 1024],
            rows_[0:1, RA:RA + 1024], ALU.mult), inc="sV")
        vgate()
        P.do("vector", lambda e: e.tensor_tensor(
            rows_[0:1, RB:RB + 1024], rows_[0:1, RB:RB + 1024],
            rows_[0:1, RC:RC + 1024], ALU.subtract), inc="sV")
        vgate()
        vv = P.do("vector", lambda e: e.tensor_scalar(
            rows_[0:1, RB:RB + 1024], rows_[0:1, RB:RB + 1024], EPS, None,
            op0=ALU.add), inc="sV")
        P.wait("scalar", "sV", vv)
        ss = P.do("scalar", lambda e: e.activation(
            out=rows_[0:1, RA:RA + 1024], in_=rows_[0:1, RB:RB + 1024], func=AF.Sqrt,
            bias=0.0, scale=1.0), inc="sS")
        P.wait("vector", "sS", ss)
        P.do("vector", lambda e: e.tensor_copy(
            rowsb[0:1, 1024:2048], rows_[0:1, RA:RA + 1024]))
        vv = P.do("vector", lambda e: e.reciprocal(
            rows_[0:1, RC:RC + 1024], rows_[0:1, RA:RA + 1024]), inc="sV")
        vgate()
        P.wait("tensor", "sV", vv)
        for ti in range(8):
            mmv = P.do("tensor", lambda e, ti=ti: e.transpose(
                psM[:, 0:1], rows_[0:1, RC + ti * 128: RC + (ti + 1) * 128],
                identf[0:1, 0:1]), inc="sM")
            P.wait("vector", "sM", mmv)
            vv = P.do("vector", lambda e, ti=ti: e.tensor_copy(
                fRc[:, ti:ti + 1], psM[:, 0:1]), inc="sV")
            P.wait("tensor", "sV", vv)

        w2mm = [0, 0]
        ev_done = [0, 0]
        for dc in range(8):
            slot = dc % 2
            P.wait("sync", "sM", w2mm[slot])
            w2lw = dma("sync", w2r[:, slot * 4096:(slot + 1) * 4096],
                       w2t_scr.ap()[dc, :, :], "sX")
            pb = ps01 if slot == 0 else ps23
            P.wait("tensor", "sV", ev_done[slot])
            P.wait("tensor", "sX", w2lw)
            for th in range(2):
                for hc in range(32):
                    P.do("tensor", lambda e, th=th, hc=hc, pb=pb, slot=slot: e.matmul(
                        pb[:, th * 512:(th + 1) * 512],
                        w2r[:, slot * 4096 + hc * 128: slot * 4096 + (hc + 1) * 128],
                        uP[:, hc * 1024 + th * 512: hc * 1024 + (th + 1) * 512],
                        start=(hc == 0), stop=False))
                P.do("tensor", lambda e, th=th, dc=dc, pb=pb: e.matmul(
                    pb[:, th * 512:(th + 1) * 512],
                    wsumr[0:1, dc * 128:(dc + 1) * 128],
                    rowsb[0:1, th * 512:(th + 1) * 512],
                    start=False, stop=False))
                mmv = P.do("tensor", lambda e, th=th, dc=dc, pb=pb: e.matmul(
                    pb[:, th * 512:(th + 1) * 512],
                    biasr[0:1, dc * 128:(dc + 1) * 128],
                    rowsb[0:1, 1024 + th * 512: 1024 + (th + 1) * 512],
                    start=False, stop=True), inc="sM" if th == 1 else None)
            w2mm[slot] = mmv
            P.wait("vector", "sM", mmv)
            vv = P.do("vector", lambda e, pb=pb: e.tensor_copy(w2o[:], pb[:]), inc="sV")
            ev_done[slot] = vv
            P.wait("tensor", "sV", vv)
            for tg in range(2):
                for k4 in range(4):
                    ti = tg * 4 + k4
                    mmv = P.do("tensor", lambda e, ti=ti, k4=k4: e.transpose(
                        psT[:, k4 * 128:(k4 + 1) * 128],
                        w2o[:, ti * 128:(ti + 1) * 128], ident[:]),
                        inc="sM" if k4 == 3 else None)
                P.wait("vector", "sM", mmv)
                for k4 in range(4):
                    ti = tg * 4 + k4
                    ych_v = P.do("vector", lambda e, ti=ti, k4=k4, dc=dc:
                                 e.scalar_tensor_tensor(
                                     ybuf[:, ti * 1024 + dc * 128:
                                          ti * 1024 + (dc + 1) * 128],
                                     psT[:, k4 * 128:(k4 + 1) * 128],
                                     fRc[:, ti:ti + 1],
                                     xpc[:, ti * 1024 + dc * 128:
                                         ti * 1024 + (dc + 1) * 128],
                                     op0=ALU.mult, op1=ALU.add),
                                 inc="sV" if k4 == 3 else None)
                P.wait("tensor", "sV", ych_v)
        P.wait("scalar", "sV", ych_v)
        yst_w = []
        for ti in range(8):
            t = f * 8 + ti
            yst_w.append(dma("scalar", y_h.ap()[t * 128:(t + 1) * 128, :],
                             ybuf[:, ti * 1024:(ti + 1) * 1024]))

    return finalize()


def kernel(L0_attn_x, L0_attn_gamma, L0_attn_Wq, L0_attn_Wkv, L0_attn_W_bilinear,
           L0_attn_Wo, L0_ffn_ln1_g, L0_ffn_ln1_b, L0_ffn_w1, L0_ffn_b1,
           L0_ffn_ln2_g, L0_ffn_ln2_b, L0_ffn_w2, L0_ffn_b2):
    tabc, tabs = _host_consts()
    id16 = np.eye(128, dtype=np.float32)
    id32 = np.eye(128, dtype=np.float32)
    wb = np.asarray(L0_attn_W_bilinear, np.float32)
    wb2 = np.zeros((8, 128, 128), np.float32)
    for p in range(8):
        wb2[p, 0:64, 0:64] = wb[2 * p]
        wb2[p, 64:128, 64:128] = wb[2 * p + 1]

    common = {
        "wq": np.ascontiguousarray(np.asarray(L0_attn_Wq, np.float32)),
        "wkv": np.ascontiguousarray(np.asarray(L0_attn_Wkv, np.float32)),
        "wb2": wb2,
        "wo": np.ascontiguousarray(np.asarray(L0_attn_Wo, np.float32)),
        "w1": np.ascontiguousarray(np.asarray(L0_ffn_w1, np.float32)),
        "w2": np.ascontiguousarray(np.asarray(L0_ffn_w2, np.float32)),
        "gam": _cols(L0_attn_gamma),
        "g1": _cols(L0_ffn_ln1_g),
        "b1ln": _cols(L0_ffn_ln1_b),
        "b1": _cols(L0_ffn_b1),
        "g2": _cols(L0_ffn_ln2_g),
        "b2ln": _cols(L0_ffn_ln2_b),
        "b2": _cols(L0_ffn_b2),
        "tabc": tabc,
        "tabs": tabs,
        "id16": id16,
        "id32": id32,
    }
    x = np.asarray(L0_attn_x, np.float32)
    in_maps = []
    for b in range(NCORES):
        m = dict(common)
        m["x"] = np.ascontiguousarray(x[b])
        in_maps.append(m)

    nc, ctx = build_nc()
    try:
        res = run_bass_kernel_spmd(nc, in_maps, core_ids=list(range(NCORES)))
    finally:
        ctx.close()
    return np.stack([r["y"] for r in res.results]).astype(np.float32)


# version 08
# speedup vs baseline: 1.3206x; 1.3206x over previous
import numpy as np
import ml_dtypes
from contextlib import ExitStack
import concourse.bass as bass
import concourse.mybir as mybir
from concourse.bass_utils import run_bass_kernel_spmd

F32 = mybir.dt.float32
BF16 = mybir.dt.bfloat16
FP8 = mybir.dt.float8e4
AF = mybir.ActivationFunctionType
ALU = mybir.AluOpType
AX = mybir.AxisListType

NCORES = 8
N = 4096
D = 1024
H = 16
DH = 64
HID = 4096
NT = 32
QC = 512
NQC = N // QC
FT = 1024
NFC = N // FT
EPS = 1e-5
ENGINES = ("sync", "tensor", "vector", "scalar", "gpsimd")


def _host_consts():
    theta = 1.0 / (10000.0 ** (np.arange(0, DH, 2, dtype=np.float64) / DH))
    pos = np.arange(N, dtype=np.float64)
    freqs = pos[:, None] * theta[None, :]
    emb = np.concatenate([freqs, freqs], axis=-1)
    cos = np.cos(emb)
    ssin = np.sin(emb)
    ssin[:, 0::2] *= -1.0

    def tl(a):
        return np.ascontiguousarray(
            a.reshape(NT, 128, DH).transpose(1, 0, 2).reshape(128, NT * DH)
        ).astype(np.float32)
    return tl(cos), tl(ssin)


def _cols(v):
    n = np.asarray(v).shape[0]
    return np.ascontiguousarray(np.asarray(v, np.float32).reshape(n // 128, 128).T)


class Ctr:
    __slots__ = ("sem", "n")

    def __init__(self, sem):
        self.sem = sem
        self.n = 0


class DmaRing:

    def __init__(self, nc, ctx, prefix, depth):
        self.ctrs = [Ctr(ctx.enter_context(nc.semaphore(f"{prefix}{i}")))
                     for i in range(depth)]
        self.i = 0

    def next(self):
        c = self.ctrs[self.i % len(self.ctrs)]
        self.i += 1
        prev = c.n
        c.n += 16
        return c, prev


class Prog:
    def __init__(self, nc, ctx):
        self.nc = nc
        self.plan = {e: [] for e in ENGINES}
        self.ctr = {}
        for name in ("sM", "sV", "sS", "sG"):
            self.ctr[name] = Ctr(ctx.enter_context(nc.semaphore(name)))
        self.rings = {
            "sync": DmaRing(nc, ctx, "rX", 10),
            "gpsimd": DmaRing(nc, ctx, "rC", 6),
            "scalar": DmaRing(nc, ctx, "rW", 6),
        }

    def emit(self, eng, fn):
        self.plan[eng].append(fn)

    def wait(self, eng, cname, val=None):
        if isinstance(val, tuple):
            return self.waitt(eng, val)
        c = self.ctr[cname]
        v = c.n if val is None else val
        if v is None or v <= 0:
            return
        assert v <= c.n, f"wait on future {cname}: {v} > {c.n}"
        self.emit(eng, lambda e, s=c.sem, v=v: e.wait_ge(s, v))

    def waitt(self, eng, tok):
        if not tok:
            return
        if isinstance(tok, list):
            for t in tok:
                self.waitt(eng, t)
            return
        c, v = tok
        self.emit(eng, lambda e, s=c.sem, v=v: e.wait_ge(s, v))

    def wait_all(self, eng, q):
        for c in self.rings[q].ctrs:
            if c.n > 0:
                self.emit(eng, lambda e, s=c.sem, v=c.n: e.wait_ge(s, v))

    def dmaq(self, q, dst, src):
        c, prev = self.rings[q].next()
        if prev > 0:
            self.emit(q, lambda e, s=c.sem, v=prev: e.wait_ge(s, v))
        self.emit(q, lambda e, d=dst, s=src, sm=c.sem: e.dma_start(
            out=d, in_=s).then_inc(sm, 16))
        return (c, c.n)

    def do(self, eng, build, inc=None):
        if inc is not None:
            c = self.ctr[inc]
            c.n += 1
            self.emit(eng, lambda e, b=build, s=c.sem: b(e).then_inc(s, 1))
            return c.n
        self.emit(eng, lambda e, b=build: b(e))
        return None

    def mark(self, cname):
        return self.ctr[cname].n


def build_nc(stop_after=None):
    nc = bass.Bass("TRN2", target_bir_lowering=False, debug=False)
    ctx = ExitStack()
    AP = bass.AP

    x_h = nc.dram_tensor("x", [N, D], F32, kind="ExternalInput")
    wq_h = nc.dram_tensor("wq", [D, D], F32, kind="ExternalInput")
    wkv_h = nc.dram_tensor("wkv", [D, 2 * D], F32, kind="ExternalInput")
    wb2_h = nc.dram_tensor("wb2", [8, 128, 128], F32, kind="ExternalInput")
    wo_h = nc.dram_tensor("wo", [D, D], F32, kind="ExternalInput")
    w1_h = nc.dram_tensor("w1", [HID, D], F32, kind="ExternalInput")
    w2_h = nc.dram_tensor("w2", [D, HID], F32, kind="ExternalInput")
    gam_h = nc.dram_tensor("gam", [128, 8], F32, kind="ExternalInput")
    g1_h = nc.dram_tensor("g1", [128, 8], F32, kind="ExternalInput")
    b1ln_h = nc.dram_tensor("b1ln", [128, 8], F32, kind="ExternalInput")
    b1_h = nc.dram_tensor("b1", [128, 32], F32, kind="ExternalInput")
    g2_h = nc.dram_tensor("g2", [128, 32], F32, kind="ExternalInput")
    b2ln_h = nc.dram_tensor("b2ln", [128, 32], F32, kind="ExternalInput")
    b2_h = nc.dram_tensor("b2", [128, 8], F32, kind="ExternalInput")
    tabc_h = nc.dram_tensor("tabc", [128, NT * DH], F32, kind="ExternalInput")
    tabs_h = nc.dram_tensor("tabs", [128, NT * DH], F32, kind="ExternalInput")
    id16_h = nc.dram_tensor("id16", [128, 128], F32, kind="ExternalInput")
    id32_h = nc.dram_tensor("id32", [128, 128], F32, kind="ExternalInput")

    xa_scr = nc.dram_tensor("xa_scr", [N, D], F32, kind="Internal")
    wqs_scr = nc.dram_tensor("wqs_scr", [D, D], BF16, kind="Internal")
    w1t_scr = nc.dram_tensor("w1t_scr", [32, 128, 1024], FP8, kind="Internal")
    w2t_scr = nc.dram_tensor("w2t_scr", [8, 128, 4096], BF16, kind="Internal")

    y_h = nc.dram_tensor("y", [N, D], F32, kind="ExternalOutput")
    dbg_h = None
    if stop_after is not None:
        dbg_h = nc.dram_tensor("dbg", [128, 40960], F32, kind="ExternalOutput")

    ps01 = ctx.enter_context(nc.psum_tensor("ps01", [128, 1024], F32))
    ps23 = ctx.enter_context(nc.psum_tensor("ps23", [128, 1024], F32))
    psPV0 = ctx.enter_context(nc.psum_tensor("psPV0", [128, 512], F32))
    psPV1 = ctx.enter_context(nc.psum_tensor("psPV1", [128, 512], F32))
    psT = ctx.enter_context(nc.psum_tensor("psT", [128, 1024], BF16))
    psM = ctx.enter_context(nc.psum_tensor("psM", [128, 512], F32))

    wb2 = ctx.enter_context(nc.sbuf_tensor("wb2_s", [128, 8 * 128], BF16))
    ident = ctx.enter_context(nc.sbuf_tensor("ident", [128, 128], BF16))
    identf = ctx.enter_context(nc.sbuf_tensor("identf", [128, 128], F32))
    onescol = ctx.enter_context(nc.sbuf_tensor("onescol", [128, 1], BF16))
    onescolf = ctx.enter_context(nc.sbuf_tensor("onescolf", [128, 1], F32))
    gam = ctx.enter_context(nc.sbuf_tensor("gam_s", [128, 8], F32))
    g1c = ctx.enter_context(nc.sbuf_tensor("g1c", [128, 8], F32))
    b1lnc = ctx.enter_context(nc.sbuf_tensor("b1lnc", [128, 16], BF16))
    b1c = ctx.enter_context(nc.sbuf_tensor("b1c", [128, 32], F32))
    g2c = ctx.enter_context(nc.sbuf_tensor("g2c", [128, 32], F32))
    b2lnc = ctx.enter_context(nc.sbuf_tensor("b2lnc", [128, 64], BF16))
    b2c = ctx.enter_context(nc.sbuf_tensor("b2c", [128, 8], F32))
    mu0 = ctx.enter_context(nc.sbuf_tensor("mu0", [128, NT], F32))
    nmu0 = ctx.enter_context(nc.sbuf_tensor("nmu0", [128, NT], F32))
    R0 = ctx.enter_context(nc.sbuf_tensor("R0", [128, NT], F32))
    ssq0 = ctx.enter_context(nc.sbuf_tensor("ssq0", [128, NT], F32))
    lnscr = ctx.enter_context(nc.sbuf_tensor("lnscr", [128, NT], F32))
    lnscr2 = ctx.enter_context(nc.sbuf_tensor("lnscr2", [128, NT], F32))
    junkb = ctx.enter_context(nc.sbuf_tensor("junkb", [128, 1024], BF16))
    junks = ctx.enter_context(nc.sbuf_tensor("junks", [128, 1024], BF16))
    rawsum = ctx.enter_context(nc.sbuf_tensor("rawsum", [128, NT], F32))
    s1bc = ctx.enter_context(nc.sbuf_tensor("s1bc", [128, 1], F32))
    m1bc = ctx.enter_context(nc.sbuf_tensor("m1bc", [128, 1], F32))
    s2bc = ctx.enter_context(nc.sbuf_tensor("s2bc", [128, 1], F32))
    m2c = ctx.enter_context(nc.sbuf_tensor("m2c", [128, 1], F32))
    gmc = ctx.enter_context(nc.sbuf_tensor("gmc", [128, 32], F32))
    gmcb = ctx.enter_context(nc.sbuf_tensor("gmcb", [128, 32], BF16))
    b1tot = ctx.enter_context(nc.sbuf_tensor("b1tot", [128, 32], F32))
    wsumr = ctx.enter_context(nc.sbuf_tensor("wsumr", [128, 1024], BF16))
    biasr = ctx.enter_context(nc.sbuf_tensor("biasr", [128, 1024], BF16))
    biasC = ctx.enter_context(nc.sbuf_tensor("biasC", [128, 8], F32))
    wsC = ctx.enter_context(nc.sbuf_tensor("wsC", [128, 8], F32))
    wpart = ctx.enter_context(nc.sbuf_tensor("wpart", [128, 1], F32))
    wtmp = ctx.enter_context(nc.sbuf_tensor("wtmp", [128, 2], F32))

    P = Prog(nc, ctx)

    def dma(q, dst, src, sem=None):
        return P.dmaq(q, dst, src)

    def LP(build):
        def f(e):
            with nc.allow_low_precision(reason="intentional bf16 operand"):
                return build(e)
        return f

    def vgate():
        P.wait("vector", "sV", P.mark("sV"))

    def phase_barrier():
        for eng in ENGINES:
            for cn in ("sM", "sV", "sS", "sG"):
                P.wait(eng, cn, P.mark(cn))
            for q in ("sync", "gpsimd", "scalar"):
                P.wait_all(eng, q)

    def ggate():
        P.wait("gpsimd", "sG", P.mark("sG"))

    dbg_state = {"off": 0}

    def dump(buf_ap, rows=128):
        w = buf_ap.shape[-1]
        off = dbg_state["off"]
        for cn in ("sV", "sS", "sM", "sG"):
            P.wait("gpsimd", cn, P.mark(cn))
        for q in ("sync", "gpsimd", "scalar"):
            P.wait_all("gpsimd", q)
        dma("gpsimd", dbg_h.ap()[0:rows, off:off + w], buf_ap)
        dbg_state["off"] += w

    def finalize():
        for cname in ("sM", "sV", "sS", "sG"):
            P.wait("gpsimd", cname, P.mark(cname))
        for q in ("sync", "gpsimd", "scalar"):
            P.wait_all("gpsimd", q)
        with nc.Block() as block:
            @block.sync
            def _(e):
                for fn in P.plan["sync"]:
                    fn(e)

            @block.tensor
            def _(e):
                for fn in P.plan["tensor"]:
                    fn(e)

            @block.vector
            def _(e):
                for fn in P.plan["vector"]:
                    fn(e)

            @block.scalar
            def _(e):
                for fn in P.plan["scalar"]:
                    fn(e)

            @block.gpsimd
            def _(e):
                for fn in P.plan["gpsimd"]:
                    fn(e)
        return nc, ctx

    dma("sync", gam[:], gam_h.ap(), "sX")
    dma("sync", g1c[:], g1_h.ap(), "sX")
    dma("sync", b1c[:], b1_h.ap(), "sX")
    dma("sync", g2c[:], g2_h.ap(), "sX")
    dma("sync", b2c[:], b2_h.ap(), "sX")
    dma("gpsimd", wb2[:], AP(wb2_h, 0, [[128, 128], [16384, 8], [1, 128]]), "sC")
    dma("gpsimd", ident[:], id16_h.ap(), "sC")
    dma("sync", identf[:], id32_h.ap(), "sX")
    dma("gpsimd", b1lnc[:, 0:8], b1ln_h.ap(), "sC")
    dma("gpsimd", b2lnc[:, 0:32], b2ln_h.ap(), "sC")
    dma("sync", lnscr[:, 0:8], b1ln_h.ap(), "sX")
    dma("sync", lnscr2[:], b2ln_h.ap(), "sX")
    P.do("vector", lambda e: e.memset(onescol[:], 1.0))
    P.do("vector", lambda e: e.memset(onescolf[:], 1.0), inc="sV")
    P.wait_all("tensor", "sync")
    P.wait_all("vector", "sync")
    P.wait_all("scalar", "sync")
    P.wait_all("tensor", "gpsimd")
    P.wait_all("gpsimd", "gpsimd")
    P.wait_all("vector", "gpsimd")
    P.do("vector", lambda e: e.tensor_tensor(
        b1lnc[:, 8:16], lnscr[:, 0:8], b1lnc[:, 0:8], ALU.subtract), inc="sV")
    P.do("vector", lambda e: e.tensor_tensor(
        b2lnc[:, 32:64], lnscr2[:], b2lnc[:, 0:32], ALU.subtract), inc="sV")
    P.wait("tensor", "sV", P.mark("sV"))
    P.wait("tensor", "sV")

    def ln_stats_tile(t, xbuf_ap, mu, nmu, ssq, xw_v):
        if xw_v:
            P.wait("vector", "sX", xw_v)
        P.do("vector", lambda e: e.reduce_sum(rawsum[:, t:t + 1], xbuf_ap, axis=AX.X),
             inc="sV")
        vgate()
        P.do("vector", lambda e: e.tensor_scalar(
            mu[:, t:t + 1], rawsum[:, t:t + 1], 1.0 / D, None, op0=ALU.mult))
        vv = P.do("vector", lambda e: e.tensor_scalar(
            nmu[:, t:t + 1], rawsum[:, t:t + 1], -1.0 / D, None, op0=ALU.mult),
            inc="sV")
        P.wait("scalar", "sV", vv)
        P.wait("scalar", "sS", P.mark("sS"))
        P.do("scalar", lambda e: e.activation(
            out=junks[:], in_=xbuf_ap, func=AF.Square,
            bias=nmu[:, t:t + 1], scale=1.0, accum_out=ssq[:, t:t + 1]), inc="sS")

    def ln_R_batch(t0, nt, ssq, Rbuf, dim):
        sl = (slice(None), slice(t0, t0 + nt))
        P.wait("vector", "sS", P.mark("sS"))
        P.do("vector", lambda e: e.tensor_scalar(
            lnscr[sl], ssq[sl], 1.0 / dim, EPS, op0=ALU.mult, op1=ALU.add), inc="sV")
        P.wait("scalar", "sV", P.mark("sV"))
        ss = P.do("scalar", lambda e: e.activation(
            out=lnscr2[sl], in_=lnscr[sl], func=AF.Sqrt, bias=0.0, scale=1.0), inc="sS")
        P.wait("vector", "sS", ss)
        P.do("vector", lambda e: e.reciprocal(Rbuf[sl], lnscr2[sl]))
        P.do("vector", lambda e: e.tensor_scalar(
            lnscr[sl], lnscr[sl], -EPS, None, op0=ALU.add), inc="sV")
        vgate()
        P.do("vector", lambda e: e.tensor_tensor(
            lnscr[sl], lnscr[sl], Rbuf[sl], ALU.mult), inc="sV")
        vgate()
        P.do("vector", lambda e: e.tensor_tensor(
            lnscr[sl], lnscr[sl], Rbuf[sl], ALU.mult), inc="sV")
        vgate()
        vv = P.do("vector", lambda e: e.tensor_scalar(
            lnscr[sl], lnscr[sl], EPS, None, op0=ALU.add), inc="sV")
        P.wait("scalar", "sV", vv)
        ss = P.do("scalar", lambda e: e.activation(
            out=lnscr2[sl], in_=lnscr[sl], func=AF.Sqrt, bias=0.0, scale=1.0), inc="sS")
        P.wait("vector", "sS", ss)
        P.do("vector", lambda e: e.reciprocal(lnscr2[sl], lnscr2[sl]), inc="sV")
        vgate()
        P.do("vector", lambda e: e.tensor_tensor(Rbuf[sl], Rbuf[sl], lnscr2[sl], ALU.mult),
             inc="sV")
        vgate()

    attctx = ctx.enter_context(ExitStack())
    kT = attctx.enter_context(nc.sbuf_tensor("kT", [128, 8 * N], BF16))
    vaug = attctx.enter_context(nc.sbuf_tensor("vaug", [128, NT * 1040], BF16))
    tabc_s = attctx.enter_context(nc.sbuf_tensor("tabc_s", [128, NT * DH], BF16))
    tabs_s = attctx.enter_context(nc.sbuf_tensor("tabs_s", [128, NT * DH], BF16))
    dma("gpsimd", tabc_s[:], tabc_h.ap(), "sC")
    dma("gpsimd", tabs_s[:], tabs_h.ap(), "sC")
    vgate()
    P.do("vector", lambda e: e.tensor_copy(
        AP(vaug, 64, [[NT * 1040, 128], [1040, NT], [65, H]]),
        onescol[:, 0:1].to_broadcast([128, NT * H])), inc="sV")
    P.wait_all("vector", "gpsimd")

    def rope_emit(dst, ps_handle, t, nheads, prow):
        w = nheads * DH
        cap = AP(tabc_s, t * DH, [[NT * DH, 128], [0, nheads], [1, DH]])
        sap = AP(tabs_s, t * DH, [[NT * DH, 128], [0, nheads], [1, DH]])
        src = AP(ps_handle, 0, [[prow, 128], [1, w]])
        partner = AP(ps_handle, 1, [[prow, 128], [DH, nheads], [2, DH // 2], [-1, 2]])
        P.do("vector", lambda e: e.tensor_tensor(dst, src, cap, ALU.mult))
        P.do("vector", lambda e: e.tensor_tensor(junkb[:, 0:w], partner, sap, ALU.mult),
             inc="sV")
        vgate()
        return P.do("vector", lambda e: e.tensor_tensor(
            dst, dst, junkb[:, 0:w], ALU.add), inc="sV")

    kvctx = ctx.enter_context(ExitStack())
    wkvb = kvctx.enter_context(nc.sbuf_tensor("wkvb", [128, 8 * 2048], BF16))
    wslf = kvctx.enter_context(nc.sbuf_tensor("wslf", [128, 1024], F32))
    xr0 = kvctx.enter_context(nc.sbuf_tensor("xr0", [128, 4 * 1024], F32))
    xn0 = kvctx.enter_context(nc.sbuf_tensor("xn0", [128, 1024], BF16))
    xnT0 = kvctx.enter_context(nc.sbuf_tensor("xnT0", [128, 1024], BF16))
    ktile = kvctx.enter_context(nc.sbuf_tensor("ktile", [128, 1024], BF16))

    pv_wsl = 0
    wq_st = None
    for dc in range(8):
        P.wait("sync", "sV", pv_wsl)
        lw = dma("sync", wslf[:], wq_h.ap()[dc * 128:(dc + 1) * 128, :])
        P.waitt("vector", lw)
        P.waitt("vector", wq_st)
        vv = P.do("vector", lambda e, dc=dc: e.tensor_scalar(
            junkb[:], wslf[:], gam[:, dc:dc + 1], 0.125,
            op0=ALU.mult, op1=ALU.mult), inc="sV")
        pv_wsl = vv
        P.wait("scalar", "sV", vv)
        wq_st = dma("scalar", wqs_scr.ap()[dc * 128:(dc + 1) * 128, :], junkb[:])
    P.waitt("vector", wq_st)
    for dc in range(8):
        for ecg in range(2):
            P.wait("sync", "sV", pv_wsl)
            lw = dma("sync", wslf[:],
                     wkv_h.ap()[dc * 128:(dc + 1) * 128, ecg * 1024:(ecg + 1) * 1024],
                     "sX")
            P.wait("vector", "sX", lw)
            pv_wsl = P.do("vector", lambda e, dc=dc, ecg=ecg: e.tensor_scalar(
                wkvb[:, dc * 2048 + ecg * 1024: dc * 2048 + (ecg + 1) * 1024],
                wslf[:], gam[:, dc:dc + 1], None, op0=ALU.mult), inc="sV")
    P.wait("tensor", "sV", P.mark("sV"))

    kv_ev = 0
    tr_ev = 0
    sq_last = 0
    for g in range(8):
        xw = []
        P.wait("sync", "sV", tr_ev)
        P.wait("sync", "sS", sq_last)
        for i in range(4):
            t = g * 4 + i
            xw.append(dma("sync", xr0[:, i * 1024:(i + 1) * 1024],
                          x_h.ap()[t * 128:(t + 1) * 128, :], "sX"))
        for i in range(4):
            ln_stats_tile(g * 4 + i, xr0[:, i * 1024:(i + 1) * 1024], mu0, nmu0, ssq0,
                          xw[i])
        sq_last = P.mark("sS")
        ln_R_batch(g * 4, 4, ssq0, R0, D)
        for i in range(4):
            t = g * 4 + i
            vv = P.do("vector", lambda e, i=i, t=t: e.tensor_scalar(
                xn0[:], xr0[:, i * 1024:(i + 1) * 1024], mu0[:, t:t + 1], R0[:, t:t + 1],
                op0=ALU.subtract, op1=ALU.mult), inc="sV")
            P.wait("tensor", "sV", vv)
            P.wait("tensor", "sV", tr_ev)
            for d8 in range(8):
                mmv = P.do("tensor", lambda e, d8=d8: e.transpose(
                    psT[:, d8 * 128:(d8 + 1) * 128],
                    xn0[:, d8 * 128:(d8 + 1) * 128], ident[:]),
                    inc="sM" if d8 == 7 else None)
            P.wait("vector", "sM", mmv)
            tr_ev = P.do("vector", lambda e: e.tensor_copy(xnT0[:], psT[:]), inc="sV")
            P.wait("tensor", "sV", tr_ev)
            P.wait("tensor", "sV", kv_ev)
            for dc in range(8):
                for ec in range(4):
                    pb = ps01 if ec < 2 else ps23
                    mmv = P.do("tensor", lambda e, dc=dc, ec=ec, pb=pb: e.matmul(
                        pb[:, (ec % 2) * 512:((ec % 2) + 1) * 512],
                        xnT0[:, dc * 128:(dc + 1) * 128],
                        wkvb[:, dc * 2048 + ec * 512: dc * 2048 + (ec + 1) * 512],
                        start=(dc == 0), stop=(dc == 7)),
                        inc="sM" if (dc == 7 and ec == 3) else None)
            P.wait("vector", "sM", mmv)
            rope_emit(ktile[:], ps01, t, 16, 1024)
            kv_ev = P.do("vector", lambda e, t=t: e.tensor_copy(
                AP(vaug, t * 1040, [[NT * 1040, 128], [65, 16], [1, 64]]), ps23[:]), inc="sV")
            P.wait("tensor", "sV", kv_ev)
            for d8 in range(8):
                mmv = P.do("tensor", lambda e, d8=d8: e.transpose(
                    psT[:, d8 * 128:(d8 + 1) * 128],
                    ktile[:, d8 * 128:(d8 + 1) * 128], ident[:]),
                    inc="sM" if d8 == 7 else None)
            P.wait("vector", "sM", mmv)
            for d8 in range(8):
                tr_ev = P.do("vector", lambda e, d8=d8, t=t: e.tensor_copy(
                    kT[:, d8 * N + t * 128: d8 * N + (t + 1) * 128],
                    psT[:, d8 * 128:(d8 + 1) * 128]),
                    inc="sV" if d8 == 7 else None)
    P.wait("tensor", "sV", tr_ev)
    phase_barrier()
    kvctx.close()

    if stop_after == "P12":
        dump(kT[:, 0:512])
        dump(kT[:, 7 * N + 1024:7 * N + 1536])
        dump(vaug[:, 0:1040])
        dump(mu0[:])
        dump(R0[:])
        return finalize()

    ackctx = ctx.enter_context(ExitStack())
    xq = ackctx.enter_context(nc.sbuf_tensor("xq", [128, 2 * 1024], F32))
    xn_t = ackctx.enter_context(nc.sbuf_tensor("xn_t", [128, 1024], BF16))
    xnT_t = ackctx.enter_context(nc.sbuf_tensor("xnT_t", [128, 1024], BF16))
    qtile = ackctx.enter_context(nc.sbuf_tensor("qtile", [128, 512], BF16))
    qTc = ackctx.enter_context(nc.sbuf_tensor("qTc", [128, 8 * 512], BF16))
    PT = ackctx.enter_context(nc.sbuf_tensor("PT", [128, 4 * 1024], BF16))
    km2b = ackctx.enter_context(nc.sbuf_tensor("km2b", [128, 128], F32))
    kmax2 = ackctx.enter_context(nc.sbuf_tensor("kmax2", [128, 16], F32))
    qm2 = ackctx.enter_context(nc.sbuf_tensor("qm2", [128, 16], F32))
    cm2 = ackctx.enter_context(nc.sbuf_tensor("cm2", [128, 16], F32))
    sq16 = ackctx.enter_context(nc.sbuf_tensor("sq16", [128, 16], F32))
    negbf = ackctx.enter_context(nc.sbuf_tensor("negbf", [128, 16], BF16))
    negc = ackctx.enter_context(nc.sbuf_tensor("negc", [128, 16], F32))
    lnr = ackctx.enter_context(nc.sbuf_tensor("lnr", [128, 1024], F32))
    recb = ackctx.enter_context(nc.sbuf_tensor("recb", [128, 1024], BF16))
    oP = ackctx.enter_context(nc.sbuf_tensor("oP", [128, 8 * 512], BF16))
    otmp = ackctx.enter_context(nc.sbuf_tensor("otmp", [128, 512], BF16))
    wsh = ackctx.enter_context(nc.sbuf_tensor("wsh", [128, 8 * 512], BF16))
    xpr = ackctx.enter_context(nc.sbuf_tensor("xpr", [128, 2 * 512], F32))

    kn_mm = [0, 0]
    kn_rd = 0
    for p8 in range(8):
        for jj in range(8):
            sel = (p8 * 8 + jj) % 2
            colb = p8 * N + jj * 512
            P.wait("scalar", "sM", kn_mm[sel])
            ssq = P.do("scalar", lambda e, colb=colb, sel=sel: e.activation(
                out=junks[:, sel * 512:(sel + 1) * 512],
                in_=kT[:, colb:colb + 512], func=AF.Square, bias=0.0, scale=1.0),
                inc="sS")
            P.wait("tensor", "sS", ssq)
            P.wait("tensor", "sV", kn_rd)
            P.do("tensor", lambda e, sel=sel: e.matmul(
                psPV0[0:1, 0:512], onescol[0:64, 0:1],
                junks[0:64, sel * 512:(sel + 1) * 512], start=True, stop=True))
            mmv = P.do("tensor", lambda e, sel=sel: e.matmul(
                psPV1[0:1, 0:512], onescol[64:128, 0:1],
                junks[64:128, sel * 512:(sel + 1) * 512], start=True, stop=True,
                tile_position=(64, 0)), inc="sM")
            kn_mm[sel] = mmv
            P.wait("vector", "sM", mmv)
            P.do("vector", lambda e, p8=p8, jj=jj: e.tensor_reduce(
                km2b[0:1, p8 * 16 + jj:p8 * 16 + jj + 1], psPV0[0:1, 0:512],
                axis=AX.X, op=ALU.max))
            kn_rd = P.do("vector", lambda e, p8=p8, jj=jj: e.tensor_reduce(
                km2b[0:1, p8 * 16 + 8 + jj:p8 * 16 + 8 + jj + 1], psPV1[0:1, 0:512],
                axis=AX.X, op=ALU.max), inc="sV")
    vgate()
    for h in range(16):
        p8, hh = h // 2, h % 2
        P.do("vector", lambda e, h=h, p8=p8, hh=hh: e.tensor_reduce(
            kmax2[0:1, h:h + 1], km2b[0:1, p8 * 16 + hh * 8:p8 * 16 + hh * 8 + 8],
            axis=AX.X, op=ALU.max), inc="sV" if h == 15 else None)
    vgate()

    nchunks = 1 if stop_after == "C1" else NQC
    xq_free = [0, 0]
    wq_free = 0
    st_w = [None, None]
    odma = [None]
    odma_list = []
    for c in range(nchunks):
        P.wait("tensor", "sS", P.mark("sS"))
        for eh in range(2):
            P.wait("sync", "sM", wq_free)
            lws = []
            for dc in range(8):
                lws.append(dma("sync", wsh[:, dc * 512:(dc + 1) * 512],
                               wqs_scr.ap()[dc * 128:(dc + 1) * 128,
                                            eh * 512:(eh + 1) * 512]))
            P.waitt("tensor", lws)
            for ti in range(4):
                t = c * 4 + ti
                sl = ti % 2
                P.wait("sync", "sV", xq_free[sl])
                xw = dma("sync", xq[:, sl * 1024:(sl + 1) * 1024],
                         x_h.ap()[t * 128:(t + 1) * 128, :], "sX")
                P.wait("vector", "sX", xw)
                xq_free[sl] = P.do("vector", lambda e, sl=sl, t=t: e.tensor_scalar(
                    xn_t[:], xq[:, sl * 1024:(sl + 1) * 1024],
                    mu0[:, t:t + 1], R0[:, t:t + 1],
                    op0=ALU.subtract, op1=ALU.mult), inc="sV")
                P.wait("tensor", "sV", xq_free[sl])
                for d8 in range(8):
                    mmv = P.do("tensor", lambda e, d8=d8: e.transpose(
                        psT[:, d8 * 128:(d8 + 1) * 128],
                        xn_t[:, d8 * 128:(d8 + 1) * 128], ident[:]),
                        inc="sM" if d8 == 7 else None)
                P.wait("vector", "sM", mmv)
                vv = P.do("vector", lambda e: e.tensor_copy(xnT_t[:], psT[:]), inc="sV")
                P.wait("tensor", "sV", vv)
                for dc in range(8):
                    mmv = P.do("tensor", lambda e, dc=dc: e.matmul(
                        psM[:, 0:512],
                        xnT_t[:, dc * 128:(dc + 1) * 128],
                        wsh[:, dc * 512:(dc + 1) * 512],
                        start=(dc == 0), stop=(dc == 7)),
                        inc="sM" if dc == 7 else None)
                wq_free = mmv
                P.wait("vector", "sM", mmv)
                rope_emit(qtile[:], psM, t, 8, 512)
                P.wait("tensor", "sV", P.mark("sV"))
                for s4 in range(4):
                    mmv = P.do("tensor", lambda e, s4=s4: e.transpose(
                        psT[:, s4 * 128:(s4 + 1) * 128],
                        qtile[:, s4 * 128:(s4 + 1) * 128], ident[:]),
                        inc="sM" if s4 == 3 else None)
                P.wait("vector", "sM", mmv)
                for s4 in range(4):
                    p = eh * 4 + s4
                    qT_ev = P.do("vector", lambda e, p=p, ti=ti, s4=s4: e.tensor_copy(
                        qTc[:, p * 512 + ti * 128: p * 512 + (ti + 1) * 128],
                        psT[:, s4 * 128:(s4 + 1) * 128]),
                        inc="sV" if s4 == 3 else None)
                P.wait("tensor", "sV", qT_ev)
        qn_mm = [0, 0]
        qn_rd = 0
        for p in range(8):
            mmv = P.do("tensor", lambda e, p=p: e.matmul(
                psM[:, 0:512], wb2[:, p * 128:(p + 1) * 128],
                qTc[:, p * 512:(p + 1) * 512], start=True, stop=True), inc="sM")
            P.wait("vector", "sM", mmv)
            vv = P.do("vector", lambda e, p=p: e.tensor_copy(
                qTc[:, p * 512:(p + 1) * 512], psM[:, 0:512]), inc="sV")
            P.wait("tensor", "sV", vv)
            sel = p % 2
            P.wait("scalar", "sM", qn_mm[sel])
            P.wait("scalar", "sV", vv)
            ssq = P.do("scalar", lambda e, p=p, sel=sel: e.activation(
                out=junks[:, sel * 512:(sel + 1) * 512],
                in_=qTc[:, p * 512:(p + 1) * 512], func=AF.Square, bias=0.0,
                scale=1.0), inc="sS")
            P.wait("tensor", "sS", ssq)
            P.wait("tensor", "sV", qn_rd)
            P.do("tensor", lambda e, sel=sel: e.matmul(
                psPV0[0:1, 0:512], onescol[0:64, 0:1],
                junks[0:64, sel * 512:(sel + 1) * 512], start=True, stop=True))
            mmq = P.do("tensor", lambda e, sel=sel: e.matmul(
                psPV1[0:1, 0:512], onescol[64:128, 0:1],
                junks[64:128, sel * 512:(sel + 1) * 512], start=True, stop=True,
                tile_position=(64, 0)), inc="sM")
            qn_mm[sel] = mmq
            P.wait("vector", "sM", mmq)
            P.do("vector", lambda e, p=p: e.tensor_reduce(
                qm2[0:1, 2 * p:2 * p + 1], psPV0[0:1, 0:512], axis=AX.X, op=ALU.max))
            qn_rd = P.do("vector", lambda e, p=p: e.tensor_reduce(
                qm2[0:1, 2 * p + 1:2 * p + 2], psPV1[0:1, 0:512], axis=AX.X,
                op=ALU.max), inc="sV")
        vgate()
        P.do("vector", lambda e: e.tensor_tensor(
            cm2[0:1, 0:16], qm2[0:1, 0:16], kmax2[0:1, 0:16], ALU.mult), inc="sV")
        vgate()
        vv = P.do("vector", lambda e: e.tensor_reduce(
            sq16[0:1, 0:8], AP(cm2, 0, [[16, 1], [2, 8], [1, 2]]), axis=AX.X,
            op=ALU.max), inc="sV")
        P.wait("scalar", "sV", vv)
        ssv = P.do("scalar", lambda e: e.activation(
            out=cm2[0:1, 8:16], in_=sq16[0:1, 0:8], func=AF.Sqrt, bias=0.0,
            scale=1.0), inc="sS")
        P.wait("vector", "sS", ssv)
        vv = P.do("vector", LP(lambda e: e.tensor_scalar(
            negbf[0:1, 0:8], cm2[0:1, 8:16], -1.0, None, op0=ALU.mult)), inc="sV")
        P.wait("tensor", "sV", vv)
        mmv = P.do("tensor", lambda e: e.matmul(
            psM[:, 0:8], onescol[0:1, 0:1].to_broadcast([1, 128]),
            negbf[0:1, 0:8], start=True, stop=True), inc="sM")
        P.wait("vector", "sM", mmv)
        negc_v = P.do("vector", lambda e: e.tensor_copy(negc[:, 0:8], psM[:, 0:8]),
                      inc="sV")

        P.wait("gpsimd", "sM", wq_free)
        wo_lw0 = []
        for cc in range(8):
            wo_lw0.append(dma("gpsimd", wsh[:, cc * 512:(cc + 1) * 512],
                              wo_h.ap()[cc * 128:(cc + 1) * 128, 0:512]))

        stat_slots = [(psM, 0)]
        slot_cons = [("sV", negc_v)]
        st_cons = [0, 0]
        ring = [0]

        def next_slot():
            s = ring[0] % len(stat_slots)
            ring[0] += 1
            return s

        def waitc(eng, tok):
            if tok is not None:
                P.wait(eng, tok[0], tok[1])

        def emit_stpv(p, negc_v):
            st_slots = [ps01, ps23]
            pt_cons = [0] * 4
            ovv = P.mark("sV")
            P.wait("scalar", "sV", negc_v)
            first_pv = [True]
            pv_q = []
            mm2l = [0]

            def emit_pv(j, pslot, expv):
                P.wait("tensor", "sS", expv)
                if first_pv[0]:
                    P.wait("tensor", "sV", ovv)
                    first_pv[0] = False
                h0 = 2 * p
                h1 = 2 * p + 1
                P.do("tensor", lambda e, j=j, pslot=pslot, h0=h0: e.matmul(
                    psPV0[0:65, :],
                    vaug[:, j * 1040 + h0 * 65: j * 1040 + h0 * 65 + 65],
                    PT[:, pslot * 1024:pslot * 1024 + 512],
                    start=(j == 0), stop=(j == 31)))
                mm2l[0] = P.do("tensor", lambda e, j=j, pslot=pslot, h1=h1:
                               e.matmul(
                                   psPV1[0:65, :],
                                   vaug[:, j * 1040 + h1 * 65:
                                        j * 1040 + h1 * 65 + 65],
                                   PT[:, pslot * 1024 + 512:(pslot + 1) * 1024],
                                   start=(j == 0), stop=(j == 31)), inc="sM")
                pt_cons[pslot] = mm2l[0]

            for j in range(32):
                ss = j % 2
                spt = st_slots[ss]
                P.wait("tensor", "sS", st_cons[ss])
                P.do("tensor", lambda e, p=p, j=j, spt=spt: e.matmul(
                    spt[:, 0:512],
                    kT[0:64, p * N + j * 128: p * N + (j + 1) * 128],
                    qTc[0:64, p * 512:(p + 1) * 512],
                    start=True, stop=True))
                mmv = P.do("tensor", lambda e, p=p, j=j, spt=spt: e.matmul(
                    spt[:, 512:1024],
                    kT[64:128, p * N + j * 128: p * N + (j + 1) * 128],
                    qTc[64:128, p * 512:(p + 1) * 512],
                    start=True, stop=True, tile_position=(64, 0)), inc="sM")
                pslot = j % 4
                P.wait("scalar", "sM", mmv)
                P.wait("scalar", "sM", pt_cons[pslot])
                st_cons[ss] = P.do("scalar", lambda e, spt=spt, pslot=pslot, p=p:
                                   e.activation(
                                       out=PT[:, pslot * 1024:(pslot + 1) * 1024],
                                       in_=spt[:, 0:1024], func=AF.Exp,
                                       bias=negc[:, p:p + 1], scale=1.0),
                                   inc="sS")
                pv_q.append((j, pslot, st_cons[ss]))
                if len(pv_q) > 2:
                    emit_pv(*pv_q.pop(0))
            while pv_q:
                emit_pv(*pv_q.pop(0))
            mmv = mm2l[0]
            P.wait("scalar", "sM", mmv)
            P.wait("scalar", "sS", P.mark("sS"))
            P.do("scalar", lambda e: e.activation(
                out=lnr[64:65, 0:512], in_=psPV0[64:65, :], func=AF.Ln,
                bias=0.0, scale=1.0))
            sl1 = P.do("scalar", lambda e: e.activation(
                out=lnr[64:65, 512:1024], in_=psPV1[64:65, :], func=AF.Ln,
                bias=0.0, scale=1.0), inc="sS")
            P.wait("tensor", "sS", sl1)
            waitc("tensor", slot_cons[0])
            P.do("tensor", lambda e: e.matmul(
                psM[0:64, 0:512], onescolf[64:65, 0:1].to_broadcast([1, 64]),
                lnr[64:65, 0:512], start=True, stop=True, tile_position=(64, 0)))
            P.wait("tensor", "sS", st_cons[0])
            mmb = P.do("tensor", lambda e: e.matmul(
                ps01[0:64, 0:512], onescolf[64:65, 0:1].to_broadcast([1, 64]),
                lnr[64:65, 512:1024], start=True, stop=True,
                tile_position=(64, 0)), inc="sM")
            P.wait("scalar", "sM", mmb)
            scv0 = P.do("scalar", lambda e: e.activation(
                out=recb[0:64, 0:512], in_=psM[0:64, :], func=AF.Exp, bias=0.0,
                scale=-1.0), inc="sS")
            scv = P.do("scalar", lambda e: e.activation(
                out=recb[0:64, 512:1024], in_=ps01[0:64, 0:512], func=AF.Exp,
                bias=0.0, scale=-1.0), inc="sS")
            slot_cons[0] = ("sS", scv0)
            st_cons[0] = scv
            P.wait("vector", "sS", scv)
            P.waitt("vector", odma[0])
            P.do("vector", lambda e, p=p: e.tensor_tensor(
                oP[0:64, p * 512:(p + 1) * 512], psPV0[0:64, :], recb[0:64, 0:512],
                ALU.mult))
            ovv = P.do("vector", lambda e, p=p: e.tensor_tensor(
                otmp[0:64, :], psPV1[0:64, :], recb[0:64, 512:1024], ALU.mult),
                inc="sV")
            P.wait("gpsimd", "sV", ovv)
            odma[0] = dma("gpsimd", oP[64:128, p * 512:(p + 1) * 512], otmp[0:64, :])
            odma_list.append(odma[0])
            P.wait("tensor", "sV", ovv)
            return P.mark("sS")

        odma_list.clear()
        for p in range(8):
            emit_stpv(p, negc_v)
        mmv = P.mark("sM")

        P.waitt("tensor", list(odma_list))
        for eh in range(2):
            if eh == 1:
                P.wait("gpsimd", "sM", mmv)
                wo_lw1 = []
                for cc in range(8):
                    wo_lw1.append(dma("gpsimd", wsh[:, cc * 512:(cc + 1) * 512],
                                      wo_h.ap()[cc * 128:(cc + 1) * 128, 512:1024]))
                P.waitt("tensor", wo_lw1)
            else:
                P.waitt("tensor", wo_lw0)
            for cc in range(8):
                for ti in range(4):
                    pb = ps01 if ti < 2 else ps23
                    mmv = P.do("tensor", lambda e, cc=cc, ti=ti, pb=pb: e.matmul(
                        pb[:, (ti % 2) * 512:((ti % 2) + 1) * 512],
                        oP[:, cc * 512 + ti * 128: cc * 512 + (ti + 1) * 128],
                        wsh[:, cc * 512:(cc + 1) * 512],
                        start=(cc == 0), stop=(cc == 7)),
                        inc="sM" if (cc == 7 and ti == 3) else None)
            P.wait("vector", "sM", mmv)
            for ti in range(4):
                pb = ps01 if ti < 2 else ps23
                slw = ti % 2
                P.waitt("vector", st_w[slw])
                vv = P.do("vector", lambda e, ti=ti, pb=pb, slw=slw: e.tensor_copy(
                    xpr[:, slw * 512:(slw + 1) * 512],
                    pb[:, (ti % 2) * 512:((ti % 2) + 1) * 512]), inc="sV")
                P.wait("scalar", "sV", vv)
                t = c * 4 + ti
                st_w[slw] = dma(
                    "scalar", xa_scr.ap()[t * 128:(t + 1) * 128,
                                          eh * 512:(eh + 1) * 512],
                    xpr[:, slw * 512:(slw + 1) * 512])
            P.wait("tensor", "sV", P.mark("sV"))
        wq_free = P.mark("sM")

    if stop_after == "C1":
        dump(qTc[:, 0:512])
        dump(negc[:])
        dump(oP[:, 0:512])
        dump(oP[:, 3 * 512:4 * 512])
        dump(PT[:, 0:1024])
        dump(recb[:])
        return finalize()

    phase_barrier()
    ackctx.close()

    wpctx = ctx.enter_context(ExitStack())
    wfsl = wpctx.enter_context(nc.sbuf_tensor("wfsl", [128, 4096], F32))
    tsl = wpctx.enter_context(nc.sbuf_tensor("tsl", [128, 4096], BF16))
    tT = wpctx.enter_context(nc.sbuf_tensor("tT", [128, 4096], BF16))
    t8 = wpctx.enter_context(nc.sbuf_tensor("t8", [128, 1024], FP8))
    gtb = wpctx.enter_context(nc.sbuf_tensor("gtb", [128, 4096], BF16))

    def wmean(w_hh, nslab, slabw, outbc, outm):
        P.wait("sync", "sG", P.mark("sG"))
        P.do("vector", lambda e: e.memset(wpart[:], 0.0))
        pv = 0
        for s in range(nslab):
            P.wait("sync", "sV", pv)
            lw = dma("sync", wfsl[:, 0:slabw], w_hh.ap()[s * 128:(s + 1) * 128, :], "sX")
            P.wait("vector", "sX", lw)
            P.do("vector", lambda e, slabw=slabw: e.reduce_sum(
                wtmp[:, 0:1], wfsl[:, 0:slabw], axis=AX.X, apply_absolute_value=True),
                inc="sV")
            vgate()
            pv = P.do("vector", lambda e: e.tensor_tensor(
                wpart[:], wpart[:], wtmp[:, 0:1], ALU.add), inc="sV")
        vgate()
        vv = P.mark("sV")
        P.wait("tensor", "sV", vv)
        mmv = P.do("tensor", lambda e: e.matmul(
            psM[0:1, 0:1], onescolf[:, 0:1], wpart[:], start=True, stop=True),
            inc="sM")
        P.wait("vector", "sM", mmv)
        P.do("vector", lambda e: e.tensor_scalar(
            wtmp[0:1, 0:1], psM[0:1, 0:1], 1.0 / (HID * D), 1e-5,
            op0=ALU.mult, op1=ALU.max), inc="sV")
        vgate()
        P.do("vector", lambda e: e.reciprocal(wtmp[0:1, 1:2], wtmp[0:1, 0:1]), inc="sV")
        vgate()
        vv = P.mark("sV")
        P.wait("tensor", "sV", vv)
        mmv = P.do("tensor", lambda e: e.matmul(
            psM[:, 0:2], onescolf[0:1, 0:1].to_broadcast([1, 128]), wtmp[0:1, 0:2],
            start=True, stop=True), inc="sM")
        P.wait("vector", "sM", mmv)
        P.do("vector", lambda e: e.tensor_copy(outm[:], psM[:, 0:1]), inc="sV")
        vv = P.do("vector", lambda e: e.tensor_copy(outbc[:], psM[:, 1:2]), inc="sV")
        P.wait("gpsimd", "sV", vv)
        P.wait("tensor", "sV", vv)
        P.wait("sync", "sV", vv)
        vgate()

    wmean(w1_h, 32, 1024, s1bc, m1bc)
    ev_v = 0
    st_w1 = None
    tp_m = 0
    for s in range(32):
        P.wait("sync", "sG", P.mark("sG"))
        lw = dma("sync", wfsl[:, 0:1024], w1_h.ap()[s * 128:(s + 1) * 128, :], "sX")
        P.wait("vector", "sX", lw)
        P.wait("vector", "sM", tp_m)
        P.do("vector", lambda e: e.tensor_scalar(
            gtb[:, 0:1024], wfsl[:, 0:1024], s1bc[:], 0.5, op0=ALU.mult,
            op1=ALU.is_gt), inc="sG")
        P.do("vector", lambda e: e.tensor_scalar(
            tsl[:, 0:1024], wfsl[:, 0:1024], s1bc[:], -0.5, op0=ALU.mult,
            op1=ALU.is_lt), inc="sG")
        P.wait("vector", "sG", P.mark("sG"))
        gv = P.do("vector", lambda e: e.tensor_tensor(
            tsl[:, 0:1024], gtb[:, 0:1024], tsl[:, 0:1024], ALU.subtract), inc="sG")
        P.wait("tensor", "sG", gv)
        P.wait("tensor", "sV", ev_v)
        for dc in range(8):
            tp_m = P.do("tensor", lambda e, dc=dc: e.transpose(
                psT[:, dc * 128:(dc + 1) * 128],
                tsl[:, dc * 128:(dc + 1) * 128], ident[:]),
                inc="sM" if dc == 7 else None)
        P.wait("vector", "sM", tp_m)
        P.waitt("vector", st_w1)
        ev_v = P.do("vector", lambda e: e.tensor_copy(t8[:, 0:1024], psT[:]), inc="sV")
        P.wait("scalar", "sV", ev_v)
        st_w1 = dma("scalar", w1t_scr.ap()[s, :, :], t8[:, 0:1024])

    wmean(w2_h, 8, 4096, s2bc, m2c)
    P.do("vector", lambda e: e.tensor_scalar(
        gmc[:], g2c[:], m2c[:], None, op0=ALU.mult), inc="sV")
    vgate()
    vv = P.do("vector", lambda e: e.tensor_copy(gmcb[:], gmc[:]), inc="sV")
    P.wait("tensor", "sV", vv)
    ev_v = 0
    st_w2 = None
    tp_m = 0
    sm_m = 0
    for s in range(8):
        P.wait("sync", "sG", P.mark("sG"))
        lw = dma("sync", wfsl[:], w2_h.ap()[s * 128:(s + 1) * 128, :], "sX")
        P.wait("vector", "sX", lw)
        P.wait("vector", "sM", tp_m)
        P.do("vector", lambda e: e.tensor_scalar(
            gtb[:], wfsl[:], s2bc[:], 0.5, op0=ALU.mult, op1=ALU.is_gt), inc="sG")
        P.do("vector", lambda e: e.tensor_scalar(
            tsl[:], wfsl[:], s2bc[:], -0.5, op0=ALU.mult, op1=ALU.is_lt), inc="sG")
        P.wait("vector", "sG", P.mark("sG"))
        gv = P.do("vector", lambda e: e.tensor_tensor(
            tsl[:], gtb[:], tsl[:], ALU.subtract), inc="sG")
        P.wait("tensor", "sG", gv)
        P.waitt("tensor", st_w2)
        for hg in range(4):
            P.wait("tensor", "sV", ev_v)
            for k8 in range(8):
                hc = hg * 8 + k8
                tp_m = P.do("tensor", lambda e, hc=hc, k8=k8: e.transpose(
                    psT[:, k8 * 128:(k8 + 1) * 128],
                    tsl[:, hc * 128:(hc + 1) * 128], ident[:]),
                    inc="sM" if k8 == 7 else None)
            P.wait("vector", "sM", tp_m)
            ev_v = P.do("vector", lambda e, hg=hg: e.tensor_copy(
                tT[:, hg * 1024:(hg + 1) * 1024], psT[:]), inc="sV")
        P.wait("tensor", "sV", ev_v)
        for hc in range(32):
            P.do("tensor", lambda e, hc=hc: e.matmul(
                psPV0[:, 0:1], tT[:, hc * 128:(hc + 1) * 128], gmcb[:, hc:hc + 1],
                start=(hc == 0), stop=(hc == 31)))
            P.do("tensor", lambda e, hc=hc: e.matmul(
                psPV1[:, 0:1], tT[:, hc * 128:(hc + 1) * 128], b2lnc[:, hc:hc + 1],
                start=(hc == 0), stop=False))
            sm_m = P.do("tensor", lambda e, hc=hc: e.matmul(
                psPV1[:, 0:1], tT[:, hc * 128:(hc + 1) * 128],
                b2lnc[:, 32 + hc: 33 + hc],
                start=False, stop=(hc == 31)), inc="sM" if hc == 31 else None)
        P.wait("vector", "sM", sm_m)
        P.do("vector", lambda e, s=s: e.tensor_copy(wsC[:, s:s + 1], psPV0[:, 0:1]),
             inc="sV")
        P.do("vector", lambda e, s=s: e.tensor_copy(biasC[:, s:s + 1], psPV1[:, 0:1]),
             inc="sV")
        for hc in range(32):
            ev_v = P.do("vector", lambda e, hc=hc: e.tensor_scalar(
                tT[:, hc * 128:(hc + 1) * 128], tT[:, hc * 128:(hc + 1) * 128],
                gmc[:, hc:hc + 1], None, op0=ALU.mult),
                inc="sV" if hc == 31 else None)
        P.wait("scalar", "sV", ev_v)
        st_w2 = dma("scalar", w2t_scr.ap()[s, :, :], tT[:])
    vgate()
    P.do("vector", lambda e: e.tensor_scalar(
        biasC[:], biasC[:], m2c[:], None, op0=ALU.mult), inc="sV")
    vgate()
    P.do("vector", lambda e: e.tensor_tensor(biasC[:], biasC[:], b2c[:], ALU.add),
         inc="sV")
    vgate()
    P.do("vector", lambda e: e.tensor_copy(junkb[:, 0:8], wsC[:]))
    vv = P.do("vector", lambda e: e.tensor_copy(junkb[:, 8:16], biasC[:]), inc="sV")
    P.wait("tensor", "sV", vv)
    for s in range(8):
        P.do("tensor", lambda e, s=s: e.transpose(
            psT[0:1, 0:128], junkb[:, s:s + 1], ident[:]))
        mmv = P.do("tensor", lambda e, s=s: e.transpose(
            psT[0:1, 128:256], junkb[:, 8 + s:9 + s], ident[:]), inc="sM")
        P.wait("vector", "sM", mmv)
        P.do("vector", lambda e, s=s: e.tensor_copy(
            wsumr[0:1, s * 128:(s + 1) * 128], psT[0:1, 0:128]))
        vv = P.do("vector", lambda e, s=s: e.tensor_copy(
            biasr[0:1, s * 128:(s + 1) * 128], psT[0:1, 128:256]), inc="sV")
        P.wait("tensor", "sV", vv)
    phase_barrier()
    wpctx.close()
    attctx.close()

    if stop_after == "WPREP":
        dump(wsumr[0:1, :], rows=1)
        dump(biasr[0:1, :], rows=1)
        dump(m1bc[:, 0:1])
        dump(m2c[:, 0:1])
        return finalize()

    fctx = ctx.enter_context(ExitStack())
    xpc = fctx.enter_context(nc.sbuf_tensor("xpc", [128, 8 * 1024], F32))
    xn2 = fctx.enter_context(nc.sbuf_tensor("xn2", [128, 1024], BF16))
    xn2gT = fctx.enter_context(nc.sbuf_tensor("xn2gT", [128, 8 * 1024], BF16))
    uP = fctx.enter_context(nc.sbuf_tensor("uP", [128, 32 * 1024], BF16))
    u2r = fctx.enter_context(nc.sbuf_tensor("u2r", [128, 2 * 1024], BF16))
    w1r = fctx.enter_context(nc.sbuf_tensor("w1r", [128, 2 * 1024], FP8))
    w2r = fctx.enter_context(nc.sbuf_tensor("w2r", [128, 2 * 4096], BF16))
    rows_ = fctx.enter_context(nc.sbuf_tensor("rows_", [128, 3 * 1024], F32))
    rowsb = fctx.enter_context(nc.sbuf_tensor("rowsb", [128, 2 * 1024], BF16))
    fRc = fctx.enter_context(nc.sbuf_tensor("fRc", [128, 8], F32))
    w2o = fctx.enter_context(nc.sbuf_tensor("w2o", [128, 1024], BF16))
    ybuf = fctx.enter_context(nc.sbuf_tensor("ybuf", [128, 8 * 1024], F32))
    fmu = fctx.enter_context(nc.sbuf_tensor("fmu", [128, 8], F32))
    fnmu = fctx.enter_context(nc.sbuf_tensor("fnmu", [128, 8], F32))
    fssq = fctx.enter_context(nc.sbuf_tensor("fssq", [128, 8], F32))
    fR = fctx.enter_context(nc.sbuf_tensor("fR", [128, 8], F32))
    RA, RB, RC = 0, 1024, 2048

    yst_w = None
    ych_v = 0
    for f in range(NFC):
        P.waitt("sync", yst_w)
        P.wait("sync", "sV", ych_v)
        xw = []
        for ti in range(8):
            t = f * 8 + ti
            t1 = dma("sync", xpc[:, ti * 1024:(ti + 1) * 1024],
                     xa_scr.ap()[t * 128:(t + 1) * 128, :])
            t2 = dma("sync", ybuf[:, ti * 1024:(ti + 1) * 1024],
                     x_h.ap()[t * 128:(t + 1) * 128, :])
            xw.append([t1, t2])
        for ti in range(8):
            P.waitt("vector", xw[ti])
            P.do("vector", lambda e, ti=ti: e.tensor_tensor(
                xpc[:, ti * 1024:(ti + 1) * 1024], xpc[:, ti * 1024:(ti + 1) * 1024],
                ybuf[:, ti * 1024:(ti + 1) * 1024], ALU.add), inc="sV")
        vgate()
        for ti in range(8):
            ln_stats_tile(ti, xpc[:, ti * 1024:(ti + 1) * 1024], fmu, fnmu, fssq, 0)
        ln_R_batch(0, 8, fssq, fR, D)
        for ti in range(8):
            vv = P.do("vector", lambda e, ti=ti: e.tensor_scalar(
                xn2[:], xpc[:, ti * 1024:(ti + 1) * 1024],
                fmu[:, ti:ti + 1], fR[:, ti:ti + 1],
                op0=ALU.subtract, op1=ALU.mult), inc="sV")
            P.wait("tensor", "sV", vv)
            for d8 in range(8):
                mmv = P.do("tensor", lambda e, d8=d8: e.transpose(
                    psT[:, d8 * 128:(d8 + 1) * 128],
                    xn2[:, d8 * 128:(d8 + 1) * 128], ident[:]),
                    inc="sM" if d8 == 7 else None)
            P.wait("vector", "sM", mmv)
            for d8 in range(8):
                vv = P.do("vector", lambda e, d8=d8, ti=ti: e.tensor_scalar(
                    xn2gT[:, d8 * 1024 + ti * 128: d8 * 1024 + (ti + 1) * 128],
                    psT[:, d8 * 128:(d8 + 1) * 128],
                    g1c[:, d8:d8 + 1], None, op0=ALU.mult),
                    inc="sV" if d8 == 7 else None)
            P.wait("tensor", "sV", vv)

        gelu_s = [0, 0]
        w1mm = [0, 0]
        for hc in range(32):
            slot = hc % 2
            P.wait("sync", "sM", w1mm[slot])
            w1lw = dma("sync", w1r[:, slot * 1024:(slot + 1) * 1024],
                       w1t_scr.ap()[hc, :, :], "sX")
            pb = ps01 if slot == 0 else ps23
            P.wait("tensor", "sS", gelu_s[slot])
            P.wait("tensor", "sX", w1lw)
            for th in range(2):
                for dc in range(8):
                    mmv = P.do("tensor", lambda e, th=th, dc=dc, pb=pb, slot=slot:
                               e.matmul(
                                   pb[:, th * 512:(th + 1) * 512],
                                   w1r[:, slot * 1024 + dc * 128:
                                       slot * 1024 + (dc + 1) * 128],
                                   xn2gT[:, dc * 1024 + th * 512:
                                         dc * 1024 + (th + 1) * 512],
                                   start=(dc == 0), stop=(dc == 7)),
                               inc="sM" if (th == 1 and dc == 7) else None)
            if f == 0:
                for half16 in range(2):
                    for dc in range(8):
                        bmv = P.do("tensor", lambda e, dc=dc, slot=slot, half16=half16:
                                   e.matmul(
                                       psPV0[:, 0:1],
                                       w1r[:, slot * 1024 + dc * 128:
                                           slot * 1024 + (dc + 1) * 128],
                                       b1lnc[:, half16 * 8 + dc: half16 * 8 + dc + 1],
                                       start=(half16 == 0 and dc == 0),
                                       stop=(half16 == 1 and dc == 7)),
                                   inc="sM" if (half16 == 1 and dc == 7) else None)
                w1mm[slot] = bmv
                P.wait("vector", "sM", bmv)
                P.do("vector", lambda e, hc=hc: e.tensor_scalar(
                    b1tot[:, hc:hc + 1], psPV0[:, 0:1], m1bc[:], None, op0=ALU.mult),
                    inc="sV")
                vgate()
                vv = P.do("vector", lambda e, hc=hc: e.tensor_tensor(
                    b1tot[:, hc:hc + 1], b1tot[:, hc:hc + 1], b1c[:, hc:hc + 1],
                    ALU.add), inc="sV")
                P.wait("scalar", "sV", vv)
                P.wait("tensor", "sV", vv)
            else:
                w1mm[slot] = mmv
            P.wait("scalar", "sM", mmv)
            gelu_s[slot] = P.do("scalar", lambda e, hc=hc, pb=pb: e.activation(
                out=uP[:, hc * 1024:(hc + 1) * 1024], in_=pb[:], func=AF.Gelu,
                bias=b1tot[:, hc:hc + 1], scale=m1bc[:]), inc="sS")
        P.wait("tensor", "sS", P.mark("sS"))

        for hc in range(32):
            for th in range(2):
                grp = psM if th == 0 else psPV0
                mmv = P.do("tensor", lambda e, hc=hc, th=th, grp=grp: e.matmul(
                    grp[0:1, :], onescol[:, 0:1],
                    uP[:, hc * 1024 + th * 512: hc * 1024 + (th + 1) * 512],
                    start=(hc == 0), stop=(hc == 31)),
                    inc="sM" if (hc == 31 and th == 1) else None)
        P.wait("vector", "sM", mmv)
        P.do("vector", lambda e: e.tensor_scalar(
            rows_[0:1, RA:RA + 512], psM[0:1, :], 1.0 / HID, None, op0=ALU.mult))
        vv = P.do("vector", lambda e: e.tensor_scalar(
            rows_[0:1, RA + 512:RA + 1024], psPV0[0:1, :], 1.0 / HID, None,
            op0=ALU.mult), inc="sV")
        P.wait("tensor", "sV", vv)
        u2_m = [0, 0]
        for hc in range(32):
            slot = hc % 2
            P.wait("vector", "sM", u2_m[slot])
            uvv = P.do("vector", lambda e, hc=hc, slot=slot: e.tensor_tensor(
                u2r[:, slot * 1024:(slot + 1) * 1024],
                uP[:, hc * 1024:(hc + 1) * 1024],
                uP[:, hc * 1024:(hc + 1) * 1024], ALU.mult), inc="sV")
            P.wait("tensor", "sV", uvv)
            for th in range(2):
                grp = psM if th == 0 else psPV0
                u2_m[slot] = P.do("tensor", lambda e, hc=hc, th=th, grp=grp, slot=slot:
                                  e.matmul(
                                      grp[0:1, :], onescol[:, 0:1],
                                      u2r[:, slot * 1024 + th * 512:
                                          slot * 1024 + (th + 1) * 512],
                                      start=(hc == 0), stop=(hc == 31)),
                                  inc="sM" if th == 1 else None)
        P.wait("vector", "sM", max(u2_m))
        P.do("vector", lambda e: e.tensor_scalar(
            rows_[0:1, RB:RB + 512], psM[0:1, :], 1.0 / HID, None, op0=ALU.mult))
        P.do("vector", lambda e: e.tensor_scalar(
            rows_[0:1, RB + 512:RB + 1024], psPV0[0:1, :], 1.0 / HID, None,
            op0=ALU.mult))
        vgate()
        P.do("vector", lambda e: e.tensor_scalar(
            rowsb[0:1, 0:1024], rows_[0:1, RA:RA + 1024], -1.0, None, op0=ALU.mult))
        P.do("vector", lambda e: e.tensor_tensor(
            rows_[0:1, RC:RC + 1024], rows_[0:1, RA:RA + 1024],
            rows_[0:1, RA:RA + 1024], ALU.mult), inc="sV")
        vgate()
        P.do("vector", lambda e: e.tensor_tensor(
            rows_[0:1, RB:RB + 1024], rows_[0:1, RB:RB + 1024],
            rows_[0:1, RC:RC + 1024], ALU.subtract), inc="sV")
        vgate()
        vv = P.do("vector", lambda e: e.tensor_scalar(
            rows_[0:1, RB:RB + 1024], rows_[0:1, RB:RB + 1024], EPS, None,
            op0=ALU.add), inc="sV")
        P.wait("scalar", "sV", vv)
        ss = P.do("scalar", lambda e: e.activation(
            out=rows_[0:1, RA:RA + 1024], in_=rows_[0:1, RB:RB + 1024], func=AF.Sqrt,
            bias=0.0, scale=1.0), inc="sS")
        P.wait("vector", "sS", ss)
        P.do("vector", lambda e: e.tensor_copy(
            rowsb[0:1, 1024:2048], rows_[0:1, RA:RA + 1024]))
        vv = P.do("vector", lambda e: e.reciprocal(
            rows_[0:1, RC:RC + 1024], rows_[0:1, RA:RA + 1024]), inc="sV")
        vgate()
        P.wait("tensor", "sV", vv)
        for ti in range(8):
            mmv = P.do("tensor", lambda e, ti=ti: e.transpose(
                psM[:, 0:1], rows_[0:1, RC + ti * 128: RC + (ti + 1) * 128],
                identf[0:1, 0:1]), inc="sM")
            P.wait("vector", "sM", mmv)
            vv = P.do("vector", lambda e, ti=ti: e.tensor_copy(
                fRc[:, ti:ti + 1], psM[:, 0:1]), inc="sV")
            P.wait("tensor", "sV", vv)

        w2mm = [0, 0]
        ev_done = [0, 0]
        for dc in range(8):
            slot = dc % 2
            P.wait("sync", "sM", w2mm[slot])
            w2lw = dma("sync", w2r[:, slot * 4096:(slot + 1) * 4096],
                       w2t_scr.ap()[dc, :, :], "sX")
            pb = ps01 if slot == 0 else ps23
            P.wait("tensor", "sV", ev_done[slot])
            P.wait("tensor", "sX", w2lw)
            for th in range(2):
                for hc in range(32):
                    P.do("tensor", lambda e, th=th, hc=hc, pb=pb, slot=slot: e.matmul(
                        pb[:, th * 512:(th + 1) * 512],
                        w2r[:, slot * 4096 + hc * 128: slot * 4096 + (hc + 1) * 128],
                        uP[:, hc * 1024 + th * 512: hc * 1024 + (th + 1) * 512],
                        start=(hc == 0), stop=False))
                P.do("tensor", lambda e, th=th, dc=dc, pb=pb: e.matmul(
                    pb[:, th * 512:(th + 1) * 512],
                    wsumr[0:1, dc * 128:(dc + 1) * 128],
                    rowsb[0:1, th * 512:(th + 1) * 512],
                    start=False, stop=False))
                mmv = P.do("tensor", lambda e, th=th, dc=dc, pb=pb: e.matmul(
                    pb[:, th * 512:(th + 1) * 512],
                    biasr[0:1, dc * 128:(dc + 1) * 128],
                    rowsb[0:1, 1024 + th * 512: 1024 + (th + 1) * 512],
                    start=False, stop=True), inc="sM" if th == 1 else None)
            w2mm[slot] = mmv
            P.wait("vector", "sM", mmv)
            vv = P.do("vector", lambda e, pb=pb: e.tensor_copy(w2o[:], pb[:]), inc="sV")
            ev_done[slot] = vv
            P.wait("tensor", "sV", vv)
            for tg in range(2):
                for k4 in range(4):
                    ti = tg * 4 + k4
                    mmv = P.do("tensor", lambda e, ti=ti, k4=k4: e.transpose(
                        psT[:, k4 * 128:(k4 + 1) * 128],
                        w2o[:, ti * 128:(ti + 1) * 128], ident[:]),
                        inc="sM" if k4 == 3 else None)
                P.wait("vector", "sM", mmv)
                for k4 in range(4):
                    ti = tg * 4 + k4
                    ych_v = P.do("vector", lambda e, ti=ti, k4=k4, dc=dc:
                                 e.scalar_tensor_tensor(
                                     ybuf[:, ti * 1024 + dc * 128:
                                          ti * 1024 + (dc + 1) * 128],
                                     psT[:, k4 * 128:(k4 + 1) * 128],
                                     fRc[:, ti:ti + 1],
                                     xpc[:, ti * 1024 + dc * 128:
                                         ti * 1024 + (dc + 1) * 128],
                                     op0=ALU.mult, op1=ALU.add),
                                 inc="sV" if k4 == 3 else None)
                P.wait("tensor", "sV", ych_v)
        P.wait("scalar", "sV", ych_v)
        yst_w = []
        for ti in range(8):
            t = f * 8 + ti
            yst_w.append(dma("scalar", y_h.ap()[t * 128:(t + 1) * 128, :],
                             ybuf[:, ti * 1024:(ti + 1) * 1024]))

    return finalize()


def kernel(L0_attn_x, L0_attn_gamma, L0_attn_Wq, L0_attn_Wkv, L0_attn_W_bilinear,
           L0_attn_Wo, L0_ffn_ln1_g, L0_ffn_ln1_b, L0_ffn_w1, L0_ffn_b1,
           L0_ffn_ln2_g, L0_ffn_ln2_b, L0_ffn_w2, L0_ffn_b2):
    tabc, tabs = _host_consts()
    id16 = np.eye(128, dtype=np.float32)
    id32 = np.eye(128, dtype=np.float32)
    wb = np.asarray(L0_attn_W_bilinear, np.float32)
    wb2 = np.zeros((8, 128, 128), np.float32)
    for p in range(8):
        wb2[p, 0:64, 0:64] = wb[2 * p]
        wb2[p, 64:128, 64:128] = wb[2 * p + 1]

    common = {
        "wq": np.ascontiguousarray(np.asarray(L0_attn_Wq, np.float32)),
        "wkv": np.ascontiguousarray(np.asarray(L0_attn_Wkv, np.float32)),
        "wb2": wb2,
        "wo": np.ascontiguousarray(np.asarray(L0_attn_Wo, np.float32)),
        "w1": np.ascontiguousarray(np.asarray(L0_ffn_w1, np.float32)),
        "w2": np.ascontiguousarray(np.asarray(L0_ffn_w2, np.float32)),
        "gam": _cols(L0_attn_gamma),
        "g1": _cols(L0_ffn_ln1_g),
        "b1ln": _cols(L0_ffn_ln1_b),
        "b1": _cols(L0_ffn_b1),
        "g2": _cols(L0_ffn_ln2_g),
        "b2ln": _cols(L0_ffn_ln2_b),
        "b2": _cols(L0_ffn_b2),
        "tabc": tabc,
        "tabs": tabs,
        "id16": id16,
        "id32": id32,
    }
    x = np.asarray(L0_attn_x, np.float32)
    in_maps = []
    for b in range(NCORES):
        m = dict(common)
        m["x"] = np.ascontiguousarray(x[b])
        in_maps.append(m)

    nc, ctx = build_nc()
    try:
        res = run_bass_kernel_spmd(nc, in_maps, core_ids=list(range(NCORES)))
    finally:
        ctx.close()
    return np.stack([r["y"] for r in res.results]).astype(np.float32)


# version 09
# speedup vs baseline: 1.3237x; 1.0023x over previous
import numpy as np
import ml_dtypes
from contextlib import ExitStack
import concourse.bass as bass
import concourse.mybir as mybir
from concourse.bass_utils import run_bass_kernel_spmd

F32 = mybir.dt.float32
BF16 = mybir.dt.bfloat16
FP8 = mybir.dt.float8e4
AF = mybir.ActivationFunctionType
ALU = mybir.AluOpType
AX = mybir.AxisListType

NCORES = 8
N = 4096
D = 1024
H = 16
DH = 64
HID = 4096
NT = 32
QC = 512
NQC = N // QC
FT = 1024
NFC = N // FT
EPS = 1e-5
ENGINES = ("sync", "tensor", "vector", "scalar", "gpsimd")


def _host_consts():
    theta = 1.0 / (10000.0 ** (np.arange(0, DH, 2, dtype=np.float64) / DH))
    pos = np.arange(N, dtype=np.float64)
    freqs = pos[:, None] * theta[None, :]
    emb = np.concatenate([freqs, freqs], axis=-1)
    cos = np.cos(emb)
    ssin = np.sin(emb)
    ssin[:, 0::2] *= -1.0

    def tl(a):
        return np.ascontiguousarray(
            a.reshape(NT, 128, DH).transpose(1, 0, 2).reshape(128, NT * DH)
        ).astype(np.float32)
    return tl(cos), tl(ssin)


def _cols(v):
    n = np.asarray(v).shape[0]
    return np.ascontiguousarray(np.asarray(v, np.float32).reshape(n // 128, 128).T)


class Ctr:
    __slots__ = ("sem", "n")

    def __init__(self, sem):
        self.sem = sem
        self.n = 0


class DmaRing:

    def __init__(self, nc, ctx, prefix, depth):
        self.ctrs = [Ctr(ctx.enter_context(nc.semaphore(f"{prefix}{i}")))
                     for i in range(depth)]
        self.i = 0

    def next(self):
        c = self.ctrs[self.i % len(self.ctrs)]
        self.i += 1
        prev = c.n
        c.n += 16
        return c, prev


class Prog:
    def __init__(self, nc, ctx):
        self.nc = nc
        self.plan = {e: [] for e in ENGINES}
        self.ctr = {}
        for name in ("sM", "sV", "sS", "sG"):
            self.ctr[name] = Ctr(ctx.enter_context(nc.semaphore(name)))
        self.rings = {
            "sync": DmaRing(nc, ctx, "rX", 10),
            "gpsimd": DmaRing(nc, ctx, "rC", 6),
            "scalar": DmaRing(nc, ctx, "rW", 6),
        }

    def emit(self, eng, fn):
        self.plan[eng].append(fn)

    def wait(self, eng, cname, val=None):
        if isinstance(val, tuple):
            return self.waitt(eng, val)
        c = self.ctr[cname]
        v = c.n if val is None else val
        if v is None or v <= 0:
            return
        assert v <= c.n, f"wait on future {cname}: {v} > {c.n}"
        self.emit(eng, lambda e, s=c.sem, v=v: e.wait_ge(s, v))

    def waitt(self, eng, tok):
        if not tok:
            return
        if isinstance(tok, list):
            for t in tok:
                self.waitt(eng, t)
            return
        c, v = tok
        self.emit(eng, lambda e, s=c.sem, v=v: e.wait_ge(s, v))

    def wait_all(self, eng, q):
        for c in self.rings[q].ctrs:
            if c.n > 0:
                self.emit(eng, lambda e, s=c.sem, v=c.n: e.wait_ge(s, v))

    def dmaq(self, q, dst, src):
        c, prev = self.rings[q].next()
        if prev > 0:
            self.emit(q, lambda e, s=c.sem, v=prev: e.wait_ge(s, v))
        self.emit(q, lambda e, d=dst, s=src, sm=c.sem: e.dma_start(
            out=d, in_=s).then_inc(sm, 16))
        return (c, c.n)

    def do(self, eng, build, inc=None):
        if inc is not None:
            c = self.ctr[inc]
            c.n += 1
            self.emit(eng, lambda e, b=build, s=c.sem: b(e).then_inc(s, 1))
            return c.n
        self.emit(eng, lambda e, b=build: b(e))
        return None

    def mark(self, cname):
        return self.ctr[cname].n


def build_nc(stop_after=None):
    nc = bass.Bass("TRN2", target_bir_lowering=False, debug=False)
    ctx = ExitStack()
    AP = bass.AP

    x_h = nc.dram_tensor("x", [N, D], F32, kind="ExternalInput")
    wq_h = nc.dram_tensor("wq", [D, D], F32, kind="ExternalInput")
    wkv_h = nc.dram_tensor("wkv", [D, 2 * D], F32, kind="ExternalInput")
    wb2_h = nc.dram_tensor("wb2", [8, 128, 128], F32, kind="ExternalInput")
    wo_h = nc.dram_tensor("wo", [D, D], F32, kind="ExternalInput")
    w1_h = nc.dram_tensor("w1", [HID, D], F32, kind="ExternalInput")
    w2_h = nc.dram_tensor("w2", [D, HID], F32, kind="ExternalInput")
    gam_h = nc.dram_tensor("gam", [128, 8], F32, kind="ExternalInput")
    g1_h = nc.dram_tensor("g1", [128, 8], F32, kind="ExternalInput")
    b1ln_h = nc.dram_tensor("b1ln", [128, 8], F32, kind="ExternalInput")
    b1_h = nc.dram_tensor("b1", [128, 32], F32, kind="ExternalInput")
    g2_h = nc.dram_tensor("g2", [128, 32], F32, kind="ExternalInput")
    b2ln_h = nc.dram_tensor("b2ln", [128, 32], F32, kind="ExternalInput")
    b2_h = nc.dram_tensor("b2", [128, 8], F32, kind="ExternalInput")
    tabc_h = nc.dram_tensor("tabc", [128, NT * DH], F32, kind="ExternalInput")
    tabs_h = nc.dram_tensor("tabs", [128, NT * DH], F32, kind="ExternalInput")
    id16_h = nc.dram_tensor("id16", [128, 128], F32, kind="ExternalInput")
    id32_h = nc.dram_tensor("id32", [128, 128], F32, kind="ExternalInput")

    xa_scr = nc.dram_tensor("xa_scr", [N, D], F32, kind="Internal")
    wqs_scr = nc.dram_tensor("wqs_scr", [D, D], BF16, kind="Internal")
    w1t_scr = nc.dram_tensor("w1t_scr", [32, 128, 1024], FP8, kind="Internal")
    w2t_scr = nc.dram_tensor("w2t_scr", [8, 128, 4096], BF16, kind="Internal")

    y_h = nc.dram_tensor("y", [N, D], F32, kind="ExternalOutput")
    dbg_h = None
    if stop_after is not None:
        dbg_h = nc.dram_tensor("dbg", [128, 40960], F32, kind="ExternalOutput")

    ps01 = ctx.enter_context(nc.psum_tensor("ps01", [128, 1024], F32))
    ps23 = ctx.enter_context(nc.psum_tensor("ps23", [128, 1024], F32))
    psPV0 = ctx.enter_context(nc.psum_tensor("psPV0", [128, 512], F32))
    psPV1 = ctx.enter_context(nc.psum_tensor("psPV1", [128, 512], F32))
    psT = ctx.enter_context(nc.psum_tensor("psT", [128, 1024], BF16))
    psM = ctx.enter_context(nc.psum_tensor("psM", [128, 512], F32))

    wb2 = ctx.enter_context(nc.sbuf_tensor("wb2_s", [128, 8 * 128], BF16))
    ident = ctx.enter_context(nc.sbuf_tensor("ident", [128, 128], BF16))
    identf = ctx.enter_context(nc.sbuf_tensor("identf", [128, 128], F32))
    onescol = ctx.enter_context(nc.sbuf_tensor("onescol", [128, 1], BF16))
    onescolf = ctx.enter_context(nc.sbuf_tensor("onescolf", [128, 1], F32))
    gam = ctx.enter_context(nc.sbuf_tensor("gam_s", [128, 8], F32))
    g1c = ctx.enter_context(nc.sbuf_tensor("g1c", [128, 8], F32))
    b1lnc = ctx.enter_context(nc.sbuf_tensor("b1lnc", [128, 16], BF16))
    b1c = ctx.enter_context(nc.sbuf_tensor("b1c", [128, 32], F32))
    g2c = ctx.enter_context(nc.sbuf_tensor("g2c", [128, 32], F32))
    b2lnc = ctx.enter_context(nc.sbuf_tensor("b2lnc", [128, 64], BF16))
    b2c = ctx.enter_context(nc.sbuf_tensor("b2c", [128, 8], F32))
    mu0 = ctx.enter_context(nc.sbuf_tensor("mu0", [128, NT], F32))
    nmu0 = ctx.enter_context(nc.sbuf_tensor("nmu0", [128, NT], F32))
    R0 = ctx.enter_context(nc.sbuf_tensor("R0", [128, NT], F32))
    ssq0 = ctx.enter_context(nc.sbuf_tensor("ssq0", [128, NT], F32))
    lnscr = ctx.enter_context(nc.sbuf_tensor("lnscr", [128, NT], F32))
    lnscr2 = ctx.enter_context(nc.sbuf_tensor("lnscr2", [128, NT], F32))
    junkb = ctx.enter_context(nc.sbuf_tensor("junkb", [128, 1024], BF16))
    junks = ctx.enter_context(nc.sbuf_tensor("junks", [128, 1024], BF16))
    rawsum = ctx.enter_context(nc.sbuf_tensor("rawsum", [128, NT], F32))
    s1bc = ctx.enter_context(nc.sbuf_tensor("s1bc", [128, 1], F32))
    m1bc = ctx.enter_context(nc.sbuf_tensor("m1bc", [128, 1], F32))
    s2bc = ctx.enter_context(nc.sbuf_tensor("s2bc", [128, 1], F32))
    m2c = ctx.enter_context(nc.sbuf_tensor("m2c", [128, 1], F32))
    gmc = ctx.enter_context(nc.sbuf_tensor("gmc", [128, 32], F32))
    gmcb = ctx.enter_context(nc.sbuf_tensor("gmcb", [128, 32], BF16))
    b1tot = ctx.enter_context(nc.sbuf_tensor("b1tot", [128, 32], F32))
    wsumr = ctx.enter_context(nc.sbuf_tensor("wsumr", [128, 1024], BF16))
    biasr = ctx.enter_context(nc.sbuf_tensor("biasr", [128, 1024], BF16))
    biasC = ctx.enter_context(nc.sbuf_tensor("biasC", [128, 8], F32))
    wsC = ctx.enter_context(nc.sbuf_tensor("wsC", [128, 8], F32))
    wpart = ctx.enter_context(nc.sbuf_tensor("wpart", [128, 1], F32))
    wtmp = ctx.enter_context(nc.sbuf_tensor("wtmp", [128, 2], F32))

    P = Prog(nc, ctx)

    def dma(q, dst, src, sem=None):
        return P.dmaq(q, dst, src)

    def LP(build):
        def f(e):
            with nc.allow_low_precision(reason="intentional bf16 operand"):
                return build(e)
        return f

    def vgate():
        P.wait("vector", "sV", P.mark("sV"))

    def phase_barrier():
        for eng in ENGINES:
            for cn in ("sM", "sV", "sS", "sG"):
                P.wait(eng, cn, P.mark(cn))
            for q in ("sync", "gpsimd", "scalar"):
                P.wait_all(eng, q)

    def ggate():
        P.wait("gpsimd", "sG", P.mark("sG"))

    dbg_state = {"off": 0}

    def dump(buf_ap, rows=128):
        w = buf_ap.shape[-1]
        off = dbg_state["off"]
        for cn in ("sV", "sS", "sM", "sG"):
            P.wait("gpsimd", cn, P.mark(cn))
        for q in ("sync", "gpsimd", "scalar"):
            P.wait_all("gpsimd", q)
        dma("gpsimd", dbg_h.ap()[0:rows, off:off + w], buf_ap)
        dbg_state["off"] += w

    def finalize():
        for cname in ("sM", "sV", "sS", "sG"):
            P.wait("gpsimd", cname, P.mark(cname))
        for q in ("sync", "gpsimd", "scalar"):
            P.wait_all("gpsimd", q)
        with nc.Block() as block:
            @block.sync
            def _(e):
                for fn in P.plan["sync"]:
                    fn(e)

            @block.tensor
            def _(e):
                for fn in P.plan["tensor"]:
                    fn(e)

            @block.vector
            def _(e):
                for fn in P.plan["vector"]:
                    fn(e)

            @block.scalar
            def _(e):
                for fn in P.plan["scalar"]:
                    fn(e)

            @block.gpsimd
            def _(e):
                for fn in P.plan["gpsimd"]:
                    fn(e)
        return nc, ctx

    dma("sync", gam[:], gam_h.ap(), "sX")
    dma("sync", g1c[:], g1_h.ap(), "sX")
    dma("sync", b1c[:], b1_h.ap(), "sX")
    dma("sync", g2c[:], g2_h.ap(), "sX")
    dma("sync", b2c[:], b2_h.ap(), "sX")
    dma("gpsimd", wb2[:], AP(wb2_h, 0, [[128, 128], [16384, 8], [1, 128]]), "sC")
    dma("gpsimd", ident[:], id16_h.ap(), "sC")
    dma("sync", identf[:], id32_h.ap(), "sX")
    dma("gpsimd", b1lnc[:, 0:8], b1ln_h.ap(), "sC")
    dma("gpsimd", b2lnc[:, 0:32], b2ln_h.ap(), "sC")
    dma("sync", lnscr[:, 0:8], b1ln_h.ap(), "sX")
    dma("sync", lnscr2[:], b2ln_h.ap(), "sX")
    P.do("vector", lambda e: e.memset(onescol[:], 1.0))
    P.do("vector", lambda e: e.memset(onescolf[:], 1.0), inc="sV")
    P.wait_all("tensor", "sync")
    P.wait_all("vector", "sync")
    P.wait_all("scalar", "sync")
    P.wait_all("tensor", "gpsimd")
    P.wait_all("gpsimd", "gpsimd")
    P.wait_all("vector", "gpsimd")
    P.do("vector", lambda e: e.tensor_tensor(
        b1lnc[:, 8:16], lnscr[:, 0:8], b1lnc[:, 0:8], ALU.subtract), inc="sV")
    P.do("vector", lambda e: e.tensor_tensor(
        b2lnc[:, 32:64], lnscr2[:], b2lnc[:, 0:32], ALU.subtract), inc="sV")
    P.wait("tensor", "sV", P.mark("sV"))
    P.wait("tensor", "sV")

    def ln_stats_tile(t, xbuf_ap, mu, nmu, ssq, xw_v):
        if xw_v:
            P.wait("vector", "sX", xw_v)
        P.do("vector", lambda e: e.reduce_sum(rawsum[:, t:t + 1], xbuf_ap, axis=AX.X),
             inc="sV")
        vgate()
        P.do("vector", lambda e: e.tensor_scalar(
            mu[:, t:t + 1], rawsum[:, t:t + 1], 1.0 / D, None, op0=ALU.mult))
        vv = P.do("vector", lambda e: e.tensor_scalar(
            nmu[:, t:t + 1], rawsum[:, t:t + 1], -1.0 / D, None, op0=ALU.mult),
            inc="sV")
        P.wait("scalar", "sV", vv)
        P.wait("scalar", "sS", P.mark("sS"))
        P.do("scalar", lambda e: e.activation(
            out=junks[:], in_=xbuf_ap, func=AF.Square,
            bias=nmu[:, t:t + 1], scale=1.0, accum_out=ssq[:, t:t + 1]), inc="sS")

    def ln_R_batch(t0, nt, ssq, Rbuf, dim):
        sl = (slice(None), slice(t0, t0 + nt))
        P.wait("vector", "sS", P.mark("sS"))
        P.do("vector", lambda e: e.tensor_scalar(
            lnscr[sl], ssq[sl], 1.0 / dim, EPS, op0=ALU.mult, op1=ALU.add), inc="sV")
        P.wait("scalar", "sV", P.mark("sV"))
        ss = P.do("scalar", lambda e: e.activation(
            out=lnscr2[sl], in_=lnscr[sl], func=AF.Sqrt, bias=0.0, scale=1.0), inc="sS")
        P.wait("vector", "sS", ss)
        P.do("vector", lambda e: e.reciprocal(Rbuf[sl], lnscr2[sl]))
        P.do("vector", lambda e: e.tensor_scalar(
            lnscr[sl], lnscr[sl], -EPS, None, op0=ALU.add), inc="sV")
        vgate()
        P.do("vector", lambda e: e.tensor_tensor(
            lnscr[sl], lnscr[sl], Rbuf[sl], ALU.mult), inc="sV")
        vgate()
        P.do("vector", lambda e: e.tensor_tensor(
            lnscr[sl], lnscr[sl], Rbuf[sl], ALU.mult), inc="sV")
        vgate()
        vv = P.do("vector", lambda e: e.tensor_scalar(
            lnscr[sl], lnscr[sl], EPS, None, op0=ALU.add), inc="sV")
        P.wait("scalar", "sV", vv)
        ss = P.do("scalar", lambda e: e.activation(
            out=lnscr2[sl], in_=lnscr[sl], func=AF.Sqrt, bias=0.0, scale=1.0), inc="sS")
        P.wait("vector", "sS", ss)
        P.do("vector", lambda e: e.reciprocal(lnscr2[sl], lnscr2[sl]), inc="sV")
        vgate()
        P.do("vector", lambda e: e.tensor_tensor(Rbuf[sl], Rbuf[sl], lnscr2[sl], ALU.mult),
             inc="sV")
        vgate()

    attctx = ctx.enter_context(ExitStack())
    kT = attctx.enter_context(nc.sbuf_tensor("kT", [128, 8 * N], BF16))
    vaug = attctx.enter_context(nc.sbuf_tensor("vaug", [128, NT * 1040], BF16))
    tabc_s = attctx.enter_context(nc.sbuf_tensor("tabc_s", [128, NT * DH], BF16))
    tabs_s = attctx.enter_context(nc.sbuf_tensor("tabs_s", [128, NT * DH], BF16))
    dma("gpsimd", tabc_s[:], tabc_h.ap(), "sC")
    dma("gpsimd", tabs_s[:], tabs_h.ap(), "sC")
    vgate()
    P.do("vector", lambda e: e.tensor_copy(
        AP(vaug, 64, [[NT * 1040, 128], [1040, NT], [65, H]]),
        onescol[:, 0:1].to_broadcast([128, NT * H])), inc="sV")
    P.wait_all("vector", "gpsimd")

    def rope_emit(dst, ps_handle, t, nheads, prow):
        w = nheads * DH
        cap = AP(tabc_s, t * DH, [[NT * DH, 128], [0, nheads], [1, DH]])
        sap = AP(tabs_s, t * DH, [[NT * DH, 128], [0, nheads], [1, DH]])
        src = AP(ps_handle, 0, [[prow, 128], [1, w]])
        partner = AP(ps_handle, 1, [[prow, 128], [DH, nheads], [2, DH // 2], [-1, 2]])
        P.do("vector", lambda e: e.tensor_tensor(dst, src, cap, ALU.mult))
        P.do("vector", lambda e: e.tensor_tensor(junkb[:, 0:w], partner, sap, ALU.mult),
             inc="sV")
        vgate()
        return P.do("vector", lambda e: e.tensor_tensor(
            dst, dst, junkb[:, 0:w], ALU.add), inc="sV")

    kvctx = ctx.enter_context(ExitStack())
    wkvb = kvctx.enter_context(nc.sbuf_tensor("wkvb", [128, 8 * 2048], BF16))
    wslf = kvctx.enter_context(nc.sbuf_tensor("wslf", [128, 1024], F32))
    xr0 = kvctx.enter_context(nc.sbuf_tensor("xr0", [128, 4 * 1024], F32))
    xn0 = kvctx.enter_context(nc.sbuf_tensor("xn0", [128, 1024], BF16))
    xnT0 = kvctx.enter_context(nc.sbuf_tensor("xnT0", [128, 1024], BF16))
    ktile = kvctx.enter_context(nc.sbuf_tensor("ktile", [128, 1024], BF16))

    pv_wsl = 0
    wq_st = None
    for dc in range(8):
        P.wait("sync", "sV", pv_wsl)
        lw = dma("sync", wslf[:], wq_h.ap()[dc * 128:(dc + 1) * 128, :])
        P.waitt("vector", lw)
        P.waitt("vector", wq_st)
        vv = P.do("vector", lambda e, dc=dc: e.tensor_scalar(
            junkb[:], wslf[:], gam[:, dc:dc + 1], 0.125,
            op0=ALU.mult, op1=ALU.mult), inc="sV")
        pv_wsl = vv
        P.wait("scalar", "sV", vv)
        wq_st = dma("scalar", wqs_scr.ap()[dc * 128:(dc + 1) * 128, :], junkb[:])
    P.waitt("vector", wq_st)
    for dc in range(8):
        for ecg in range(2):
            P.wait("sync", "sV", pv_wsl)
            lw = dma("sync", wslf[:],
                     wkv_h.ap()[dc * 128:(dc + 1) * 128, ecg * 1024:(ecg + 1) * 1024],
                     "sX")
            P.wait("vector", "sX", lw)
            pv_wsl = P.do("vector", lambda e, dc=dc, ecg=ecg: e.tensor_scalar(
                wkvb[:, dc * 2048 + ecg * 1024: dc * 2048 + (ecg + 1) * 1024],
                wslf[:], gam[:, dc:dc + 1], None, op0=ALU.mult), inc="sV")
    P.wait("tensor", "sV", P.mark("sV"))

    kv_ev = 0
    tr_ev = 0
    sq_last = 0
    for g in range(8):
        xw = []
        P.wait("sync", "sV", tr_ev)
        P.wait("sync", "sS", sq_last)
        for i in range(4):
            t = g * 4 + i
            xw.append(dma("sync", xr0[:, i * 1024:(i + 1) * 1024],
                          x_h.ap()[t * 128:(t + 1) * 128, :], "sX"))
        for i in range(4):
            ln_stats_tile(g * 4 + i, xr0[:, i * 1024:(i + 1) * 1024], mu0, nmu0, ssq0,
                          xw[i])
        sq_last = P.mark("sS")
        ln_R_batch(g * 4, 4, ssq0, R0, D)
        for i in range(4):
            t = g * 4 + i
            vv = P.do("vector", lambda e, i=i, t=t: e.tensor_scalar(
                xn0[:], xr0[:, i * 1024:(i + 1) * 1024], mu0[:, t:t + 1], R0[:, t:t + 1],
                op0=ALU.subtract, op1=ALU.mult), inc="sV")
            P.wait("tensor", "sV", vv)
            P.wait("tensor", "sV", tr_ev)
            for d8 in range(8):
                mmv = P.do("tensor", lambda e, d8=d8: e.transpose(
                    psT[:, d8 * 128:(d8 + 1) * 128],
                    xn0[:, d8 * 128:(d8 + 1) * 128], ident[:]),
                    inc="sM" if d8 == 7 else None)
            P.wait("vector", "sM", mmv)
            tr_ev = P.do("vector", lambda e: e.tensor_copy(xnT0[:], psT[:]), inc="sV")
            P.wait("tensor", "sV", tr_ev)
            P.wait("tensor", "sV", kv_ev)
            for dc in range(8):
                for ec in range(4):
                    pb = ps01 if ec < 2 else ps23
                    mmv = P.do("tensor", lambda e, dc=dc, ec=ec, pb=pb: e.matmul(
                        pb[:, (ec % 2) * 512:((ec % 2) + 1) * 512],
                        xnT0[:, dc * 128:(dc + 1) * 128],
                        wkvb[:, dc * 2048 + ec * 512: dc * 2048 + (ec + 1) * 512],
                        start=(dc == 0), stop=(dc == 7)),
                        inc="sM" if (dc == 7 and ec == 3) else None)
            P.wait("vector", "sM", mmv)
            rope_emit(ktile[:], ps01, t, 16, 1024)
            kv_ev = P.do("vector", lambda e, t=t: e.tensor_copy(
                AP(vaug, t * 1040, [[NT * 1040, 128], [65, 16], [1, 64]]), ps23[:]), inc="sV")
            P.wait("tensor", "sV", kv_ev)
            for d8 in range(8):
                mmv = P.do("tensor", lambda e, d8=d8: e.transpose(
                    psT[:, d8 * 128:(d8 + 1) * 128],
                    ktile[:, d8 * 128:(d8 + 1) * 128], ident[:]),
                    inc="sM" if d8 == 7 else None)
            P.wait("vector", "sM", mmv)
            for d8 in range(8):
                tr_ev = P.do("vector", lambda e, d8=d8, t=t: e.tensor_copy(
                    kT[:, d8 * N + t * 128: d8 * N + (t + 1) * 128],
                    psT[:, d8 * 128:(d8 + 1) * 128]),
                    inc="sV" if d8 == 7 else None)
    P.wait("tensor", "sV", tr_ev)
    phase_barrier()
    kvctx.close()

    if stop_after == "P12":
        dump(kT[:, 0:512])
        dump(kT[:, 7 * N + 1024:7 * N + 1536])
        dump(vaug[:, 0:1040])
        dump(mu0[:])
        dump(R0[:])
        return finalize()

    ackctx = ctx.enter_context(ExitStack())
    xq = ackctx.enter_context(nc.sbuf_tensor("xq", [128, 1024], F32))
    xn_t = ackctx.enter_context(nc.sbuf_tensor("xn_t", [128, 1024], BF16))
    xnT4 = ackctx.enter_context(nc.sbuf_tensor("xnT4", [128, 4096], BF16))
    qtile = ackctx.enter_context(nc.sbuf_tensor("qtile", [128, 512], BF16))
    qTc = ackctx.enter_context(nc.sbuf_tensor("qTc", [128, 8 * 512], BF16))
    PT = ackctx.enter_context(nc.sbuf_tensor("PT", [128, 4 * 1024], BF16))
    km2b = ackctx.enter_context(nc.sbuf_tensor("km2b", [128, 128], F32))
    kmax2 = ackctx.enter_context(nc.sbuf_tensor("kmax2", [128, 16], F32))
    qm2 = ackctx.enter_context(nc.sbuf_tensor("qm2", [128, 16], F32))
    cm2 = ackctx.enter_context(nc.sbuf_tensor("cm2", [128, 16], F32))
    sq16 = ackctx.enter_context(nc.sbuf_tensor("sq16", [128, 16], F32))
    negbf = ackctx.enter_context(nc.sbuf_tensor("negbf", [128, 16], BF16))
    negc = ackctx.enter_context(nc.sbuf_tensor("negc", [128, 16], F32))
    recb = ackctx.enter_context(nc.sbuf_tensor("recb", [128, 1024], BF16))
    oP = ackctx.enter_context(nc.sbuf_tensor("oP", [128, 8 * 512], BF16))
    otmp = ackctx.enter_context(nc.sbuf_tensor("otmp", [128, 512], BF16))
    wsh = ackctx.enter_context(nc.sbuf_tensor("wsh", [128, 8 * 512], BF16))
    xpr = ackctx.enter_context(nc.sbuf_tensor("xpr", [128, 2 * 512], F32))

    kn_mm = [0, 0]
    kn_rd = 0
    for p8 in range(8):
        for jj in range(8):
            sel = (p8 * 8 + jj) % 2
            colb = p8 * N + jj * 512
            P.wait("scalar", "sM", kn_mm[sel])
            ssq = P.do("scalar", lambda e, colb=colb, sel=sel: e.activation(
                out=junks[:, sel * 512:(sel + 1) * 512],
                in_=kT[:, colb:colb + 512], func=AF.Square, bias=0.0, scale=1.0),
                inc="sS")
            P.wait("tensor", "sS", ssq)
            P.wait("tensor", "sV", kn_rd)
            P.do("tensor", lambda e, sel=sel: e.matmul(
                psPV0[0:1, 0:512], onescol[0:64, 0:1],
                junks[0:64, sel * 512:(sel + 1) * 512], start=True, stop=True))
            mmv = P.do("tensor", lambda e, sel=sel: e.matmul(
                psPV1[0:1, 0:512], onescol[64:128, 0:1],
                junks[64:128, sel * 512:(sel + 1) * 512], start=True, stop=True,
                tile_position=(64, 0)), inc="sM")
            kn_mm[sel] = mmv
            P.wait("vector", "sM", mmv)
            P.do("vector", lambda e, p8=p8, jj=jj: e.tensor_reduce(
                km2b[0:1, p8 * 16 + jj:p8 * 16 + jj + 1], psPV0[0:1, 0:512],
                axis=AX.X, op=ALU.max))
            kn_rd = P.do("vector", lambda e, p8=p8, jj=jj: e.tensor_reduce(
                km2b[0:1, p8 * 16 + 8 + jj:p8 * 16 + 8 + jj + 1], psPV1[0:1, 0:512],
                axis=AX.X, op=ALU.max), inc="sV")
    vgate()
    for h in range(16):
        p8, hh = h // 2, h % 2
        P.do("vector", lambda e, h=h, p8=p8, hh=hh: e.tensor_reduce(
            kmax2[0:1, h:h + 1], km2b[0:1, p8 * 16 + hh * 8:p8 * 16 + hh * 8 + 8],
            axis=AX.X, op=ALU.max), inc="sV" if h == 15 else None)
    vgate()

    nchunks = 1 if stop_after == "C1" else NQC
    xq_free = [0, 0]
    wq_free = 0
    st_w = [None, None]
    odma = [None]
    odma_list = []
    xnT_ev = [0, 0, 0, 0]
    for c in range(nchunks):
        P.wait("tensor", "sS", P.mark("sS"))
        xnT_guard = P.mark("sM")
        for eh in range(2):
            P.wait("sync", "sM", wq_free)
            lws = []
            for dc in range(8):
                lws.append(dma("sync", wsh[:, dc * 512:(dc + 1) * 512],
                               wqs_scr.ap()[dc * 128:(dc + 1) * 128,
                                            eh * 512:(eh + 1) * 512]))
            P.waitt("tensor", lws)
            for ti in range(4):
                t = c * 4 + ti
                if eh == 0:
                    P.wait("sync", "sV", xq_free[0])
                    xw = dma("sync", xq[:], x_h.ap()[t * 128:(t + 1) * 128, :], "sX")
                    P.wait("vector", "sX", xw)
                    xq_free[0] = P.do("vector", lambda e, t=t: e.tensor_scalar(
                        xn_t[:], xq[:], mu0[:, t:t + 1], R0[:, t:t + 1],
                        op0=ALU.subtract, op1=ALU.mult), inc="sV")
                    P.wait("tensor", "sV", xq_free[0])
                    for d8 in range(8):
                        mmv = P.do("tensor", lambda e, d8=d8: e.transpose(
                            psT[:, d8 * 128:(d8 + 1) * 128],
                            xn_t[:, d8 * 128:(d8 + 1) * 128], ident[:]),
                            inc="sM" if d8 == 7 else None)
                    P.wait("vector", "sM", mmv)
                    P.wait("vector", "sM", xnT_guard)
                    vv = P.do("vector", lambda e, ti=ti: e.tensor_copy(
                        xnT4[:, ti * 1024:(ti + 1) * 1024], psT[:]), inc="sV")
                    xnT_ev[ti] = vv
                    P.wait("tensor", "sV", vv)
                else:
                    P.wait("tensor", "sV", xnT_ev[ti])
                for dc in range(8):
                    mmv = P.do("tensor", lambda e, dc=dc, ti=ti: e.matmul(
                        psM[:, 0:512],
                        xnT4[:, ti * 1024 + dc * 128: ti * 1024 + (dc + 1) * 128],
                        wsh[:, dc * 512:(dc + 1) * 512],
                        start=(dc == 0), stop=(dc == 7)),
                        inc="sM" if dc == 7 else None)
                wq_free = mmv
                P.wait("vector", "sM", mmv)
                rope_emit(qtile[:], psM, t, 8, 512)
                P.wait("tensor", "sV", P.mark("sV"))
                for s4 in range(4):
                    mmv = P.do("tensor", lambda e, s4=s4: e.transpose(
                        psT[:, s4 * 128:(s4 + 1) * 128],
                        qtile[:, s4 * 128:(s4 + 1) * 128], ident[:]),
                        inc="sM" if s4 == 3 else None)
                P.wait("vector", "sM", mmv)
                for s4 in range(4):
                    p = eh * 4 + s4
                    qT_ev = P.do("vector", lambda e, p=p, ti=ti, s4=s4: e.tensor_copy(
                        qTc[:, p * 512 + ti * 128: p * 512 + (ti + 1) * 128],
                        psT[:, s4 * 128:(s4 + 1) * 128]),
                        inc="sV" if s4 == 3 else None)
                P.wait("tensor", "sV", qT_ev)
        qn_mm = [0, 0]
        qn_rd = 0
        for p in range(8):
            mmv = P.do("tensor", lambda e, p=p: e.matmul(
                psM[:, 0:512], wb2[:, p * 128:(p + 1) * 128],
                qTc[:, p * 512:(p + 1) * 512], start=True, stop=True), inc="sM")
            P.wait("vector", "sM", mmv)
            vv = P.do("vector", lambda e, p=p: e.tensor_copy(
                qTc[:, p * 512:(p + 1) * 512], psM[:, 0:512]), inc="sV")
            P.wait("tensor", "sV", vv)
            sel = p % 2
            P.wait("scalar", "sM", qn_mm[sel])
            P.wait("scalar", "sV", vv)
            ssq = P.do("scalar", lambda e, p=p, sel=sel: e.activation(
                out=junks[:, sel * 512:(sel + 1) * 512],
                in_=qTc[:, p * 512:(p + 1) * 512], func=AF.Square, bias=0.0,
                scale=1.0), inc="sS")
            P.wait("tensor", "sS", ssq)
            P.wait("tensor", "sV", qn_rd)
            P.do("tensor", lambda e, sel=sel: e.matmul(
                psPV0[0:1, 0:512], onescol[0:64, 0:1],
                junks[0:64, sel * 512:(sel + 1) * 512], start=True, stop=True))
            mmq = P.do("tensor", lambda e, sel=sel: e.matmul(
                psPV1[0:1, 0:512], onescol[64:128, 0:1],
                junks[64:128, sel * 512:(sel + 1) * 512], start=True, stop=True,
                tile_position=(64, 0)), inc="sM")
            qn_mm[sel] = mmq
            P.wait("vector", "sM", mmq)
            P.do("vector", lambda e, p=p: e.tensor_reduce(
                qm2[0:1, 2 * p:2 * p + 1], psPV0[0:1, 0:512], axis=AX.X, op=ALU.max))
            qn_rd = P.do("vector", lambda e, p=p: e.tensor_reduce(
                qm2[0:1, 2 * p + 1:2 * p + 2], psPV1[0:1, 0:512], axis=AX.X,
                op=ALU.max), inc="sV")
        vgate()
        P.do("vector", lambda e: e.tensor_tensor(
            cm2[0:1, 0:16], qm2[0:1, 0:16], kmax2[0:1, 0:16], ALU.mult), inc="sV")
        vgate()
        vv = P.do("vector", lambda e: e.tensor_reduce(
            sq16[0:1, 0:8], AP(cm2, 0, [[16, 1], [2, 8], [1, 2]]), axis=AX.X,
            op=ALU.max), inc="sV")
        P.wait("scalar", "sV", vv)
        ssv = P.do("scalar", lambda e: e.activation(
            out=cm2[0:1, 8:16], in_=sq16[0:1, 0:8], func=AF.Sqrt, bias=0.0,
            scale=1.0), inc="sS")
        P.wait("vector", "sS", ssv)
        vv = P.do("vector", LP(lambda e: e.tensor_scalar(
            negbf[0:1, 0:8], cm2[0:1, 8:16], -1.0, None, op0=ALU.mult)), inc="sV")
        P.wait("tensor", "sV", vv)
        mmv = P.do("tensor", lambda e: e.matmul(
            psM[:, 0:8], onescol[0:1, 0:1].to_broadcast([1, 128]),
            negbf[0:1, 0:8], start=True, stop=True), inc="sM")
        P.wait("vector", "sM", mmv)
        negc_v = P.do("vector", lambda e: e.tensor_copy(negc[:, 0:8], psM[:, 0:8]),
                      inc="sV")

        P.wait("gpsimd", "sM", wq_free)
        wo_lw0 = []
        for cc in range(8):
            wo_lw0.append(dma("gpsimd", wsh[:, cc * 512:(cc + 1) * 512],
                              wo_h.ap()[cc * 128:(cc + 1) * 128, 0:512]))

        stat_slots = [(psM, 0)]
        slot_cons = [("sV", negc_v)]
        st_cons = [0, 0]
        ring = [0]

        def next_slot():
            s = ring[0] % len(stat_slots)
            ring[0] += 1
            return s

        def waitc(eng, tok):
            if tok is not None:
                P.wait(eng, tok[0], tok[1])

        def emit_stpv(p, negc_v):
            st_slots = [ps01, ps23]
            pt_cons = [0] * 4
            ovv = P.mark("sV")
            P.wait("scalar", "sV", negc_v)
            first_pv = [True]
            pv_q = []
            mm2l = [0]

            def emit_pv(j, pslot, expv):
                P.wait("tensor", "sS", expv)
                if first_pv[0]:
                    P.wait("tensor", "sV", ovv)
                    first_pv[0] = False
                h0 = 2 * p
                h1 = 2 * p + 1
                P.do("tensor", lambda e, j=j, pslot=pslot, h0=h0: e.matmul(
                    psPV0[0:65, :],
                    vaug[:, j * 1040 + h0 * 65: j * 1040 + h0 * 65 + 65],
                    PT[:, pslot * 1024:pslot * 1024 + 512],
                    start=(j == 0), stop=(j == 31)))
                mm2l[0] = P.do("tensor", lambda e, j=j, pslot=pslot, h1=h1:
                               e.matmul(
                                   psPV1[0:65, :],
                                   vaug[:, j * 1040 + h1 * 65:
                                        j * 1040 + h1 * 65 + 65],
                                   PT[:, pslot * 1024 + 512:(pslot + 1) * 1024],
                                   start=(j == 0), stop=(j == 31)), inc="sM")
                pt_cons[pslot] = mm2l[0]

            for j in range(32):
                ss = (j + p) % 2
                spt = st_slots[ss]
                P.wait("tensor", "sS", st_cons[ss])
                P.do("tensor", lambda e, p=p, j=j, spt=spt: e.matmul(
                    spt[:, 0:512],
                    kT[0:64, p * N + j * 128: p * N + (j + 1) * 128],
                    qTc[0:64, p * 512:(p + 1) * 512],
                    start=True, stop=True))
                mmv = P.do("tensor", lambda e, p=p, j=j, spt=spt: e.matmul(
                    spt[:, 512:1024],
                    kT[64:128, p * N + j * 128: p * N + (j + 1) * 128],
                    qTc[64:128, p * 512:(p + 1) * 512],
                    start=True, stop=True, tile_position=(64, 0)), inc="sM")
                pslot = j % 4
                P.wait("scalar", "sM", mmv)
                P.wait("scalar", "sM", pt_cons[pslot])
                st_cons[ss] = P.do("scalar", lambda e, spt=spt, pslot=pslot, p=p:
                                   e.activation(
                                       out=PT[:, pslot * 1024:(pslot + 1) * 1024],
                                       in_=spt[:, 0:1024], func=AF.Exp,
                                       bias=negc[:, p:p + 1], scale=1.0),
                                   inc="sS")
                pv_q.append((j, pslot, st_cons[ss]))
                if len(pv_q) > 2:
                    emit_pv(*pv_q.pop(0))
            while pv_q:
                emit_pv(*pv_q.pop(0))
            mmv = mm2l[0]
            P.wait("scalar", "sM", mmv)
            P.wait("scalar", "sS", P.mark("sS"))
            if p == 0:
                P.waitt("scalar", [tk for tk in st_w if tk])
            P.do("scalar", lambda e: e.activation(
                out=xpr[64:65, 0:512], in_=psPV0[64:65, :], func=AF.Ln,
                bias=0.0, scale=1.0))
            sl1 = P.do("scalar", lambda e: e.activation(
                out=xpr[64:65, 512:1024], in_=psPV1[64:65, :], func=AF.Ln,
                bias=0.0, scale=1.0), inc="sS")
            P.wait("tensor", "sS", sl1)
            waitc("tensor", slot_cons[0])
            P.do("tensor", lambda e: e.matmul(
                psM[0:64, 0:512], onescolf[64:65, 0:1].to_broadcast([1, 64]),
                xpr[64:65, 0:512], start=True, stop=True, tile_position=(64, 0)))
            tb = p % 2
            tpt = st_slots[tb]
            P.wait("tensor", "sS", st_cons[tb])
            mmb = P.do("tensor", lambda e, tpt=tpt: e.matmul(
                tpt[0:64, 0:512], onescolf[64:65, 0:1].to_broadcast([1, 64]),
                xpr[64:65, 512:1024], start=True, stop=True,
                tile_position=(64, 0)), inc="sM")
            P.wait("scalar", "sM", mmb)
            scv0 = P.do("scalar", lambda e: e.activation(
                out=recb[0:64, 0:512], in_=psM[0:64, :], func=AF.Exp, bias=0.0,
                scale=-1.0), inc="sS")
            scv = P.do("scalar", lambda e, tpt=tpt: e.activation(
                out=recb[0:64, 512:1024], in_=tpt[0:64, 0:512], func=AF.Exp,
                bias=0.0, scale=-1.0), inc="sS")
            slot_cons[0] = ("sS", scv0)
            st_cons[tb] = scv
            P.wait("vector", "sS", scv)
            P.waitt("vector", odma[0])
            P.do("vector", lambda e, p=p: e.tensor_tensor(
                oP[0:64, p * 512:(p + 1) * 512], psPV0[0:64, :], recb[0:64, 0:512],
                ALU.mult))
            ovv = P.do("vector", lambda e, p=p: e.tensor_tensor(
                otmp[0:64, :], psPV1[0:64, :], recb[0:64, 512:1024], ALU.mult),
                inc="sV")
            P.wait("gpsimd", "sV", ovv)
            odma[0] = dma("gpsimd", oP[64:128, p * 512:(p + 1) * 512], otmp[0:64, :])
            odma_list.append(odma[0])
            P.wait("tensor", "sV", ovv)
            return P.mark("sS")

        odma_list.clear()
        for p in range(8):
            emit_stpv(p, negc_v)
        mmv = P.mark("sM")

        P.waitt("tensor", list(odma_list))
        for eh in range(2):
            if eh == 1:
                P.wait("gpsimd", "sM", mmv)
                wo_lw1 = []
                for cc in range(8):
                    wo_lw1.append(dma("gpsimd", wsh[:, cc * 512:(cc + 1) * 512],
                                      wo_h.ap()[cc * 128:(cc + 1) * 128, 512:1024]))
                P.waitt("tensor", wo_lw1)
            else:
                P.waitt("tensor", wo_lw0)
            for cc in range(8):
                for ti in range(4):
                    pb = ps01 if ti < 2 else ps23
                    mmv = P.do("tensor", lambda e, cc=cc, ti=ti, pb=pb: e.matmul(
                        pb[:, (ti % 2) * 512:((ti % 2) + 1) * 512],
                        oP[:, cc * 512 + ti * 128: cc * 512 + (ti + 1) * 128],
                        wsh[:, cc * 512:(cc + 1) * 512],
                        start=(cc == 0), stop=(cc == 7)),
                        inc="sM" if (cc == 7 and ti == 3) else None)
            P.wait("vector", "sM", mmv)
            for ti in range(4):
                pb = ps01 if ti < 2 else ps23
                slw = ti % 2
                P.waitt("vector", st_w[slw])
                vv = P.do("vector", lambda e, ti=ti, pb=pb, slw=slw: e.tensor_copy(
                    xpr[:, slw * 512:(slw + 1) * 512],
                    pb[:, (ti % 2) * 512:((ti % 2) + 1) * 512]), inc="sV")
                P.wait("scalar", "sV", vv)
                t = c * 4 + ti
                st_w[slw] = dma(
                    "scalar", xa_scr.ap()[t * 128:(t + 1) * 128,
                                          eh * 512:(eh + 1) * 512],
                    xpr[:, slw * 512:(slw + 1) * 512])
            P.wait("tensor", "sV", P.mark("sV"))
        wq_free = P.mark("sM")

    if stop_after == "C1":
        dump(qTc[:, 0:512])
        dump(negc[:])
        dump(oP[:, 0:512])
        dump(oP[:, 3 * 512:4 * 512])
        dump(PT[:, 0:1024])
        dump(recb[:])
        return finalize()

    phase_barrier()
    ackctx.close()

    wpctx = ctx.enter_context(ExitStack())
    wfsl = wpctx.enter_context(nc.sbuf_tensor("wfsl", [128, 4096], F32))
    tsl = wpctx.enter_context(nc.sbuf_tensor("tsl", [128, 4096], BF16))
    tT = wpctx.enter_context(nc.sbuf_tensor("tT", [128, 4096], BF16))
    t8 = wpctx.enter_context(nc.sbuf_tensor("t8", [128, 1024], FP8))
    gtb = wpctx.enter_context(nc.sbuf_tensor("gtb", [128, 4096], BF16))

    def wmean(w_hh, nslab, slabw, outbc, outm):
        P.wait("sync", "sG", P.mark("sG"))
        P.do("vector", lambda e: e.memset(wpart[:], 0.0))
        pv = 0
        for s in range(nslab):
            P.wait("sync", "sV", pv)
            lw = dma("sync", wfsl[:, 0:slabw], w_hh.ap()[s * 128:(s + 1) * 128, :], "sX")
            P.wait("vector", "sX", lw)
            P.do("vector", lambda e, slabw=slabw: e.reduce_sum(
                wtmp[:, 0:1], wfsl[:, 0:slabw], axis=AX.X, apply_absolute_value=True),
                inc="sV")
            vgate()
            pv = P.do("vector", lambda e: e.tensor_tensor(
                wpart[:], wpart[:], wtmp[:, 0:1], ALU.add), inc="sV")
        vgate()
        vv = P.mark("sV")
        P.wait("tensor", "sV", vv)
        mmv = P.do("tensor", lambda e: e.matmul(
            psM[0:1, 0:1], onescolf[:, 0:1], wpart[:], start=True, stop=True),
            inc="sM")
        P.wait("vector", "sM", mmv)
        P.do("vector", lambda e: e.tensor_scalar(
            wtmp[0:1, 0:1], psM[0:1, 0:1], 1.0 / (HID * D), 1e-5,
            op0=ALU.mult, op1=ALU.max), inc="sV")
        vgate()
        P.do("vector", lambda e: e.reciprocal(wtmp[0:1, 1:2], wtmp[0:1, 0:1]), inc="sV")
        vgate()
        vv = P.mark("sV")
        P.wait("tensor", "sV", vv)
        mmv = P.do("tensor", lambda e: e.matmul(
            psM[:, 0:2], onescolf[0:1, 0:1].to_broadcast([1, 128]), wtmp[0:1, 0:2],
            start=True, stop=True), inc="sM")
        P.wait("vector", "sM", mmv)
        P.do("vector", lambda e: e.tensor_copy(outm[:], psM[:, 0:1]), inc="sV")
        vv = P.do("vector", lambda e: e.tensor_copy(outbc[:], psM[:, 1:2]), inc="sV")
        P.wait("gpsimd", "sV", vv)
        P.wait("tensor", "sV", vv)
        P.wait("sync", "sV", vv)
        vgate()

    wmean(w1_h, 32, 1024, s1bc, m1bc)
    ev_v = 0
    st_w1 = None
    tp_m = 0
    for s in range(32):
        P.wait("sync", "sG", P.mark("sG"))
        lw = dma("sync", wfsl[:, 0:1024], w1_h.ap()[s * 128:(s + 1) * 128, :], "sX")
        P.wait("vector", "sX", lw)
        P.wait("vector", "sM", tp_m)
        P.do("vector", lambda e: e.tensor_scalar(
            gtb[:, 0:1024], wfsl[:, 0:1024], s1bc[:], 0.5, op0=ALU.mult,
            op1=ALU.is_gt), inc="sG")
        P.do("vector", lambda e: e.tensor_scalar(
            tsl[:, 0:1024], wfsl[:, 0:1024], s1bc[:], -0.5, op0=ALU.mult,
            op1=ALU.is_lt), inc="sG")
        P.wait("vector", "sG", P.mark("sG"))
        gv = P.do("vector", lambda e: e.tensor_tensor(
            tsl[:, 0:1024], gtb[:, 0:1024], tsl[:, 0:1024], ALU.subtract), inc="sG")
        P.wait("tensor", "sG", gv)
        P.wait("tensor", "sV", ev_v)
        for dc in range(8):
            tp_m = P.do("tensor", lambda e, dc=dc: e.transpose(
                psT[:, dc * 128:(dc + 1) * 128],
                tsl[:, dc * 128:(dc + 1) * 128], ident[:]),
                inc="sM" if dc == 7 else None)
        P.wait("vector", "sM", tp_m)
        P.waitt("vector", st_w1)
        ev_v = P.do("vector", lambda e: e.tensor_copy(t8[:, 0:1024], psT[:]), inc="sV")
        P.wait("scalar", "sV", ev_v)
        st_w1 = dma("scalar", w1t_scr.ap()[s, :, :], t8[:, 0:1024])

    wmean(w2_h, 8, 4096, s2bc, m2c)
    P.do("vector", lambda e: e.tensor_scalar(
        gmc[:], g2c[:], m2c[:], None, op0=ALU.mult), inc="sV")
    vgate()
    vv = P.do("vector", lambda e: e.tensor_copy(gmcb[:], gmc[:]), inc="sV")
    P.wait("tensor", "sV", vv)
    ev_v = 0
    st_w2 = None
    tp_m = 0
    sm_m = 0
    for s in range(8):
        P.wait("sync", "sG", P.mark("sG"))
        lw = dma("sync", wfsl[:], w2_h.ap()[s * 128:(s + 1) * 128, :], "sX")
        P.wait("vector", "sX", lw)
        P.wait("vector", "sM", tp_m)
        P.do("vector", lambda e: e.tensor_scalar(
            gtb[:], wfsl[:], s2bc[:], 0.5, op0=ALU.mult, op1=ALU.is_gt), inc="sG")
        P.do("vector", lambda e: e.tensor_scalar(
            tsl[:], wfsl[:], s2bc[:], -0.5, op0=ALU.mult, op1=ALU.is_lt), inc="sG")
        P.wait("vector", "sG", P.mark("sG"))
        gv = P.do("vector", lambda e: e.tensor_tensor(
            tsl[:], gtb[:], tsl[:], ALU.subtract), inc="sG")
        P.wait("tensor", "sG", gv)
        P.waitt("tensor", st_w2)
        for hg in range(4):
            P.wait("tensor", "sV", ev_v)
            for k8 in range(8):
                hc = hg * 8 + k8
                tp_m = P.do("tensor", lambda e, hc=hc, k8=k8: e.transpose(
                    psT[:, k8 * 128:(k8 + 1) * 128],
                    tsl[:, hc * 128:(hc + 1) * 128], ident[:]),
                    inc="sM" if k8 == 7 else None)
            P.wait("vector", "sM", tp_m)
            ev_v = P.do("vector", lambda e, hg=hg: e.tensor_copy(
                tT[:, hg * 1024:(hg + 1) * 1024], psT[:]), inc="sV")
        P.wait("tensor", "sV", ev_v)
        for hc in range(32):
            P.do("tensor", lambda e, hc=hc: e.matmul(
                psPV0[:, 0:1], tT[:, hc * 128:(hc + 1) * 128], gmcb[:, hc:hc + 1],
                start=(hc == 0), stop=(hc == 31)))
            P.do("tensor", lambda e, hc=hc: e.matmul(
                psPV1[:, 0:1], tT[:, hc * 128:(hc + 1) * 128], b2lnc[:, hc:hc + 1],
                start=(hc == 0), stop=False))
            sm_m = P.do("tensor", lambda e, hc=hc: e.matmul(
                psPV1[:, 0:1], tT[:, hc * 128:(hc + 1) * 128],
                b2lnc[:, 32 + hc: 33 + hc],
                start=False, stop=(hc == 31)), inc="sM" if hc == 31 else None)
        P.wait("vector", "sM", sm_m)
        P.do("vector", lambda e, s=s: e.tensor_copy(wsC[:, s:s + 1], psPV0[:, 0:1]),
             inc="sV")
        P.do("vector", lambda e, s=s: e.tensor_copy(biasC[:, s:s + 1], psPV1[:, 0:1]),
             inc="sV")
        for hc in range(32):
            ev_v = P.do("vector", lambda e, hc=hc: e.tensor_scalar(
                tT[:, hc * 128:(hc + 1) * 128], tT[:, hc * 128:(hc + 1) * 128],
                gmc[:, hc:hc + 1], None, op0=ALU.mult),
                inc="sV" if hc == 31 else None)
        P.wait("scalar", "sV", ev_v)
        st_w2 = dma("scalar", w2t_scr.ap()[s, :, :], tT[:])
    vgate()
    P.do("vector", lambda e: e.tensor_scalar(
        biasC[:], biasC[:], m2c[:], None, op0=ALU.mult), inc="sV")
    vgate()
    P.do("vector", lambda e: e.tensor_tensor(biasC[:], biasC[:], b2c[:], ALU.add),
         inc="sV")
    vgate()
    P.do("vector", lambda e: e.tensor_copy(junkb[:, 0:8], wsC[:]))
    vv = P.do("vector", lambda e: e.tensor_copy(junkb[:, 8:16], biasC[:]), inc="sV")
    P.wait("tensor", "sV", vv)
    for s in range(8):
        P.do("tensor", lambda e, s=s: e.transpose(
            psT[0:1, 0:128], junkb[:, s:s + 1], ident[:]))
        mmv = P.do("tensor", lambda e, s=s: e.transpose(
            psT[0:1, 128:256], junkb[:, 8 + s:9 + s], ident[:]), inc="sM")
        P.wait("vector", "sM", mmv)
        P.do("vector", lambda e, s=s: e.tensor_copy(
            wsumr[0:1, s * 128:(s + 1) * 128], psT[0:1, 0:128]))
        vv = P.do("vector", lambda e, s=s: e.tensor_copy(
            biasr[0:1, s * 128:(s + 1) * 128], psT[0:1, 128:256]), inc="sV")
        P.wait("tensor", "sV", vv)
    phase_barrier()
    wpctx.close()
    attctx.close()

    if stop_after == "WPREP":
        dump(wsumr[0:1, :], rows=1)
        dump(biasr[0:1, :], rows=1)
        dump(m1bc[:, 0:1])
        dump(m2c[:, 0:1])
        return finalize()

    fctx = ctx.enter_context(ExitStack())
    xpc = fctx.enter_context(nc.sbuf_tensor("xpc", [128, 8 * 1024], F32))
    xn2 = fctx.enter_context(nc.sbuf_tensor("xn2", [128, 1024], BF16))
    xn2gT = fctx.enter_context(nc.sbuf_tensor("xn2gT", [128, 8 * 1024], BF16))
    uP = fctx.enter_context(nc.sbuf_tensor("uP", [128, 32 * 1024], BF16))
    u2r = fctx.enter_context(nc.sbuf_tensor("u2r", [128, 2 * 1024], BF16))
    w1r = fctx.enter_context(nc.sbuf_tensor("w1r", [128, 2 * 1024], FP8))
    w2r = fctx.enter_context(nc.sbuf_tensor("w2r", [128, 2 * 4096], BF16))
    rows_ = fctx.enter_context(nc.sbuf_tensor("rows_", [128, 3 * 1024], F32))
    rowsb = fctx.enter_context(nc.sbuf_tensor("rowsb", [128, 2 * 1024], BF16))
    fRc = fctx.enter_context(nc.sbuf_tensor("fRc", [128, 8], F32))
    w2o = fctx.enter_context(nc.sbuf_tensor("w2o", [128, 1024], BF16))
    ybuf = fctx.enter_context(nc.sbuf_tensor("ybuf", [128, 8 * 1024], F32))
    fmu = fctx.enter_context(nc.sbuf_tensor("fmu", [128, 8], F32))
    fnmu = fctx.enter_context(nc.sbuf_tensor("fnmu", [128, 8], F32))
    fssq = fctx.enter_context(nc.sbuf_tensor("fssq", [128, 8], F32))
    fR = fctx.enter_context(nc.sbuf_tensor("fR", [128, 8], F32))
    RA, RB, RC = 0, 1024, 2048

    yst_w = None
    ych_v = 0
    for f in range(NFC):
        P.waitt("sync", yst_w)
        P.wait("sync", "sV", ych_v)
        xw = []
        for ti in range(8):
            t = f * 8 + ti
            t1 = dma("sync", xpc[:, ti * 1024:(ti + 1) * 1024],
                     xa_scr.ap()[t * 128:(t + 1) * 128, :])
            t2 = dma("sync", ybuf[:, ti * 1024:(ti + 1) * 1024],
                     x_h.ap()[t * 128:(t + 1) * 128, :])
            xw.append([t1, t2])
        for ti in range(8):
            P.waitt("vector", xw[ti])
            P.do("vector", lambda e, ti=ti: e.tensor_tensor(
                xpc[:, ti * 1024:(ti + 1) * 1024], xpc[:, ti * 1024:(ti + 1) * 1024],
                ybuf[:, ti * 1024:(ti + 1) * 1024], ALU.add), inc="sV")
        vgate()
        for ti in range(8):
            ln_stats_tile(ti, xpc[:, ti * 1024:(ti + 1) * 1024], fmu, fnmu, fssq, 0)
        ln_R_batch(0, 8, fssq, fR, D)
        for ti in range(8):
            vv = P.do("vector", lambda e, ti=ti: e.tensor_scalar(
                xn2[:], xpc[:, ti * 1024:(ti + 1) * 1024],
                fmu[:, ti:ti + 1], fR[:, ti:ti + 1],
                op0=ALU.subtract, op1=ALU.mult), inc="sV")
            P.wait("tensor", "sV", vv)
            for d8 in range(8):
                mmv = P.do("tensor", lambda e, d8=d8: e.transpose(
                    psT[:, d8 * 128:(d8 + 1) * 128],
                    xn2[:, d8 * 128:(d8 + 1) * 128], ident[:]),
                    inc="sM" if d8 == 7 else None)
            P.wait("vector", "sM", mmv)
            for d8 in range(8):
                vv = P.do("vector", lambda e, d8=d8, ti=ti: e.tensor_scalar(
                    xn2gT[:, d8 * 1024 + ti * 128: d8 * 1024 + (ti + 1) * 128],
                    psT[:, d8 * 128:(d8 + 1) * 128],
                    g1c[:, d8:d8 + 1], None, op0=ALU.mult),
                    inc="sV" if d8 == 7 else None)
            P.wait("tensor", "sV", vv)

        gelu_s = [0, 0]
        w1mm = [0, 0]
        for hc in range(32):
            slot = hc % 2
            P.wait("sync", "sM", w1mm[slot])
            w1lw = dma("sync", w1r[:, slot * 1024:(slot + 1) * 1024],
                       w1t_scr.ap()[hc, :, :], "sX")
            pb = ps01 if slot == 0 else ps23
            P.wait("tensor", "sS", gelu_s[slot])
            P.wait("tensor", "sX", w1lw)
            for th in range(2):
                for dc in range(8):
                    mmv = P.do("tensor", lambda e, th=th, dc=dc, pb=pb, slot=slot:
                               e.matmul(
                                   pb[:, th * 512:(th + 1) * 512],
                                   w1r[:, slot * 1024 + dc * 128:
                                       slot * 1024 + (dc + 1) * 128],
                                   xn2gT[:, dc * 1024 + th * 512:
                                         dc * 1024 + (th + 1) * 512],
                                   start=(dc == 0), stop=(dc == 7)),
                               inc="sM" if (th == 1 and dc == 7) else None)
            if f == 0:
                for half16 in range(2):
                    for dc in range(8):
                        bmv = P.do("tensor", lambda e, dc=dc, slot=slot, half16=half16:
                                   e.matmul(
                                       psPV0[:, 0:1],
                                       w1r[:, slot * 1024 + dc * 128:
                                           slot * 1024 + (dc + 1) * 128],
                                       b1lnc[:, half16 * 8 + dc: half16 * 8 + dc + 1],
                                       start=(half16 == 0 and dc == 0),
                                       stop=(half16 == 1 and dc == 7)),
                                   inc="sM" if (half16 == 1 and dc == 7) else None)
                w1mm[slot] = bmv
                P.wait("vector", "sM", bmv)
                P.do("vector", lambda e, hc=hc: e.tensor_scalar(
                    b1tot[:, hc:hc + 1], psPV0[:, 0:1], m1bc[:], None, op0=ALU.mult),
                    inc="sV")
                vgate()
                vv = P.do("vector", lambda e, hc=hc: e.tensor_tensor(
                    b1tot[:, hc:hc + 1], b1tot[:, hc:hc + 1], b1c[:, hc:hc + 1],
                    ALU.add), inc="sV")
                P.wait("scalar", "sV", vv)
                P.wait("tensor", "sV", vv)
            else:
                w1mm[slot] = mmv
            P.wait("scalar", "sM", mmv)
            gelu_s[slot] = P.do("scalar", lambda e, hc=hc, pb=pb: e.activation(
                out=uP[:, hc * 1024:(hc + 1) * 1024], in_=pb[:], func=AF.Gelu,
                bias=b1tot[:, hc:hc + 1], scale=m1bc[:]), inc="sS")
        P.wait("tensor", "sS", P.mark("sS"))

        for hc in range(32):
            for th in range(2):
                grp = psM if th == 0 else psPV0
                mmv = P.do("tensor", lambda e, hc=hc, th=th, grp=grp: e.matmul(
                    grp[0:1, :], onescol[:, 0:1],
                    uP[:, hc * 1024 + th * 512: hc * 1024 + (th + 1) * 512],
                    start=(hc == 0), stop=(hc == 31)),
                    inc="sM" if (hc == 31 and th == 1) else None)
        P.wait("vector", "sM", mmv)
        P.do("vector", lambda e: e.tensor_scalar(
            rows_[0:1, RA:RA + 512], psM[0:1, :], 1.0 / HID, None, op0=ALU.mult))
        vv = P.do("vector", lambda e: e.tensor_scalar(
            rows_[0:1, RA + 512:RA + 1024], psPV0[0:1, :], 1.0 / HID, None,
            op0=ALU.mult), inc="sV")
        P.wait("tensor", "sV", vv)
        u2_m = [0, 0]
        for hc in range(32):
            slot = hc % 2
            P.wait("vector", "sM", u2_m[slot])
            uvv = P.do("vector", lambda e, hc=hc, slot=slot: e.tensor_tensor(
                u2r[:, slot * 1024:(slot + 1) * 1024],
                uP[:, hc * 1024:(hc + 1) * 1024],
                uP[:, hc * 1024:(hc + 1) * 1024], ALU.mult), inc="sV")
            P.wait("tensor", "sV", uvv)
            for th in range(2):
                grp = psM if th == 0 else psPV0
                u2_m[slot] = P.do("tensor", lambda e, hc=hc, th=th, grp=grp, slot=slot:
                                  e.matmul(
                                      grp[0:1, :], onescol[:, 0:1],
                                      u2r[:, slot * 1024 + th * 512:
                                          slot * 1024 + (th + 1) * 512],
                                      start=(hc == 0), stop=(hc == 31)),
                                  inc="sM" if th == 1 else None)
        P.wait("vector", "sM", max(u2_m))
        P.do("vector", lambda e: e.tensor_scalar(
            rows_[0:1, RB:RB + 512], psM[0:1, :], 1.0 / HID, None, op0=ALU.mult))
        P.do("vector", lambda e: e.tensor_scalar(
            rows_[0:1, RB + 512:RB + 1024], psPV0[0:1, :], 1.0 / HID, None,
            op0=ALU.mult))
        vgate()
        P.do("vector", lambda e: e.tensor_scalar(
            rowsb[0:1, 0:1024], rows_[0:1, RA:RA + 1024], -1.0, None, op0=ALU.mult))
        P.do("vector", lambda e: e.tensor_tensor(
            rows_[0:1, RC:RC + 1024], rows_[0:1, RA:RA + 1024],
            rows_[0:1, RA:RA + 1024], ALU.mult), inc="sV")
        vgate()
        P.do("vector", lambda e: e.tensor_tensor(
            rows_[0:1, RB:RB + 1024], rows_[0:1, RB:RB + 1024],
            rows_[0:1, RC:RC + 1024], ALU.subtract), inc="sV")
        vgate()
        vv = P.do("vector", lambda e: e.tensor_scalar(
            rows_[0:1, RB:RB + 1024], rows_[0:1, RB:RB + 1024], EPS, None,
            op0=ALU.add), inc="sV")
        P.wait("scalar", "sV", vv)
        ss = P.do("scalar", lambda e: e.activation(
            out=rows_[0:1, RA:RA + 1024], in_=rows_[0:1, RB:RB + 1024], func=AF.Sqrt,
            bias=0.0, scale=1.0), inc="sS")
        P.wait("vector", "sS", ss)
        P.do("vector", lambda e: e.tensor_copy(
            rowsb[0:1, 1024:2048], rows_[0:1, RA:RA + 1024]))
        vv = P.do("vector", lambda e: e.reciprocal(
            rows_[0:1, RC:RC + 1024], rows_[0:1, RA:RA + 1024]), inc="sV")
        vgate()
        P.wait("tensor", "sV", vv)
        for ti in range(8):
            mmv = P.do("tensor", lambda e, ti=ti: e.transpose(
                psM[:, 0:1], rows_[0:1, RC + ti * 128: RC + (ti + 1) * 128],
                identf[0:1, 0:1]), inc="sM")
            P.wait("vector", "sM", mmv)
            vv = P.do("vector", lambda e, ti=ti: e.tensor_copy(
                fRc[:, ti:ti + 1], psM[:, 0:1]), inc="sV")
            P.wait("tensor", "sV", vv)

        w2mm = [0, 0]
        ev_done = [0, 0]
        for dc in range(8):
            slot = dc % 2
            P.wait("sync", "sM", w2mm[slot])
            w2lw = dma("sync", w2r[:, slot * 4096:(slot + 1) * 4096],
                       w2t_scr.ap()[dc, :, :], "sX")
            pb = ps01 if slot == 0 else ps23
            P.wait("tensor", "sV", ev_done[slot])
            P.wait("tensor", "sX", w2lw)
            for th in range(2):
                for hc in range(32):
                    P.do("tensor", lambda e, th=th, hc=hc, pb=pb, slot=slot: e.matmul(
                        pb[:, th * 512:(th + 1) * 512],
                        w2r[:, slot * 4096 + hc * 128: slot * 4096 + (hc + 1) * 128],
                        uP[:, hc * 1024 + th * 512: hc * 1024 + (th + 1) * 512],
                        start=(hc == 0), stop=False))
                P.do("tensor", lambda e, th=th, dc=dc, pb=pb: e.matmul(
                    pb[:, th * 512:(th + 1) * 512],
                    wsumr[0:1, dc * 128:(dc + 1) * 128],
                    rowsb[0:1, th * 512:(th + 1) * 512],
                    start=False, stop=False))
                mmv = P.do("tensor", lambda e, th=th, dc=dc, pb=pb: e.matmul(
                    pb[:, th * 512:(th + 1) * 512],
                    biasr[0:1, dc * 128:(dc + 1) * 128],
                    rowsb[0:1, 1024 + th * 512: 1024 + (th + 1) * 512],
                    start=False, stop=True), inc="sM" if th == 1 else None)
            w2mm[slot] = mmv
            P.wait("vector", "sM", mmv)
            vv = P.do("vector", lambda e, pb=pb: e.tensor_copy(w2o[:], pb[:]), inc="sV")
            ev_done[slot] = vv
            P.wait("tensor", "sV", vv)
            for tg in range(2):
                for k4 in range(4):
                    ti = tg * 4 + k4
                    mmv = P.do("tensor", lambda e, ti=ti, k4=k4: e.transpose(
                        psT[:, k4 * 128:(k4 + 1) * 128],
                        w2o[:, ti * 128:(ti + 1) * 128], ident[:]),
                        inc="sM" if k4 == 3 else None)
                P.wait("vector", "sM", mmv)
                for k4 in range(4):
                    ti = tg * 4 + k4
                    ych_v = P.do("vector", lambda e, ti=ti, k4=k4, dc=dc:
                                 e.scalar_tensor_tensor(
                                     ybuf[:, ti * 1024 + dc * 128:
                                          ti * 1024 + (dc + 1) * 128],
                                     psT[:, k4 * 128:(k4 + 1) * 128],
                                     fRc[:, ti:ti + 1],
                                     xpc[:, ti * 1024 + dc * 128:
                                         ti * 1024 + (dc + 1) * 128],
                                     op0=ALU.mult, op1=ALU.add),
                                 inc="sV" if k4 == 3 else None)
                P.wait("tensor", "sV", ych_v)
        P.wait("scalar", "sV", ych_v)
        yst_w = []
        for ti in range(8):
            t = f * 8 + ti
            yst_w.append(dma("scalar", y_h.ap()[t * 128:(t + 1) * 128, :],
                             ybuf[:, ti * 1024:(ti + 1) * 1024]))

    return finalize()


def kernel(L0_attn_x, L0_attn_gamma, L0_attn_Wq, L0_attn_Wkv, L0_attn_W_bilinear,
           L0_attn_Wo, L0_ffn_ln1_g, L0_ffn_ln1_b, L0_ffn_w1, L0_ffn_b1,
           L0_ffn_ln2_g, L0_ffn_ln2_b, L0_ffn_w2, L0_ffn_b2):
    tabc, tabs = _host_consts()
    id16 = np.eye(128, dtype=np.float32)
    id32 = np.eye(128, dtype=np.float32)
    wb = np.asarray(L0_attn_W_bilinear, np.float32)
    wb2 = np.zeros((8, 128, 128), np.float32)
    for p in range(8):
        wb2[p, 0:64, 0:64] = wb[2 * p]
        wb2[p, 64:128, 64:128] = wb[2 * p + 1]

    common = {
        "wq": np.ascontiguousarray(np.asarray(L0_attn_Wq, np.float32)),
        "wkv": np.ascontiguousarray(np.asarray(L0_attn_Wkv, np.float32)),
        "wb2": wb2,
        "wo": np.ascontiguousarray(np.asarray(L0_attn_Wo, np.float32)),
        "w1": np.ascontiguousarray(np.asarray(L0_ffn_w1, np.float32)),
        "w2": np.ascontiguousarray(np.asarray(L0_ffn_w2, np.float32)),
        "gam": _cols(L0_attn_gamma),
        "g1": _cols(L0_ffn_ln1_g),
        "b1ln": _cols(L0_ffn_ln1_b),
        "b1": _cols(L0_ffn_b1),
        "g2": _cols(L0_ffn_ln2_g),
        "b2ln": _cols(L0_ffn_ln2_b),
        "b2": _cols(L0_ffn_b2),
        "tabc": tabc,
        "tabs": tabs,
        "id16": id16,
        "id32": id32,
    }
    x = np.asarray(L0_attn_x, np.float32)
    in_maps = []
    for b in range(NCORES):
        m = dict(common)
        m["x"] = np.ascontiguousarray(x[b])
        in_maps.append(m)

    nc, ctx = build_nc()
    try:
        res = run_bass_kernel_spmd(nc, in_maps, core_ids=list(range(NCORES)))
    finally:
        ctx.close()
    return np.stack([r["y"] for r in res.results]).astype(np.float32)


# version 10 (final)
# speedup vs baseline: 1.3568x; 1.0250x over previous
import numpy as np
import ml_dtypes
from contextlib import ExitStack
import concourse.bass as bass
import concourse.mybir as mybir
from concourse.bass_utils import run_bass_kernel_spmd

F32 = mybir.dt.float32
BF16 = mybir.dt.bfloat16
FP8 = mybir.dt.float8e4
AF = mybir.ActivationFunctionType
ALU = mybir.AluOpType
AX = mybir.AxisListType

NCORES = 8
N = 4096
D = 1024
H = 16
DH = 64
HID = 4096
NT = 32
QC = 512
NQC = N // QC
FT = 1024
NFC = N // FT
EPS = 1e-5
ENGINES = ("sync", "tensor", "vector", "scalar", "gpsimd")


def _host_consts():
    theta = 1.0 / (10000.0 ** (np.arange(0, DH, 2, dtype=np.float64) / DH))
    pos = np.arange(N, dtype=np.float64)
    freqs = pos[:, None] * theta[None, :]
    emb = np.concatenate([freqs, freqs], axis=-1)
    cos = np.cos(emb)
    ssin = np.sin(emb)
    ssin[:, 0::2] *= -1.0

    def tl(a):
        return np.ascontiguousarray(
            a.reshape(NT, 128, DH).transpose(1, 0, 2).reshape(128, NT * DH)
        ).astype(np.float32)
    return tl(cos), tl(ssin)


def _cols(v):
    n = np.asarray(v).shape[0]
    return np.ascontiguousarray(np.asarray(v, np.float32).reshape(n // 128, 128).T)


class Ctr:
    __slots__ = ("sem", "n")

    def __init__(self, sem):
        self.sem = sem
        self.n = 0


class DmaRing:

    def __init__(self, nc, ctx, prefix, depth):
        self.ctrs = [Ctr(ctx.enter_context(nc.semaphore(f"{prefix}{i}")))
                     for i in range(depth)]
        self.i = 0

    def next(self):
        c = self.ctrs[self.i % len(self.ctrs)]
        self.i += 1
        prev = c.n
        c.n += 16
        return c, prev


class Prog:
    def __init__(self, nc, ctx):
        self.nc = nc
        self.plan = {e: [] for e in ENGINES}
        self.ctr = {}
        for name in ("sM", "sV", "sS", "sG"):
            self.ctr[name] = Ctr(ctx.enter_context(nc.semaphore(name)))
        self.rings = {
            "sync": DmaRing(nc, ctx, "rX", 10),
            "gpsimd": DmaRing(nc, ctx, "rC", 6),
            "scalar": DmaRing(nc, ctx, "rW", 6),
        }

    def emit(self, eng, fn):
        self.plan[eng].append(fn)

    def wait(self, eng, cname, val=None):
        if isinstance(val, tuple):
            return self.waitt(eng, val)
        c = self.ctr[cname]
        v = c.n if val is None else val
        if v is None or v <= 0:
            return
        assert v <= c.n, f"wait on future {cname}: {v} > {c.n}"
        self.emit(eng, lambda e, s=c.sem, v=v: e.wait_ge(s, v))

    def waitt(self, eng, tok):
        if not tok:
            return
        if isinstance(tok, list):
            for t in tok:
                self.waitt(eng, t)
            return
        c, v = tok
        self.emit(eng, lambda e, s=c.sem, v=v: e.wait_ge(s, v))

    def wait_all(self, eng, q):
        for c in self.rings[q].ctrs:
            if c.n > 0:
                self.emit(eng, lambda e, s=c.sem, v=c.n: e.wait_ge(s, v))

    def dmaq(self, q, dst, src):
        c, prev = self.rings[q].next()
        if prev > 0:
            self.emit(q, lambda e, s=c.sem, v=prev: e.wait_ge(s, v))
        self.emit(q, lambda e, d=dst, s=src, sm=c.sem: e.dma_start(
            out=d, in_=s).then_inc(sm, 16))
        return (c, c.n)

    def do(self, eng, build, inc=None):
        if inc is not None:
            c = self.ctr[inc]
            c.n += 1
            self.emit(eng, lambda e, b=build, s=c.sem: b(e).then_inc(s, 1))
            return c.n
        self.emit(eng, lambda e, b=build: b(e))
        return None

    def mark(self, cname):
        return self.ctr[cname].n


def build_nc(stop_after=None):
    nc = bass.Bass("TRN2", target_bir_lowering=False, debug=False)
    ctx = ExitStack()
    AP = bass.AP

    x_h = nc.dram_tensor("x", [N, D], F32, kind="ExternalInput")
    wq_h = nc.dram_tensor("wq", [D, D], F32, kind="ExternalInput")
    wkv_h = nc.dram_tensor("wkv", [D, 2 * D], F32, kind="ExternalInput")
    wb2_h = nc.dram_tensor("wb2", [8, 128, 128], F32, kind="ExternalInput")
    wo_h = nc.dram_tensor("wo", [D, D], F32, kind="ExternalInput")
    w1_h = nc.dram_tensor("w1", [HID, D], F32, kind="ExternalInput")
    w2_h = nc.dram_tensor("w2", [D, HID], F32, kind="ExternalInput")
    gam_h = nc.dram_tensor("gam", [128, 8], F32, kind="ExternalInput")
    g1_h = nc.dram_tensor("g1", [128, 8], F32, kind="ExternalInput")
    b1ln_h = nc.dram_tensor("b1ln", [128, 8], F32, kind="ExternalInput")
    b1_h = nc.dram_tensor("b1", [128, 32], F32, kind="ExternalInput")
    g2_h = nc.dram_tensor("g2", [128, 32], F32, kind="ExternalInput")
    b2ln_h = nc.dram_tensor("b2ln", [128, 32], F32, kind="ExternalInput")
    b2_h = nc.dram_tensor("b2", [128, 8], F32, kind="ExternalInput")
    tabc_h = nc.dram_tensor("tabc", [128, NT * DH], F32, kind="ExternalInput")
    tabs_h = nc.dram_tensor("tabs", [128, NT * DH], F32, kind="ExternalInput")
    id16_h = nc.dram_tensor("id16", [128, 128], F32, kind="ExternalInput")
    id32_h = nc.dram_tensor("id32", [128, 128], F32, kind="ExternalInput")

    xa_scr = nc.dram_tensor("xa_scr", [N, D], F32, kind="Internal")
    wqs_scr = nc.dram_tensor("wqs_scr", [D, D], BF16, kind="Internal")
    w1t_scr = nc.dram_tensor("w1t_scr", [32, 128, 1024], FP8, kind="Internal")
    w2t_scr = nc.dram_tensor("w2t_scr", [8, 128, 4096], BF16, kind="Internal")

    y_h = nc.dram_tensor("y", [N, D], F32, kind="ExternalOutput")
    dbg_h = None
    if stop_after is not None:
        dbg_h = nc.dram_tensor("dbg", [128, 40960], F32, kind="ExternalOutput")

    ps01 = ctx.enter_context(nc.psum_tensor("ps01", [128, 1024], F32))
    ps23 = ctx.enter_context(nc.psum_tensor("ps23", [128, 1024], F32))
    psPV0 = ctx.enter_context(nc.psum_tensor("psPV0", [128, 512], F32))
    psPV1 = ctx.enter_context(nc.psum_tensor("psPV1", [128, 512], F32))
    psT = ctx.enter_context(nc.psum_tensor("psT", [128, 1024], BF16))
    psM = ctx.enter_context(nc.psum_tensor("psM", [128, 512], F32))

    wb2 = ctx.enter_context(nc.sbuf_tensor("wb2_s", [128, 8 * 128], BF16))
    ident = ctx.enter_context(nc.sbuf_tensor("ident", [128, 128], BF16))
    identf = ctx.enter_context(nc.sbuf_tensor("identf", [128, 128], F32))
    onescol = ctx.enter_context(nc.sbuf_tensor("onescol", [128, 1], BF16))
    onescolf = ctx.enter_context(nc.sbuf_tensor("onescolf", [128, 1], F32))
    gam = ctx.enter_context(nc.sbuf_tensor("gam_s", [128, 8], F32))
    g1c = ctx.enter_context(nc.sbuf_tensor("g1c", [128, 8], F32))
    b1lnc = ctx.enter_context(nc.sbuf_tensor("b1lnc", [128, 16], BF16))
    b1c = ctx.enter_context(nc.sbuf_tensor("b1c", [128, 32], F32))
    g2c = ctx.enter_context(nc.sbuf_tensor("g2c", [128, 32], F32))
    b2lnc = ctx.enter_context(nc.sbuf_tensor("b2lnc", [128, 64], BF16))
    b2c = ctx.enter_context(nc.sbuf_tensor("b2c", [128, 8], F32))
    mu0 = ctx.enter_context(nc.sbuf_tensor("mu0", [128, NT], F32))
    nmu0 = ctx.enter_context(nc.sbuf_tensor("nmu0", [128, NT], F32))
    R0 = ctx.enter_context(nc.sbuf_tensor("R0", [128, NT], F32))
    ssq0 = ctx.enter_context(nc.sbuf_tensor("ssq0", [128, NT], F32))
    lnscr = ctx.enter_context(nc.sbuf_tensor("lnscr", [128, NT], F32))
    lnscr2 = ctx.enter_context(nc.sbuf_tensor("lnscr2", [128, NT], F32))
    junkb = ctx.enter_context(nc.sbuf_tensor("junkb", [128, 1024], BF16))
    junks = ctx.enter_context(nc.sbuf_tensor("junks", [128, 1024], BF16))
    rawsum = ctx.enter_context(nc.sbuf_tensor("rawsum", [128, NT], F32))
    s1bc = ctx.enter_context(nc.sbuf_tensor("s1bc", [128, 1], F32))
    m1bc = ctx.enter_context(nc.sbuf_tensor("m1bc", [128, 1], F32))
    s2bc = ctx.enter_context(nc.sbuf_tensor("s2bc", [128, 1], F32))
    m2c = ctx.enter_context(nc.sbuf_tensor("m2c", [128, 1], F32))
    gmc = ctx.enter_context(nc.sbuf_tensor("gmc", [128, 32], F32))
    gmcb = ctx.enter_context(nc.sbuf_tensor("gmcb", [128, 32], BF16))
    b1tot = ctx.enter_context(nc.sbuf_tensor("b1tot", [128, 32], F32))
    wsumr = ctx.enter_context(nc.sbuf_tensor("wsumr", [128, 1024], BF16))
    biasr = ctx.enter_context(nc.sbuf_tensor("biasr", [128, 1024], BF16))
    biasC = ctx.enter_context(nc.sbuf_tensor("biasC", [128, 8], F32))
    wsC = ctx.enter_context(nc.sbuf_tensor("wsC", [128, 8], F32))
    wpart = ctx.enter_context(nc.sbuf_tensor("wpart", [128, 1], F32))
    wtmp = ctx.enter_context(nc.sbuf_tensor("wtmp", [128, 2], F32))

    P = Prog(nc, ctx)

    def dma(q, dst, src, sem=None):
        return P.dmaq(q, dst, src)

    def LP(build):
        def f(e):
            with nc.allow_low_precision(reason="intentional bf16 operand"):
                return build(e)
        return f

    def vgate():
        P.wait("vector", "sV", P.mark("sV"))

    def phase_barrier():
        for eng in ENGINES:
            for cn in ("sM", "sV", "sS", "sG"):
                P.wait(eng, cn, P.mark(cn))
            for q in ("sync", "gpsimd", "scalar"):
                P.wait_all(eng, q)

    def ggate():
        P.wait("gpsimd", "sG", P.mark("sG"))

    dbg_state = {"off": 0}

    def dump(buf_ap, rows=128):
        w = buf_ap.shape[-1]
        off = dbg_state["off"]
        for cn in ("sV", "sS", "sM", "sG"):
            P.wait("gpsimd", cn, P.mark(cn))
        for q in ("sync", "gpsimd", "scalar"):
            P.wait_all("gpsimd", q)
        dma("gpsimd", dbg_h.ap()[0:rows, off:off + w], buf_ap)
        dbg_state["off"] += w

    def finalize():
        for cname in ("sM", "sV", "sS", "sG"):
            P.wait("gpsimd", cname, P.mark(cname))
        for q in ("sync", "gpsimd", "scalar"):
            P.wait_all("gpsimd", q)
        with nc.Block() as block:
            @block.sync
            def _(e):
                for fn in P.plan["sync"]:
                    fn(e)

            @block.tensor
            def _(e):
                for fn in P.plan["tensor"]:
                    fn(e)

            @block.vector
            def _(e):
                for fn in P.plan["vector"]:
                    fn(e)

            @block.scalar
            def _(e):
                for fn in P.plan["scalar"]:
                    fn(e)

            @block.gpsimd
            def _(e):
                for fn in P.plan["gpsimd"]:
                    fn(e)
        return nc, ctx

    dma("sync", gam[:], gam_h.ap(), "sX")
    dma("sync", g1c[:], g1_h.ap(), "sX")
    dma("sync", b1c[:], b1_h.ap(), "sX")
    dma("sync", g2c[:], g2_h.ap(), "sX")
    dma("sync", b2c[:], b2_h.ap(), "sX")
    dma("gpsimd", wb2[:], AP(wb2_h, 0, [[128, 128], [16384, 8], [1, 128]]), "sC")
    dma("gpsimd", ident[:], id16_h.ap(), "sC")
    dma("sync", identf[:], id32_h.ap(), "sX")
    dma("gpsimd", b1lnc[:, 0:8], b1ln_h.ap(), "sC")
    dma("gpsimd", b2lnc[:, 0:32], b2ln_h.ap(), "sC")
    dma("sync", lnscr[:, 0:8], b1ln_h.ap(), "sX")
    dma("sync", lnscr2[:], b2ln_h.ap(), "sX")
    P.do("vector", lambda e: e.memset(onescol[:], 1.0))
    P.do("vector", lambda e: e.memset(onescolf[:], 1.0), inc="sV")
    P.wait_all("tensor", "sync")
    P.wait_all("vector", "sync")
    P.wait_all("scalar", "sync")
    P.wait_all("tensor", "gpsimd")
    P.wait_all("gpsimd", "gpsimd")
    P.wait_all("vector", "gpsimd")
    P.do("vector", lambda e: e.tensor_tensor(
        b1lnc[:, 8:16], lnscr[:, 0:8], b1lnc[:, 0:8], ALU.subtract), inc="sV")
    P.do("vector", lambda e: e.tensor_tensor(
        b2lnc[:, 32:64], lnscr2[:], b2lnc[:, 0:32], ALU.subtract), inc="sV")
    P.wait("tensor", "sV", P.mark("sV"))
    P.wait("tensor", "sV")

    def ln_stats_tile(t, xbuf_ap, mu, nmu, ssq, xw_v):
        if xw_v:
            P.wait("vector", "sX", xw_v)
        P.do("vector", lambda e: e.reduce_sum(rawsum[:, t:t + 1], xbuf_ap, axis=AX.X),
             inc="sV")
        vgate()
        P.do("vector", lambda e: e.tensor_scalar(
            mu[:, t:t + 1], rawsum[:, t:t + 1], 1.0 / D, None, op0=ALU.mult))
        vv = P.do("vector", lambda e: e.tensor_scalar(
            nmu[:, t:t + 1], rawsum[:, t:t + 1], -1.0 / D, None, op0=ALU.mult),
            inc="sV")
        P.wait("scalar", "sV", vv)
        P.wait("scalar", "sS", P.mark("sS"))
        P.do("scalar", lambda e: e.activation(
            out=junks[:], in_=xbuf_ap, func=AF.Square,
            bias=nmu[:, t:t + 1], scale=1.0, accum_out=ssq[:, t:t + 1]), inc="sS")

    def ln_R_batch(t0, nt, ssq, Rbuf, dim):
        sl = (slice(None), slice(t0, t0 + nt))
        P.wait("vector", "sS", P.mark("sS"))
        P.do("vector", lambda e: e.tensor_scalar(
            lnscr[sl], ssq[sl], 1.0 / dim, EPS, op0=ALU.mult, op1=ALU.add), inc="sV")
        P.wait("scalar", "sV", P.mark("sV"))
        ss = P.do("scalar", lambda e: e.activation(
            out=lnscr2[sl], in_=lnscr[sl], func=AF.Sqrt, bias=0.0, scale=1.0), inc="sS")
        P.wait("vector", "sS", ss)
        P.do("vector", lambda e: e.reciprocal(Rbuf[sl], lnscr2[sl]))
        P.do("vector", lambda e: e.tensor_scalar(
            lnscr[sl], lnscr[sl], -EPS, None, op0=ALU.add), inc="sV")
        vgate()
        P.do("vector", lambda e: e.tensor_tensor(
            lnscr[sl], lnscr[sl], Rbuf[sl], ALU.mult), inc="sV")
        vgate()
        P.do("vector", lambda e: e.tensor_tensor(
            lnscr[sl], lnscr[sl], Rbuf[sl], ALU.mult), inc="sV")
        vgate()
        vv = P.do("vector", lambda e: e.tensor_scalar(
            lnscr[sl], lnscr[sl], EPS, None, op0=ALU.add), inc="sV")
        P.wait("scalar", "sV", vv)
        ss = P.do("scalar", lambda e: e.activation(
            out=lnscr2[sl], in_=lnscr[sl], func=AF.Sqrt, bias=0.0, scale=1.0), inc="sS")
        P.wait("vector", "sS", ss)
        P.do("vector", lambda e: e.reciprocal(lnscr2[sl], lnscr2[sl]), inc="sV")
        vgate()
        P.do("vector", lambda e: e.tensor_tensor(Rbuf[sl], Rbuf[sl], lnscr2[sl], ALU.mult),
             inc="sV")
        vgate()

    attctx = ctx.enter_context(ExitStack())
    kT = attctx.enter_context(nc.sbuf_tensor("kT", [128, 8 * N], BF16))
    vaug = attctx.enter_context(nc.sbuf_tensor("vaug", [128, NT * 1040], BF16))
    tabc_s = attctx.enter_context(nc.sbuf_tensor("tabc_s", [128, NT * DH], BF16))
    tabs_s = attctx.enter_context(nc.sbuf_tensor("tabs_s", [128, NT * DH], BF16))
    dma("gpsimd", tabc_s[:], tabc_h.ap(), "sC")
    dma("gpsimd", tabs_s[:], tabs_h.ap(), "sC")
    vgate()
    P.do("vector", lambda e: e.tensor_copy(
        AP(vaug, 64, [[NT * 1040, 128], [1040, NT], [65, H]]),
        onescol[:, 0:1].to_broadcast([128, NT * H])), inc="sV")
    P.wait_all("vector", "gpsimd")

    def rope_emit(dst, ps_handle, t, nheads, prow):
        w = nheads * DH
        cap = AP(tabc_s, t * DH, [[NT * DH, 128], [0, nheads], [1, DH]])
        sap = AP(tabs_s, t * DH, [[NT * DH, 128], [0, nheads], [1, DH]])
        src = AP(ps_handle, 0, [[prow, 128], [1, w]])
        partner = AP(ps_handle, 1, [[prow, 128], [DH, nheads], [2, DH // 2], [-1, 2]])
        P.do("vector", lambda e: e.tensor_tensor(dst, src, cap, ALU.mult))
        P.do("vector", lambda e: e.tensor_tensor(junkb[:, 0:w], partner, sap, ALU.mult),
             inc="sV")
        vgate()
        return P.do("vector", lambda e: e.tensor_tensor(
            dst, dst, junkb[:, 0:w], ALU.add), inc="sV")

    kvctx = ctx.enter_context(ExitStack())
    wkvb = kvctx.enter_context(nc.sbuf_tensor("wkvb", [128, 8 * 2048], BF16))
    wslf = kvctx.enter_context(nc.sbuf_tensor("wslf", [128, 1024], F32))
    xr0 = kvctx.enter_context(nc.sbuf_tensor("xr0", [128, 4 * 1024], F32))
    xn0 = kvctx.enter_context(nc.sbuf_tensor("xn0", [128, 1024], BF16))
    xnT0 = kvctx.enter_context(nc.sbuf_tensor("xnT0", [128, 1024], BF16))
    ktile = kvctx.enter_context(nc.sbuf_tensor("ktile", [128, 1024], BF16))

    pv_wsl = 0
    wq_st = None
    for dc in range(8):
        P.wait("sync", "sV", pv_wsl)
        lw = dma("sync", wslf[:], wq_h.ap()[dc * 128:(dc + 1) * 128, :])
        P.waitt("vector", lw)
        P.waitt("vector", wq_st)
        vv = P.do("vector", lambda e, dc=dc: e.tensor_scalar(
            junkb[:], wslf[:], gam[:, dc:dc + 1], 0.125,
            op0=ALU.mult, op1=ALU.mult), inc="sV")
        pv_wsl = vv
        P.wait("scalar", "sV", vv)
        wq_st = dma("scalar", wqs_scr.ap()[dc * 128:(dc + 1) * 128, :], junkb[:])
    P.waitt("vector", wq_st)
    for dc in range(8):
        for ecg in range(2):
            P.wait("sync", "sV", pv_wsl)
            lw = dma("sync", wslf[:],
                     wkv_h.ap()[dc * 128:(dc + 1) * 128, ecg * 1024:(ecg + 1) * 1024],
                     "sX")
            P.wait("vector", "sX", lw)
            pv_wsl = P.do("vector", lambda e, dc=dc, ecg=ecg: e.tensor_scalar(
                wkvb[:, dc * 2048 + ecg * 1024: dc * 2048 + (ecg + 1) * 1024],
                wslf[:], gam[:, dc:dc + 1], None, op0=ALU.mult), inc="sV")
    P.wait("tensor", "sV", P.mark("sV"))

    kv_ev = 0
    tr_ev = 0
    sq_last = 0
    for g in range(8):
        xw = []
        P.wait("sync", "sV", tr_ev)
        P.wait("sync", "sS", sq_last)
        for i in range(4):
            t = g * 4 + i
            xw.append(dma("sync", xr0[:, i * 1024:(i + 1) * 1024],
                          x_h.ap()[t * 128:(t + 1) * 128, :], "sX"))
        for i in range(4):
            ln_stats_tile(g * 4 + i, xr0[:, i * 1024:(i + 1) * 1024], mu0, nmu0, ssq0,
                          xw[i])
        sq_last = P.mark("sS")
        ln_R_batch(g * 4, 4, ssq0, R0, D)
        for i in range(4):
            t = g * 4 + i
            vv = P.do("vector", lambda e, i=i, t=t: e.tensor_scalar(
                xn0[:], xr0[:, i * 1024:(i + 1) * 1024], mu0[:, t:t + 1], R0[:, t:t + 1],
                op0=ALU.subtract, op1=ALU.mult), inc="sV")
            P.wait("tensor", "sV", vv)
            P.wait("tensor", "sV", tr_ev)
            for d8 in range(8):
                mmv = P.do("tensor", lambda e, d8=d8: e.transpose(
                    psT[:, d8 * 128:(d8 + 1) * 128],
                    xn0[:, d8 * 128:(d8 + 1) * 128], ident[:]),
                    inc="sM" if d8 == 7 else None)
            P.wait("vector", "sM", mmv)
            tr_ev = P.do("vector", lambda e: e.tensor_copy(xnT0[:], psT[:]), inc="sV")
            P.wait("tensor", "sV", tr_ev)
            P.wait("tensor", "sV", kv_ev)
            for dc in range(8):
                for ec in range(4):
                    pb = ps01 if ec < 2 else ps23
                    mmv = P.do("tensor", lambda e, dc=dc, ec=ec, pb=pb: e.matmul(
                        pb[:, (ec % 2) * 512:((ec % 2) + 1) * 512],
                        xnT0[:, dc * 128:(dc + 1) * 128],
                        wkvb[:, dc * 2048 + ec * 512: dc * 2048 + (ec + 1) * 512],
                        start=(dc == 0), stop=(dc == 7)),
                        inc="sM" if (dc == 7 and ec == 3) else None)
            P.wait("vector", "sM", mmv)
            rope_emit(ktile[:], ps01, t, 16, 1024)
            kv_ev = P.do("vector", lambda e, t=t: e.tensor_copy(
                AP(vaug, t * 1040, [[NT * 1040, 128], [65, 16], [1, 64]]), ps23[:]), inc="sV")
            P.wait("tensor", "sV", kv_ev)
            for d8 in range(8):
                mmv = P.do("tensor", lambda e, d8=d8: e.transpose(
                    psT[:, d8 * 128:(d8 + 1) * 128],
                    ktile[:, d8 * 128:(d8 + 1) * 128], ident[:]),
                    inc="sM" if d8 == 7 else None)
            P.wait("vector", "sM", mmv)
            for d8 in range(8):
                tr_ev = P.do("vector", lambda e, d8=d8, t=t: e.tensor_copy(
                    kT[:, d8 * N + t * 128: d8 * N + (t + 1) * 128],
                    psT[:, d8 * 128:(d8 + 1) * 128]),
                    inc="sV" if d8 == 7 else None)
    P.wait("tensor", "sV", tr_ev)
    phase_barrier()
    kvctx.close()

    if stop_after == "P12":
        dump(kT[:, 0:512])
        dump(kT[:, 7 * N + 1024:7 * N + 1536])
        dump(vaug[:, 0:1040])
        dump(mu0[:])
        dump(R0[:])
        return finalize()

    ackctx = ctx.enter_context(ExitStack())
    xq = ackctx.enter_context(nc.sbuf_tensor("xq", [128, 1024], F32))
    xn_t = ackctx.enter_context(nc.sbuf_tensor("xn_t", [128, 1024], BF16))
    xnT4 = ackctx.enter_context(nc.sbuf_tensor("xnT4", [128, 4096], BF16))
    qtile = ackctx.enter_context(nc.sbuf_tensor("qtile", [128, 512], BF16))
    qTc = ackctx.enter_context(nc.sbuf_tensor("qTc", [128, 8 * 512], BF16))
    PT = ackctx.enter_context(nc.sbuf_tensor("PT", [128, 4 * 1024], BF16))
    km2b = ackctx.enter_context(nc.sbuf_tensor("km2b", [128, 128], F32))
    kmax2 = ackctx.enter_context(nc.sbuf_tensor("kmax2", [128, 16], F32))
    qm2 = ackctx.enter_context(nc.sbuf_tensor("qm2", [128, 16], F32))
    cm2 = ackctx.enter_context(nc.sbuf_tensor("cm2", [128, 16], F32))
    sq16 = ackctx.enter_context(nc.sbuf_tensor("sq16", [128, 16], F32))
    negbf = ackctx.enter_context(nc.sbuf_tensor("negbf", [128, 16], BF16))
    negc = ackctx.enter_context(nc.sbuf_tensor("negc", [128, 16], F32))
    recb = ackctx.enter_context(nc.sbuf_tensor("recb", [128, 1024], BF16))
    oP = ackctx.enter_context(nc.sbuf_tensor("oP", [128, 8 * 512], BF16))
    otmp = ackctx.enter_context(nc.sbuf_tensor("otmp", [128, 512], BF16))
    wsh = ackctx.enter_context(nc.sbuf_tensor("wsh", [128, 8 * 512], BF16))
    xpr = ackctx.enter_context(nc.sbuf_tensor("xpr", [128, 2 * 512], F32))

    kn_mm = [0, 0]
    kn_rd = 0
    for p8 in range(8):
        for jj in range(8):
            sel = (p8 * 8 + jj) % 2
            colb = p8 * N + jj * 512
            P.wait("scalar", "sM", kn_mm[sel])
            ssq = P.do("scalar", lambda e, colb=colb, sel=sel: e.activation(
                out=junks[:, sel * 512:(sel + 1) * 512],
                in_=kT[:, colb:colb + 512], func=AF.Square, bias=0.0, scale=1.0),
                inc="sS")
            P.wait("tensor", "sS", ssq)
            P.wait("tensor", "sV", kn_rd)
            P.do("tensor", lambda e, sel=sel: e.matmul(
                psPV0[0:1, 0:512], onescol[0:64, 0:1],
                junks[0:64, sel * 512:(sel + 1) * 512], start=True, stop=True))
            mmv = P.do("tensor", lambda e, sel=sel: e.matmul(
                psPV1[0:1, 0:512], onescol[64:128, 0:1],
                junks[64:128, sel * 512:(sel + 1) * 512], start=True, stop=True,
                tile_position=(64, 0)), inc="sM")
            kn_mm[sel] = mmv
            P.wait("vector", "sM", mmv)
            P.do("vector", lambda e, p8=p8, jj=jj: e.tensor_reduce(
                km2b[0:1, p8 * 16 + jj:p8 * 16 + jj + 1], psPV0[0:1, 0:512],
                axis=AX.X, op=ALU.max))
            kn_rd = P.do("vector", lambda e, p8=p8, jj=jj: e.tensor_reduce(
                km2b[0:1, p8 * 16 + 8 + jj:p8 * 16 + 8 + jj + 1], psPV1[0:1, 0:512],
                axis=AX.X, op=ALU.max), inc="sV")
    vgate()
    for h in range(16):
        p8, hh = h // 2, h % 2
        P.do("vector", lambda e, h=h, p8=p8, hh=hh: e.tensor_reduce(
            kmax2[0:1, h:h + 1], km2b[0:1, p8 * 16 + hh * 8:p8 * 16 + hh * 8 + 8],
            axis=AX.X, op=ALU.max), inc="sV" if h == 15 else None)
    vgate()

    nchunks = 1 if stop_after == "C1" else NQC
    xq_free = [0, 0]
    wq_free = 0
    st_w = [None, None]
    odma = [None]
    odma_list = []
    xnT_ev = [0, 0, 0, 0]
    for c in range(nchunks):
        P.wait("tensor", "sS", P.mark("sS"))
        xnT_guard = P.mark("sM")
        for eh in range(2):
            P.wait("sync", "sM", wq_free)
            lws = []
            for dc in range(8):
                lws.append(dma("sync", wsh[:, dc * 512:(dc + 1) * 512],
                               wqs_scr.ap()[dc * 128:(dc + 1) * 128,
                                            eh * 512:(eh + 1) * 512]))
            P.waitt("tensor", lws)
            for ti in range(4):
                t = c * 4 + ti
                if eh == 0:
                    P.wait("sync", "sV", xq_free[0])
                    xw = dma("sync", xq[:], x_h.ap()[t * 128:(t + 1) * 128, :], "sX")
                    P.wait("vector", "sX", xw)
                    xq_free[0] = P.do("vector", lambda e, t=t: e.tensor_scalar(
                        xn_t[:], xq[:], mu0[:, t:t + 1], R0[:, t:t + 1],
                        op0=ALU.subtract, op1=ALU.mult), inc="sV")
                    P.wait("tensor", "sV", xq_free[0])
                    for d8 in range(8):
                        mmv = P.do("tensor", lambda e, d8=d8: e.transpose(
                            psT[:, d8 * 128:(d8 + 1) * 128],
                            xn_t[:, d8 * 128:(d8 + 1) * 128], ident[:]),
                            inc="sM" if d8 == 7 else None)
                    P.wait("vector", "sM", mmv)
                    P.wait("vector", "sM", xnT_guard)
                    vv = P.do("vector", lambda e, ti=ti: e.tensor_copy(
                        xnT4[:, ti * 1024:(ti + 1) * 1024], psT[:]), inc="sV")
                    xnT_ev[ti] = vv
                    P.wait("tensor", "sV", vv)
                else:
                    P.wait("tensor", "sV", xnT_ev[ti])
                for dc in range(8):
                    mmv = P.do("tensor", lambda e, dc=dc, ti=ti: e.matmul(
                        psM[:, 0:512],
                        xnT4[:, ti * 1024 + dc * 128: ti * 1024 + (dc + 1) * 128],
                        wsh[:, dc * 512:(dc + 1) * 512],
                        start=(dc == 0), stop=(dc == 7)),
                        inc="sM" if dc == 7 else None)
                wq_free = mmv
                P.wait("vector", "sM", mmv)
                rope_emit(qtile[:], psM, t, 8, 512)
                P.wait("tensor", "sV", P.mark("sV"))
                for s4 in range(4):
                    mmv = P.do("tensor", lambda e, s4=s4: e.transpose(
                        psT[:, s4 * 128:(s4 + 1) * 128],
                        qtile[:, s4 * 128:(s4 + 1) * 128], ident[:]),
                        inc="sM" if s4 == 3 else None)
                P.wait("vector", "sM", mmv)
                for s4 in range(4):
                    p = eh * 4 + s4
                    qT_ev = P.do("vector", lambda e, p=p, ti=ti, s4=s4: e.tensor_copy(
                        qTc[:, p * 512 + ti * 128: p * 512 + (ti + 1) * 128],
                        psT[:, s4 * 128:(s4 + 1) * 128]),
                        inc="sV" if s4 == 3 else None)
                P.wait("tensor", "sV", qT_ev)
        qn_mm = [0, 0]
        qn_rd = 0
        for p in range(8):
            mmv = P.do("tensor", lambda e, p=p: e.matmul(
                psM[:, 0:512], wb2[:, p * 128:(p + 1) * 128],
                qTc[:, p * 512:(p + 1) * 512], start=True, stop=True), inc="sM")
            P.wait("vector", "sM", mmv)
            vv = P.do("vector", lambda e, p=p: e.tensor_copy(
                qTc[:, p * 512:(p + 1) * 512], psM[:, 0:512]), inc="sV")
            P.wait("tensor", "sV", vv)
            sel = p % 2
            P.wait("scalar", "sM", qn_mm[sel])
            P.wait("scalar", "sV", vv)
            ssq = P.do("scalar", lambda e, p=p, sel=sel: e.activation(
                out=junks[:, sel * 512:(sel + 1) * 512],
                in_=qTc[:, p * 512:(p + 1) * 512], func=AF.Square, bias=0.0,
                scale=1.0), inc="sS")
            P.wait("tensor", "sS", ssq)
            P.wait("tensor", "sV", qn_rd)
            P.do("tensor", lambda e, sel=sel: e.matmul(
                psPV0[0:1, 0:512], onescol[0:64, 0:1],
                junks[0:64, sel * 512:(sel + 1) * 512], start=True, stop=True))
            mmq = P.do("tensor", lambda e, sel=sel: e.matmul(
                psPV1[0:1, 0:512], onescol[64:128, 0:1],
                junks[64:128, sel * 512:(sel + 1) * 512], start=True, stop=True,
                tile_position=(64, 0)), inc="sM")
            qn_mm[sel] = mmq
            P.wait("vector", "sM", mmq)
            P.do("vector", lambda e, p=p: e.tensor_reduce(
                qm2[0:1, 2 * p:2 * p + 1], psPV0[0:1, 0:512], axis=AX.X, op=ALU.max))
            qn_rd = P.do("vector", lambda e, p=p: e.tensor_reduce(
                qm2[0:1, 2 * p + 1:2 * p + 2], psPV1[0:1, 0:512], axis=AX.X,
                op=ALU.max), inc="sV")
        vgate()
        P.do("vector", lambda e: e.tensor_tensor(
            cm2[0:1, 0:16], qm2[0:1, 0:16], kmax2[0:1, 0:16], ALU.mult), inc="sV")
        vgate()
        vv = P.do("vector", lambda e: e.tensor_reduce(
            sq16[0:1, 0:8], AP(cm2, 0, [[16, 1], [2, 8], [1, 2]]), axis=AX.X,
            op=ALU.max), inc="sV")
        P.wait("scalar", "sV", vv)
        ssv = P.do("scalar", lambda e: e.activation(
            out=cm2[0:1, 8:16], in_=sq16[0:1, 0:8], func=AF.Sqrt, bias=0.0,
            scale=1.0), inc="sS")
        P.wait("vector", "sS", ssv)
        vv = P.do("vector", LP(lambda e: e.tensor_scalar(
            negbf[0:1, 0:8], cm2[0:1, 8:16], -1.0, None, op0=ALU.mult)), inc="sV")
        P.wait("tensor", "sV", vv)
        mmv = P.do("tensor", lambda e: e.matmul(
            psM[:, 0:8], onescol[0:1, 0:1].to_broadcast([1, 128]),
            negbf[0:1, 0:8], start=True, stop=True), inc="sM")
        P.wait("vector", "sM", mmv)
        negc_v = P.do("vector", lambda e: e.tensor_copy(negc[:, 0:8], psM[:, 0:8]),
                      inc="sV")

        P.wait("gpsimd", "sM", wq_free)
        wo_lw0 = []
        for cc in range(8):
            wo_lw0.append(dma("gpsimd", wsh[:, cc * 512:(cc + 1) * 512],
                              wo_h.ap()[cc * 128:(cc + 1) * 128, 0:512]))

        stat_slots = [(psM, 0)]
        slot_cons = [("sV", negc_v)]
        st_cons = [0, 0]
        ring = [0]

        def next_slot():
            s = ring[0] % len(stat_slots)
            ring[0] += 1
            return s

        def waitc(eng, tok):
            if tok is not None:
                P.wait(eng, tok[0], tok[1])

        def emit_stpv(p, negc_v):
            st_slots = [ps01, ps23]
            pt_cons = [0] * 4
            ovv = P.mark("sV")
            P.wait("scalar", "sV", negc_v)
            first_pv = [True]
            pv_q = []
            mm2l = [0]

            def emit_pv(j, pslot, expv):
                P.wait("tensor", "sS", expv)
                if first_pv[0]:
                    P.wait("tensor", "sV", ovv)
                    first_pv[0] = False
                h0 = 2 * p
                h1 = 2 * p + 1
                P.do("tensor", lambda e, j=j, pslot=pslot, h0=h0: e.matmul(
                    psPV0[0:65, :],
                    vaug[:, j * 1040 + h0 * 65: j * 1040 + h0 * 65 + 65],
                    PT[:, pslot * 1024:pslot * 1024 + 512],
                    start=(j == 0), stop=(j == 31)))
                mm2l[0] = P.do("tensor", lambda e, j=j, pslot=pslot, h1=h1:
                               e.matmul(
                                   psPV1[0:65, :],
                                   vaug[:, j * 1040 + h1 * 65:
                                        j * 1040 + h1 * 65 + 65],
                                   PT[:, pslot * 1024 + 512:(pslot + 1) * 1024],
                                   start=(j == 0), stop=(j == 31)), inc="sM")
                pt_cons[pslot] = mm2l[0]

            for j in range(32):
                ss = (j + p) % 2
                spt = st_slots[ss]
                P.wait("tensor", "sS", st_cons[ss])
                P.do("tensor", lambda e, p=p, j=j, spt=spt: e.matmul(
                    spt[:, 0:512],
                    kT[0:64, p * N + j * 128: p * N + (j + 1) * 128],
                    qTc[0:64, p * 512:(p + 1) * 512],
                    start=True, stop=True))
                mmv = P.do("tensor", lambda e, p=p, j=j, spt=spt: e.matmul(
                    spt[:, 512:1024],
                    kT[64:128, p * N + j * 128: p * N + (j + 1) * 128],
                    qTc[64:128, p * 512:(p + 1) * 512],
                    start=True, stop=True, tile_position=(64, 0)), inc="sM")
                pslot = j % 4
                P.wait("scalar", "sM", mmv)
                P.wait("scalar", "sM", pt_cons[pslot])
                st_cons[ss] = P.do("scalar", lambda e, spt=spt, pslot=pslot, p=p:
                                   e.activation(
                                       out=PT[:, pslot * 1024:(pslot + 1) * 1024],
                                       in_=spt[:, 0:1024], func=AF.Exp,
                                       bias=negc[:, p:p + 1], scale=1.0),
                                   inc="sS")
                pv_q.append((j, pslot, st_cons[ss]))
                if len(pv_q) > 2:
                    emit_pv(*pv_q.pop(0))
            while pv_q:
                emit_pv(*pv_q.pop(0))
            mmv = mm2l[0]
            P.wait("scalar", "sM", mmv)
            P.wait("scalar", "sS", P.mark("sS"))
            if p == 0:
                P.waitt("scalar", [tk for tk in st_w if tk])
            P.do("scalar", lambda e: e.activation(
                out=xpr[64:65, 0:512], in_=psPV0[64:65, :], func=AF.Ln,
                bias=0.0, scale=1.0))
            sl1 = P.do("scalar", lambda e: e.activation(
                out=xpr[64:65, 512:1024], in_=psPV1[64:65, :], func=AF.Ln,
                bias=0.0, scale=1.0), inc="sS")
            P.wait("tensor", "sS", sl1)
            waitc("tensor", slot_cons[0])
            P.do("tensor", lambda e: e.matmul(
                psM[0:64, 0:512], onescolf[64:65, 0:1].to_broadcast([1, 64]),
                xpr[64:65, 0:512], start=True, stop=True, tile_position=(64, 0)))
            tb = p % 2
            tpt = st_slots[tb]
            P.wait("tensor", "sS", st_cons[tb])
            mmb = P.do("tensor", lambda e, tpt=tpt: e.matmul(
                tpt[0:64, 0:512], onescolf[64:65, 0:1].to_broadcast([1, 64]),
                xpr[64:65, 512:1024], start=True, stop=True,
                tile_position=(64, 0)), inc="sM")
            P.wait("scalar", "sM", mmb)
            scv0 = P.do("scalar", lambda e: e.activation(
                out=recb[0:64, 0:512], in_=psM[0:64, :], func=AF.Exp, bias=0.0,
                scale=-1.0), inc="sS")
            scv = P.do("scalar", lambda e, tpt=tpt: e.activation(
                out=recb[0:64, 512:1024], in_=tpt[0:64, 0:512], func=AF.Exp,
                bias=0.0, scale=-1.0), inc="sS")
            slot_cons[0] = ("sS", scv0)
            st_cons[tb] = scv
            P.wait("vector", "sS", scv)
            P.waitt("vector", odma[0])
            P.do("vector", lambda e, p=p: e.tensor_tensor(
                oP[0:64, p * 512:(p + 1) * 512], psPV0[0:64, :], recb[0:64, 0:512],
                ALU.mult))
            ovv = P.do("vector", lambda e, p=p: e.tensor_tensor(
                otmp[0:64, :], psPV1[0:64, :], recb[0:64, 512:1024], ALU.mult),
                inc="sV")
            P.wait("gpsimd", "sV", ovv)
            odma[0] = dma("gpsimd", oP[64:128, p * 512:(p + 1) * 512], otmp[0:64, :])
            odma_list.append(odma[0])
            P.wait("tensor", "sV", ovv)
            return P.mark("sS")

        odma_list.clear()
        for p in range(8):
            emit_stpv(p, negc_v)
        mmv = P.mark("sM")

        P.waitt("tensor", list(odma_list))
        for eh in range(2):
            if eh == 1:
                P.wait("gpsimd", "sM", mmv)
                wo_lw1 = []
                for cc in range(8):
                    wo_lw1.append(dma("gpsimd", wsh[:, cc * 512:(cc + 1) * 512],
                                      wo_h.ap()[cc * 128:(cc + 1) * 128, 512:1024]))
                P.waitt("tensor", wo_lw1)
            else:
                P.waitt("tensor", wo_lw0)
            for cc in range(8):
                for ti in range(4):
                    pb = ps01 if ti < 2 else ps23
                    mmv = P.do("tensor", lambda e, cc=cc, ti=ti, pb=pb: e.matmul(
                        pb[:, (ti % 2) * 512:((ti % 2) + 1) * 512],
                        oP[:, cc * 512 + ti * 128: cc * 512 + (ti + 1) * 128],
                        wsh[:, cc * 512:(cc + 1) * 512],
                        start=(cc == 0), stop=(cc == 7)),
                        inc="sM" if (cc == 7 and ti == 3) else None)
            P.wait("vector", "sM", mmv)
            for ti in range(4):
                pb = ps01 if ti < 2 else ps23
                slw = ti % 2
                P.waitt("vector", st_w[slw])
                vv = P.do("vector", lambda e, ti=ti, pb=pb, slw=slw: e.tensor_copy(
                    xpr[:, slw * 512:(slw + 1) * 512],
                    pb[:, (ti % 2) * 512:((ti % 2) + 1) * 512]), inc="sV")
                P.wait("scalar", "sV", vv)
                t = c * 4 + ti
                st_w[slw] = dma(
                    "scalar", xa_scr.ap()[t * 128:(t + 1) * 128,
                                          eh * 512:(eh + 1) * 512],
                    xpr[:, slw * 512:(slw + 1) * 512])
            P.wait("tensor", "sV", P.mark("sV"))
        wq_free = P.mark("sM")

    if stop_after == "C1":
        dump(qTc[:, 0:512])
        dump(negc[:])
        dump(oP[:, 0:512])
        dump(oP[:, 3 * 512:4 * 512])
        dump(PT[:, 0:1024])
        dump(recb[:])
        return finalize()

    phase_barrier()
    ackctx.close()

    wpctx = ctx.enter_context(ExitStack())
    wfsl = wpctx.enter_context(nc.sbuf_tensor("wfsl", [128, 4096], F32))
    tsl = wpctx.enter_context(nc.sbuf_tensor("tsl", [128, 4096], BF16))
    tT = wpctx.enter_context(nc.sbuf_tensor("tT", [128, 4096], BF16))
    t8 = wpctx.enter_context(nc.sbuf_tensor("t8", [128, 1024], FP8))
    gtb = wpctx.enter_context(nc.sbuf_tensor("gtb", [128, 4096], BF16))

    def wmean(w_hh, nslab, slabw, outbc, outm):
        P.wait("sync", "sG", P.mark("sG"))
        P.wait("scalar", "sG", P.mark("sG"))
        P.do("vector", lambda e: e.memset(wpart[:], 0.0))
        half = 2048 if slabw > 2048 else slabw
        subs = [(s, o) for s in range(nslab) for o in range(0, slabw, half)]
        nsub = len(subs)
        toks = [None] * nsub

        def issue(i):
            sel = i % 2
            qn = "sync" if sel == 0 else "scalar"
            s, o = subs[i]
            toks[i] = dma(qn, wfsl[:, sel * 2048: sel * 2048 + half],
                          w_hh.ap()[s * 128:(s + 1) * 128, o:o + half])
        issue(0)
        if nsub > 1:
            issue(1)
        pv = 0
        for i in range(nsub):
            P.waitt("vector", toks[i])
            vgate()
            P.do("vector", lambda e, sel=i % 2, half=half: e.reduce_sum(
                wtmp[:, 0:1], wfsl[:, sel * 2048: sel * 2048 + half], axis=AX.X,
                apply_absolute_value=True), inc="sV")
            vgate()
            pv = P.do("vector", lambda e: e.tensor_tensor(
                wpart[:], wpart[:], wtmp[:, 0:1], ALU.add), inc="sV")
            if i + 2 < nsub:
                P.wait("sync" if i % 2 == 0 else "scalar", "sV", pv)
                issue(i + 2)
        vgate()
        vv = P.mark("sV")
        P.wait("tensor", "sV", vv)
        mmv = P.do("tensor", lambda e: e.matmul(
            psM[0:1, 0:1], onescolf[:, 0:1], wpart[:], start=True, stop=True),
            inc="sM")
        P.wait("vector", "sM", mmv)
        P.do("vector", lambda e: e.tensor_scalar(
            wtmp[0:1, 0:1], psM[0:1, 0:1], 1.0 / (HID * D), 1e-5,
            op0=ALU.mult, op1=ALU.max), inc="sV")
        vgate()
        P.do("vector", lambda e: e.reciprocal(wtmp[0:1, 1:2], wtmp[0:1, 0:1]), inc="sV")
        vgate()
        vv = P.mark("sV")
        P.wait("tensor", "sV", vv)
        mmv = P.do("tensor", lambda e: e.matmul(
            psM[:, 0:2], onescolf[0:1, 0:1].to_broadcast([1, 128]), wtmp[0:1, 0:2],
            start=True, stop=True), inc="sM")
        P.wait("vector", "sM", mmv)
        P.do("vector", lambda e: e.tensor_copy(outm[:], psM[:, 0:1]), inc="sV")
        vv = P.do("vector", lambda e: e.tensor_copy(outbc[:], psM[:, 1:2]), inc="sV")
        P.wait("gpsimd", "sV", vv)
        P.wait("tensor", "sV", vv)
        P.wait("sync", "sV", vv)
        P.wait("scalar", "sV", vv)
        vgate()

    wmean(w1_h, 32, 1024, s1bc, m1bc)
    ev_v = 0
    st_w1 = None
    tp_m = 0
    q1toks = [None] * 32

    def q1issue(s):
        sel = s % 2
        qn = "sync" if sel == 0 else "scalar"
        q1toks[s] = dma(qn, wfsl[:, sel * 2048: sel * 2048 + 1024],
                        w1_h.ap()[s * 128:(s + 1) * 128, :])
    P.wait("sync", "sV", P.mark("sV"))
    P.wait("scalar", "sV", P.mark("sV"))
    q1issue(0)
    q1issue(1)
    for s in range(32):
        off = (s % 2) * 2048
        P.waitt("vector", q1toks[s])
        P.wait("vector", "sM", tp_m)
        P.do("vector", lambda e, off=off: e.tensor_scalar(
            gtb[:, 0:1024], wfsl[:, off:off + 1024], s1bc[:], 0.5, op0=ALU.mult,
            op1=ALU.is_gt), inc="sG")
        P.do("vector", lambda e, off=off: e.tensor_scalar(
            tsl[:, 0:1024], wfsl[:, off:off + 1024], s1bc[:], -0.5, op0=ALU.mult,
            op1=ALU.is_lt), inc="sG")
        P.wait("vector", "sG", P.mark("sG"))
        gv = P.do("vector", lambda e: e.tensor_tensor(
            tsl[:, 0:1024], gtb[:, 0:1024], tsl[:, 0:1024], ALU.subtract), inc="sG")
        if s + 2 < 32:
            P.wait("sync" if s % 2 == 0 else "scalar", "sG", gv)
            q1issue(s + 2)
        P.wait("tensor", "sG", gv)
        P.wait("tensor", "sV", ev_v)
        for dc in range(8):
            tp_m = P.do("tensor", lambda e, dc=dc: e.transpose(
                psT[:, dc * 128:(dc + 1) * 128],
                tsl[:, dc * 128:(dc + 1) * 128], ident[:]),
                inc="sM" if dc == 7 else None)
        P.wait("vector", "sM", tp_m)
        P.waitt("vector", st_w1)
        ev_v = P.do("vector", lambda e: e.tensor_copy(t8[:, 0:1024], psT[:]), inc="sV")
        P.wait("scalar", "sV", ev_v)
        st_w1 = dma("scalar", w1t_scr.ap()[s, :, :], t8[:, 0:1024])

    wmean(w2_h, 8, 4096, s2bc, m2c)
    P.do("vector", lambda e: e.tensor_scalar(
        gmc[:], g2c[:], m2c[:], None, op0=ALU.mult), inc="sV")
    vgate()
    vv = P.do("vector", lambda e: e.tensor_copy(gmcb[:], gmc[:]), inc="sV")
    P.wait("tensor", "sV", vv)
    ev_v = 0
    st_w2 = None
    tp_m = 0
    sm_m = 0
    for s in range(8):
        P.wait("sync", "sG", P.mark("sG"))
        P.wait("scalar", "sG", P.mark("sG"))
        lwa = dma("sync", wfsl[:, 0:2048], w2_h.ap()[s * 128:(s + 1) * 128, 0:2048])
        lwb = dma("scalar", wfsl[:, 2048:4096],
                  w2_h.ap()[s * 128:(s + 1) * 128, 2048:4096])
        P.waitt("vector", [lwa, lwb])
        P.wait("vector", "sM", tp_m)
        P.do("vector", lambda e: e.tensor_scalar(
            gtb[:], wfsl[:], s2bc[:], 0.5, op0=ALU.mult, op1=ALU.is_gt), inc="sG")
        P.do("vector", lambda e: e.tensor_scalar(
            tsl[:], wfsl[:], s2bc[:], -0.5, op0=ALU.mult, op1=ALU.is_lt), inc="sG")
        P.wait("vector", "sG", P.mark("sG"))
        gv = P.do("vector", lambda e: e.tensor_tensor(
            tsl[:], gtb[:], tsl[:], ALU.subtract), inc="sG")
        P.wait("tensor", "sG", gv)
        P.waitt("tensor", st_w2)
        for hg in range(4):
            P.wait("tensor", "sV", ev_v)
            for k8 in range(8):
                hc = hg * 8 + k8
                tp_m = P.do("tensor", lambda e, hc=hc, k8=k8: e.transpose(
                    psT[:, k8 * 128:(k8 + 1) * 128],
                    tsl[:, hc * 128:(hc + 1) * 128], ident[:]),
                    inc="sM" if k8 == 7 else None)
            P.wait("vector", "sM", tp_m)
            ev_v = P.do("vector", lambda e, hg=hg: e.tensor_copy(
                tT[:, hg * 1024:(hg + 1) * 1024], psT[:]), inc="sV")
        P.wait("tensor", "sV", ev_v)
        for hc in range(32):
            P.do("tensor", lambda e, hc=hc: e.matmul(
                psPV0[:, 0:1], tT[:, hc * 128:(hc + 1) * 128], gmcb[:, hc:hc + 1],
                start=(hc == 0), stop=(hc == 31)))
            P.do("tensor", lambda e, hc=hc: e.matmul(
                psPV1[:, 0:1], tT[:, hc * 128:(hc + 1) * 128], b2lnc[:, hc:hc + 1],
                start=(hc == 0), stop=False))
            sm_m = P.do("tensor", lambda e, hc=hc: e.matmul(
                psPV1[:, 0:1], tT[:, hc * 128:(hc + 1) * 128],
                b2lnc[:, 32 + hc: 33 + hc],
                start=False, stop=(hc == 31)), inc="sM" if hc == 31 else None)
        P.wait("vector", "sM", sm_m)
        P.do("vector", lambda e, s=s: e.tensor_copy(wsC[:, s:s + 1], psPV0[:, 0:1]),
             inc="sV")
        P.do("vector", lambda e, s=s: e.tensor_copy(biasC[:, s:s + 1], psPV1[:, 0:1]),
             inc="sV")
        for hc in range(32):
            ev_v = P.do("vector", lambda e, hc=hc: e.tensor_scalar(
                tT[:, hc * 128:(hc + 1) * 128], tT[:, hc * 128:(hc + 1) * 128],
                gmc[:, hc:hc + 1], None, op0=ALU.mult),
                inc="sV" if hc == 31 else None)
        P.wait("scalar", "sV", ev_v)
        st_w2 = dma("scalar", w2t_scr.ap()[s, :, :], tT[:])
    vgate()
    P.do("vector", lambda e: e.tensor_scalar(
        biasC[:], biasC[:], m2c[:], None, op0=ALU.mult), inc="sV")
    vgate()
    P.do("vector", lambda e: e.tensor_tensor(biasC[:], biasC[:], b2c[:], ALU.add),
         inc="sV")
    vgate()
    P.do("vector", lambda e: e.tensor_copy(junkb[:, 0:8], wsC[:]))
    vv = P.do("vector", lambda e: e.tensor_copy(junkb[:, 8:16], biasC[:]), inc="sV")
    P.wait("tensor", "sV", vv)
    for s in range(8):
        P.do("tensor", lambda e, s=s: e.transpose(
            psT[0:1, 0:128], junkb[:, s:s + 1], ident[:]))
        mmv = P.do("tensor", lambda e, s=s: e.transpose(
            psT[0:1, 128:256], junkb[:, 8 + s:9 + s], ident[:]), inc="sM")
        P.wait("vector", "sM", mmv)
        P.do("vector", lambda e, s=s: e.tensor_copy(
            wsumr[0:1, s * 128:(s + 1) * 128], psT[0:1, 0:128]))
        vv = P.do("vector", lambda e, s=s: e.tensor_copy(
            biasr[0:1, s * 128:(s + 1) * 128], psT[0:1, 128:256]), inc="sV")
        P.wait("tensor", "sV", vv)
    phase_barrier()
    wpctx.close()
    attctx.close()

    if stop_after == "WPREP":
        dump(wsumr[0:1, :], rows=1)
        dump(biasr[0:1, :], rows=1)
        dump(m1bc[:, 0:1])
        dump(m2c[:, 0:1])
        return finalize()

    fctx = ctx.enter_context(ExitStack())
    xpc = fctx.enter_context(nc.sbuf_tensor("xpc", [128, 8 * 1024], F32))
    xn2 = fctx.enter_context(nc.sbuf_tensor("xn2", [128, 1024], BF16))
    xn2gT = fctx.enter_context(nc.sbuf_tensor("xn2gT", [128, 8 * 1024], BF16))
    uP = fctx.enter_context(nc.sbuf_tensor("uP", [128, 32 * 1024], BF16))
    u2r = fctx.enter_context(nc.sbuf_tensor("u2r", [128, 2 * 1024], BF16))
    w1r = fctx.enter_context(nc.sbuf_tensor("w1r", [128, 2 * 1024], FP8))
    w2r = fctx.enter_context(nc.sbuf_tensor("w2r", [128, 2 * 4096], BF16))
    rows_ = fctx.enter_context(nc.sbuf_tensor("rows_", [128, 3 * 1024], F32))
    rowsb = fctx.enter_context(nc.sbuf_tensor("rowsb", [128, 2 * 1024], BF16))
    fRc = fctx.enter_context(nc.sbuf_tensor("fRc", [128, 8], F32))
    w2o = fctx.enter_context(nc.sbuf_tensor("w2o", [128, 1024], BF16))
    ybuf = fctx.enter_context(nc.sbuf_tensor("ybuf", [128, 8 * 1024], F32))
    fmu = fctx.enter_context(nc.sbuf_tensor("fmu", [128, 8], F32))
    fnmu = fctx.enter_context(nc.sbuf_tensor("fnmu", [128, 8], F32))
    fssq = fctx.enter_context(nc.sbuf_tensor("fssq", [128, 8], F32))
    fR = fctx.enter_context(nc.sbuf_tensor("fR", [128, 8], F32))
    RA, RB, RC = 0, 1024, 2048

    yst_w = None
    ych_v = 0
    for f in range(NFC):
        P.waitt("sync", yst_w)
        P.wait("sync", "sV", ych_v)
        xw = []
        for ti in range(8):
            t = f * 8 + ti
            t1 = dma("sync", xpc[:, ti * 1024:(ti + 1) * 1024],
                     xa_scr.ap()[t * 128:(t + 1) * 128, :])
            t2 = dma("sync", ybuf[:, ti * 1024:(ti + 1) * 1024],
                     x_h.ap()[t * 128:(t + 1) * 128, :])
            xw.append([t1, t2])
        for ti in range(8):
            P.waitt("vector", xw[ti])
            P.do("vector", lambda e, ti=ti: e.tensor_tensor(
                xpc[:, ti * 1024:(ti + 1) * 1024], xpc[:, ti * 1024:(ti + 1) * 1024],
                ybuf[:, ti * 1024:(ti + 1) * 1024], ALU.add), inc="sV")
        vgate()
        for ti in range(8):
            ln_stats_tile(ti, xpc[:, ti * 1024:(ti + 1) * 1024], fmu, fnmu, fssq, 0)
        ln_R_batch(0, 8, fssq, fR, D)
        for ti in range(8):
            vv = P.do("vector", lambda e, ti=ti: e.tensor_scalar(
                xn2[:], xpc[:, ti * 1024:(ti + 1) * 1024],
                fmu[:, ti:ti + 1], fR[:, ti:ti + 1],
                op0=ALU.subtract, op1=ALU.mult), inc="sV")
            P.wait("tensor", "sV", vv)
            for d8 in range(8):
                mmv = P.do("tensor", lambda e, d8=d8: e.transpose(
                    psT[:, d8 * 128:(d8 + 1) * 128],
                    xn2[:, d8 * 128:(d8 + 1) * 128], ident[:]),
                    inc="sM" if d8 == 7 else None)
            P.wait("vector", "sM", mmv)
            for d8 in range(8):
                vv = P.do("vector", lambda e, d8=d8, ti=ti: e.tensor_scalar(
                    xn2gT[:, d8 * 1024 + ti * 128: d8 * 1024 + (ti + 1) * 128],
                    psT[:, d8 * 128:(d8 + 1) * 128],
                    g1c[:, d8:d8 + 1], None, op0=ALU.mult),
                    inc="sV" if d8 == 7 else None)
            P.wait("tensor", "sV", vv)

        gelu_s = [0, 0]
        w1mm = [0, 0]
        for hc in range(32):
            slot = hc % 2
            P.wait("sync", "sM", w1mm[slot])
            w1lw = dma("sync", w1r[:, slot * 1024:(slot + 1) * 1024],
                       w1t_scr.ap()[hc, :, :], "sX")
            pb = ps01 if slot == 0 else ps23
            P.wait("tensor", "sS", gelu_s[slot])
            P.wait("tensor", "sX", w1lw)
            for th in range(2):
                for dc in range(8):
                    mmv = P.do("tensor", lambda e, th=th, dc=dc, pb=pb, slot=slot:
                               e.matmul(
                                   pb[:, th * 512:(th + 1) * 512],
                                   w1r[:, slot * 1024 + dc * 128:
                                       slot * 1024 + (dc + 1) * 128],
                                   xn2gT[:, dc * 1024 + th * 512:
                                         dc * 1024 + (th + 1) * 512],
                                   start=(dc == 0), stop=(dc == 7)),
                               inc="sM" if (th == 1 and dc == 7) else None)
            if f == 0:
                for half16 in range(2):
                    for dc in range(8):
                        bmv = P.do("tensor", lambda e, dc=dc, slot=slot, half16=half16:
                                   e.matmul(
                                       psPV0[:, 0:1],
                                       w1r[:, slot * 1024 + dc * 128:
                                           slot * 1024 + (dc + 1) * 128],
                                       b1lnc[:, half16 * 8 + dc: half16 * 8 + dc + 1],
                                       start=(half16 == 0 and dc == 0),
                                       stop=(half16 == 1 and dc == 7)),
                                   inc="sM" if (half16 == 1 and dc == 7) else None)
                w1mm[slot] = bmv
                P.wait("vector", "sM", bmv)
                P.do("vector", lambda e, hc=hc: e.tensor_scalar(
                    b1tot[:, hc:hc + 1], psPV0[:, 0:1], m1bc[:], None, op0=ALU.mult),
                    inc="sV")
                vgate()
                vv = P.do("vector", lambda e, hc=hc: e.tensor_tensor(
                    b1tot[:, hc:hc + 1], b1tot[:, hc:hc + 1], b1c[:, hc:hc + 1],
                    ALU.add), inc="sV")
                P.wait("scalar", "sV", vv)
                P.wait("tensor", "sV", vv)
            else:
                w1mm[slot] = mmv
            P.wait("scalar", "sM", mmv)
            gelu_s[slot] = P.do("scalar", lambda e, hc=hc, pb=pb: e.activation(
                out=uP[:, hc * 1024:(hc + 1) * 1024], in_=pb[:], func=AF.Gelu,
                bias=b1tot[:, hc:hc + 1], scale=m1bc[:]), inc="sS")
        P.wait("tensor", "sS", P.mark("sS"))

        for hc in range(32):
            for th in range(2):
                grp = psM if th == 0 else psPV0
                mmv = P.do("tensor", lambda e, hc=hc, th=th, grp=grp: e.matmul(
                    grp[0:1, :], onescol[:, 0:1],
                    uP[:, hc * 1024 + th * 512: hc * 1024 + (th + 1) * 512],
                    start=(hc == 0), stop=(hc == 31)),
                    inc="sM" if (hc == 31 and th == 1) else None)
        P.wait("vector", "sM", mmv)
        P.do("vector", lambda e: e.tensor_scalar(
            rows_[0:1, RA:RA + 512], psM[0:1, :], 1.0 / HID, None, op0=ALU.mult))
        vv = P.do("vector", lambda e: e.tensor_scalar(
            rows_[0:1, RA + 512:RA + 1024], psPV0[0:1, :], 1.0 / HID, None,
            op0=ALU.mult), inc="sV")
        P.wait("tensor", "sV", vv)
        u2_m = [0, 0]
        for hc in range(32):
            slot = hc % 2
            P.wait("vector", "sM", u2_m[slot])
            uvv = P.do("vector", lambda e, hc=hc, slot=slot: e.tensor_tensor(
                u2r[:, slot * 1024:(slot + 1) * 1024],
                uP[:, hc * 1024:(hc + 1) * 1024],
                uP[:, hc * 1024:(hc + 1) * 1024], ALU.mult), inc="sV")
            P.wait("tensor", "sV", uvv)
            for th in range(2):
                grp = psM if th == 0 else psPV0
                u2_m[slot] = P.do("tensor", lambda e, hc=hc, th=th, grp=grp, slot=slot:
                                  e.matmul(
                                      grp[0:1, :], onescol[:, 0:1],
                                      u2r[:, slot * 1024 + th * 512:
                                          slot * 1024 + (th + 1) * 512],
                                      start=(hc == 0), stop=(hc == 31)),
                                  inc="sM" if th == 1 else None)
        P.wait("vector", "sM", max(u2_m))
        P.do("vector", lambda e: e.tensor_scalar(
            rows_[0:1, RB:RB + 512], psM[0:1, :], 1.0 / HID, None, op0=ALU.mult))
        P.do("vector", lambda e: e.tensor_scalar(
            rows_[0:1, RB + 512:RB + 1024], psPV0[0:1, :], 1.0 / HID, None,
            op0=ALU.mult))
        vgate()
        P.do("vector", lambda e: e.tensor_scalar(
            rowsb[0:1, 0:1024], rows_[0:1, RA:RA + 1024], -1.0, None, op0=ALU.mult))
        P.do("vector", lambda e: e.tensor_tensor(
            rows_[0:1, RC:RC + 1024], rows_[0:1, RA:RA + 1024],
            rows_[0:1, RA:RA + 1024], ALU.mult), inc="sV")
        vgate()
        P.do("vector", lambda e: e.tensor_tensor(
            rows_[0:1, RB:RB + 1024], rows_[0:1, RB:RB + 1024],
            rows_[0:1, RC:RC + 1024], ALU.subtract), inc="sV")
        vgate()
        vv = P.do("vector", lambda e: e.tensor_scalar(
            rows_[0:1, RB:RB + 1024], rows_[0:1, RB:RB + 1024], EPS, None,
            op0=ALU.add), inc="sV")
        P.wait("scalar", "sV", vv)
        ss = P.do("scalar", lambda e: e.activation(
            out=rows_[0:1, RA:RA + 1024], in_=rows_[0:1, RB:RB + 1024], func=AF.Sqrt,
            bias=0.0, scale=1.0), inc="sS")
        P.wait("vector", "sS", ss)
        P.do("vector", lambda e: e.tensor_copy(
            rowsb[0:1, 1024:2048], rows_[0:1, RA:RA + 1024]))
        vv = P.do("vector", lambda e: e.reciprocal(
            rows_[0:1, RC:RC + 1024], rows_[0:1, RA:RA + 1024]), inc="sV")
        vgate()
        P.wait("tensor", "sV", vv)
        for ti in range(8):
            mmv = P.do("tensor", lambda e, ti=ti: e.transpose(
                psM[:, 0:1], rows_[0:1, RC + ti * 128: RC + (ti + 1) * 128],
                identf[0:1, 0:1]), inc="sM")
            P.wait("vector", "sM", mmv)
            vv = P.do("vector", lambda e, ti=ti: e.tensor_copy(
                fRc[:, ti:ti + 1], psM[:, 0:1]), inc="sV")
            P.wait("tensor", "sV", vv)

        w2mm = [0, 0]
        ev_done = [0, 0]
        for dc in range(8):
            slot = dc % 2
            P.wait("sync", "sM", w2mm[slot])
            w2lw = dma("sync", w2r[:, slot * 4096:(slot + 1) * 4096],
                       w2t_scr.ap()[dc, :, :], "sX")
            pb = ps01 if slot == 0 else ps23
            P.wait("tensor", "sV", ev_done[slot])
            P.wait("tensor", "sX", w2lw)
            for th in range(2):
                for hc in range(32):
                    P.do("tensor", lambda e, th=th, hc=hc, pb=pb, slot=slot: e.matmul(
                        pb[:, th * 512:(th + 1) * 512],
                        w2r[:, slot * 4096 + hc * 128: slot * 4096 + (hc + 1) * 128],
                        uP[:, hc * 1024 + th * 512: hc * 1024 + (th + 1) * 512],
                        start=(hc == 0), stop=False))
                P.do("tensor", lambda e, th=th, dc=dc, pb=pb: e.matmul(
                    pb[:, th * 512:(th + 1) * 512],
                    wsumr[0:1, dc * 128:(dc + 1) * 128],
                    rowsb[0:1, th * 512:(th + 1) * 512],
                    start=False, stop=False))
                mmv = P.do("tensor", lambda e, th=th, dc=dc, pb=pb: e.matmul(
                    pb[:, th * 512:(th + 1) * 512],
                    biasr[0:1, dc * 128:(dc + 1) * 128],
                    rowsb[0:1, 1024 + th * 512: 1024 + (th + 1) * 512],
                    start=False, stop=True), inc="sM" if th == 1 else None)
            w2mm[slot] = mmv
            P.wait("vector", "sM", mmv)
            vv = P.do("vector", lambda e, pb=pb: e.tensor_copy(w2o[:], pb[:]), inc="sV")
            ev_done[slot] = vv
            P.wait("tensor", "sV", vv)
            for tg in range(2):
                for k4 in range(4):
                    ti = tg * 4 + k4
                    mmv = P.do("tensor", lambda e, ti=ti, k4=k4: e.transpose(
                        psT[:, k4 * 128:(k4 + 1) * 128],
                        w2o[:, ti * 128:(ti + 1) * 128], ident[:]),
                        inc="sM" if k4 == 3 else None)
                P.wait("vector", "sM", mmv)
                for k4 in range(4):
                    ti = tg * 4 + k4
                    ych_v = P.do("vector", lambda e, ti=ti, k4=k4, dc=dc:
                                 e.scalar_tensor_tensor(
                                     ybuf[:, ti * 1024 + dc * 128:
                                          ti * 1024 + (dc + 1) * 128],
                                     psT[:, k4 * 128:(k4 + 1) * 128],
                                     fRc[:, ti:ti + 1],
                                     xpc[:, ti * 1024 + dc * 128:
                                         ti * 1024 + (dc + 1) * 128],
                                     op0=ALU.mult, op1=ALU.add),
                                 inc="sV" if k4 == 3 else None)
                P.wait("tensor", "sV", ych_v)
        P.wait("scalar", "sV", ych_v)
        yst_w = []
        for ti in range(8):
            t = f * 8 + ti
            yst_w.append(dma("scalar", y_h.ap()[t * 128:(t + 1) * 128, :],
                             ybuf[:, ti * 1024:(ti + 1) * 1024]))

    return finalize()


def kernel(L0_attn_x, L0_attn_gamma, L0_attn_Wq, L0_attn_Wkv, L0_attn_W_bilinear,
           L0_attn_Wo, L0_ffn_ln1_g, L0_ffn_ln1_b, L0_ffn_w1, L0_ffn_b1,
           L0_ffn_ln2_g, L0_ffn_ln2_b, L0_ffn_w2, L0_ffn_b2):
    tabc, tabs = _host_consts()
    id16 = np.eye(128, dtype=np.float32)
    id32 = np.eye(128, dtype=np.float32)
    wb = np.asarray(L0_attn_W_bilinear, np.float32)
    wb2 = np.zeros((8, 128, 128), np.float32)
    for p in range(8):
        wb2[p, 0:64, 0:64] = wb[2 * p]
        wb2[p, 64:128, 64:128] = wb[2 * p + 1]

    common = {
        "wq": np.ascontiguousarray(np.asarray(L0_attn_Wq, np.float32)),
        "wkv": np.ascontiguousarray(np.asarray(L0_attn_Wkv, np.float32)),
        "wb2": wb2,
        "wo": np.ascontiguousarray(np.asarray(L0_attn_Wo, np.float32)),
        "w1": np.ascontiguousarray(np.asarray(L0_ffn_w1, np.float32)),
        "w2": np.ascontiguousarray(np.asarray(L0_ffn_w2, np.float32)),
        "gam": _cols(L0_attn_gamma),
        "g1": _cols(L0_ffn_ln1_g),
        "b1ln": _cols(L0_ffn_ln1_b),
        "b1": _cols(L0_ffn_b1),
        "g2": _cols(L0_ffn_ln2_g),
        "b2ln": _cols(L0_ffn_ln2_b),
        "b2": _cols(L0_ffn_b2),
        "tabc": tabc,
        "tabs": tabs,
        "id16": id16,
        "id32": id32,
    }
    x = np.asarray(L0_attn_x, np.float32)
    in_maps = []
    for b in range(NCORES):
        m = dict(common)
        m["x"] = np.ascontiguousarray(x[b])
        in_maps.append(m)

    nc, ctx = build_nc()
    try:
        res = run_bass_kernel_spmd(nc, in_maps, core_ids=list(range(NCORES)))
    finally:
        ctx.close()
    return np.stack([r["y"] for r in res.results]).astype(np.float32)
